# version 11
# baseline (speedup 1.0000x reference)
import contextlib
import numpy as np
import ml_dtypes
import concourse.bass as bass
import concourse.mybir as mybir
from concourse.bass_utils import run_bass_kernel_spmd

F32 = mybir.dt.float32
BF16 = mybir.dt.bfloat16
AF = mybir.ActivationFunctionType
ALU = mybir.AluOpType
AX = mybir.AxisListType
NBF = ml_dtypes.bfloat16

D = 1024
DF = 4096
EPS = 1e-5


class Sched:
    ENGS = ("Y", "G", "T", "S", "V")

    def __init__(self, nc, es):
        self.nc = nc
        self.esem = {}
        self.ecnt = {e: 0 for e in self.ENGS}
        self.streams = {}
        self.ops = []
        self.hw = {e: {} for e in self.ENGS}
        self.es = es
        for e in self.ENGS:
            self.esem[e] = es.enter_context(nc.semaphore(f"p{e}"))
        self.phase_deps = ()

    def stream(self, name, cls):
        if not hasattr(self, "_spool"):
            self._spool = {"H": [], "G": []}
            self._sems = []
            self._scnt = []
        if name in self.streams:
            idx = self.streams[name]
            assert self._sems[idx][1] == cls, f"stream {name} class mismatch"
            return idx
        if self._spool[cls]:
            idx = self._spool[cls].pop()
        else:
            idx = len(self._sems)
            sem = self.es.enter_context(self.nc.semaphore(f"dma{idx}"))
            self._sems.append((sem, cls))
            self._scnt.append(0)
        self.streams[name] = idx
        return idx

    def op(self, e, fn, deps=(), name=""):
        self.ecnt[e] += 1
        tok = (e, self.ecnt[e])
        self.ops.append((e, fn, tuple(deps) + tuple(self.phase_deps), tok, name))
        return tok

    def dma(self, e, stream, out, in_, deps=(), name="", **kw):
        cls = "G" if e == "G" else "H"
        idx = self.stream(stream, cls)
        self._scnt[idx] += 16
        tok = (("d", idx), self._scnt[idx])
        def fn(eng, out=out, in_=in_, kw=kw):
            return eng.dma_start(out=out, in_=in_, **kw)
        self.ops.append((e, fn, tuple(deps) + tuple(self.phase_deps), tok,
                         name or f"dma:{stream}"))
        return tok

    def tok_now(self):
        toks = [(e, self.ecnt[e]) for e in self.ENGS if self.ecnt[e] > 0]
        if hasattr(self, "_scnt"):
            toks += [(("d", i), c) for i, c in enumerate(self._scnt) if c > 0]
        return tuple(toks)

    def _sem_of(self, key):
        if key in self.esem:
            return self.esem[key]
        return self._sems[key[1]][0]

    def flush(self):
        nc = self.nc
        ops = self.ops
        self.ops = []
        if not ops:
            return
        with nc.Block() as block:
            meths = {"Y": block.sync, "G": block.gpsimd, "T": block.tensor,
                     "S": block.scalar, "V": block.vector}
            for ekey in self.ENGS:
                eops = [o for o in ops if o[0] == ekey]
                if not eops:
                    continue
                def body(eng, eops=eops, ekey=ekey):
                    hw = self.hw[ekey]
                    for (_, fn, deps, tok, name) in eops:
                        for (dk, dv) in deps:
                            if dk == ekey and tok[0] == ekey and dv >= tok[1]:
                                raise RuntimeError(f"self-dep future {name}")
                            if hw.get(dk, 0) < dv:
                                eng.wait_ge(self._sem_of(dk), dv)
                                hw[dk] = dv
                        ins = fn(eng)
                        if tok[0] in self.esem:
                            ins.then_inc(self.esem[ekey], 1)
                        else:
                            ins.then_inc(self._sem_of(tok[0]), 16)
                meths[ekey](body)

    def barrier(self):
        self.flush()
        self.phase_deps = self.tok_now()
        if hasattr(self, "_spool"):
            for name, idx in self.streams.items():
                self._spool[self._sems[idx][1]].append(idx)
            self.streams = {}

    def finish(self, extra_deps=()):
        self.flush()
        toks = self.tok_now() + tuple(extra_deps)
        nc = self.nc
        with nc.Block() as block:
            def body(eng):
                for (dk, dv) in toks:
                    eng.wait_ge(self._sem_of(dk), dv)
            block.sync(body)



def prechunk(w_lhsT):
    d_in, d_out = w_lhsT.shape
    KC, OC = d_in // 128, d_out // 128
    w = w_lhsT.reshape(KC, 128, OC, 128).transpose(2, 1, 0, 3).reshape(OC * 128, KC * 128)
    return np.ascontiguousarray(w)

def colsplit(vec):
    return np.ascontiguousarray(vec.reshape(-1, 128))



class MK:
    def __init__(self, seq=4096, tb=512):
        self.seq = seq
        self.tb = tb
        self.nb = seq // tb
        nc = bass.Bass("TRN2", target_bir_lowering=False)
        self.nc = nc
        self.es = contextlib.ExitStack()
        self.S = Sched(nc, self.es)
        self.inputs = {}
        self.dram = {}
        self.debug_outs = []

    def din(self, name, shape, dtype=F32):
        ap = self.nc.dram_tensor(name, list(shape), dtype, kind="ExternalInput").ap()
        self.dram[name] = ap
        return ap

    def dscratch(self, name, shape, dtype=F32, debug=False):
        kind = "ExternalOutput" if (debug and name in self.debug_outs) else "Internal"
        ap = self.nc.dram_tensor(name, list(shape), dtype, kind=kind).ap()
        self.dram[name] = ap
        return ap

    def dout(self, name, shape, dtype=F32):
        ap = self.nc.dram_tensor(name, list(shape), dtype, kind="ExternalOutput").ap()
        self.dram[name] = ap
        return ap

    def emit_consts(self):
        nc, S = self.nc, self.S
        es = self.es
        self.ones_col_bf = es.enter_context(nc.sbuf_tensor("ones_col_bf", [128, 1], BF16))
        self.ones_row_f = es.enter_context(nc.sbuf_tensor("ones_row_f", [1, 128], F32))
        self.ones_row_bf = es.enter_context(nc.sbuf_tensor("ones_row_bf", [1, 128], BF16))
        self.idn_bf = es.enter_context(nc.sbuf_tensor("idn_bf", [128, 128], BF16))
        self.idn_f = es.enter_context(nc.sbuf_tensor("idn_f", [128, 128], F32))
        idn_host = self.din("c_idn", [128, 128], F32)
        S.op("V", lambda v: v.memset(self.ones_col_bf[:], 1.0))
        S.op("V", lambda v: v.memset(self.ones_row_f[:], 1.0))
        S.op("V", lambda v: v.memset(self.ones_row_bf[:], 1.0))
        t1 = S.dma("Y", "consts", self.idn_f[:], idn_host[:])
        t2 = S.dma("G", "gconsts", self.idn_bf[:], idn_host[:])
        self.inputs["c_idn"] = np.eye(128, dtype=np.float32)
        self.colsb = es.enter_context(nc.sbuf_tensor("colsb", [128, 256], F32))
        self.ncols = 0
        self.fcsb = es.enter_context(nc.sbuf_tensor("fcsb", [128, 16], F32))
        self.nfc = 0
        self.fcmap = {}
        for v in (EPS,):
            self.fconst(v)

    def fconst(self, val):
        if val in self.fcmap:
            return self.fcmap[val]
        i = self.nfc
        assert i < 16
        self.nfc += 1
        ap = self.fcsb[:, i:i + 1]
        self.S.op("V", lambda v, ap=ap, val=val: v.memset(ap, float(val)))
        self.nc.const_aps.register(F32, float(val), ap)
        self.fcmap[val] = ap
        return ap

    def load_cols(self, name, host_oc128):
        OC = host_oc128.shape[0]
        assert host_oc128.shape == (OC, 128)
        iname = "c_" + name
        ap = self.din(iname, [OC, 128], F32)
        self.inputs[iname] = np.ascontiguousarray(host_oc128.astype(np.float32))
        c0 = self.ncols
        assert c0 + OC <= 256
        self.ncols += OC
        for o in range(OC):
            self.S.dma("Y", "consts", self.colsb[:, c0 + o:c0 + o + 1], ap[o:o + 1, :])
        return c0, None

    def col(self, c0, o):
        return self.colsb[:, c0 + o:c0 + o + 1]

    def precast(self, name, host_arr, chunk_elems=4 << 20):
        arr = np.ascontiguousarray(host_arr.astype(np.float32))
        shp = list(arr.shape)
        ap_in = self.din(name, shp, F32)
        self.inputs[name] = arr
        ap_bf = self.dscratch("b_" + name, shp, BF16, debug=True)
        n = int(np.prod(shp))
        flat_in = ap_in.rearrange("a b -> (a b)") if len(shp) == 2 else ap_in
        flat_out = ap_bf.rearrange("a b -> (a b)") if len(shp) == 2 else ap_bf
        toks = []
        for i in range(0, n, chunk_elems):
            j = min(n, i + chunk_elems)
            toks.append(self.S.dma("G", "cast_" + name, flat_out[i:j], flat_in[i:j]))
        return ap_bf, toks

    def emit_prologue(self, x_in, xA):
        S = self.S
        nc = self.nc
        seq = self.seq
        with contextlib.ExitStack() as ph:
            xt = [ph.enter_context(nc.sbuf_tensor(f"pro_xt{i}", [128, D], F32)) for i in range(2)]
            ot = [ph.enter_context(nc.sbuf_tensor(f"pro_ot{i}", [128, D], F32)) for i in range(2)]
            pst = [ph.enter_context(nc.psum_tensor(f"pro_ps{i}", [128, 512], F32)) for i in range(2)]
            last_evac = [None, None]
            last_store = [None, None]
            for i in range(seq // 128):
                sl = i % 2
                deps = [last_evac[sl]] if last_evac[sl] else []
                tl = S.dma("Y", f"pro_x{sl}", xt[sl][:], x_in[i * 128:(i + 1) * 128, :], deps=deps)
                th = []
                ev0 = None
                for h in range(2):
                    def tr(t, sl=sl, h=h):
                        ins = None
                        for c in range(4):
                            cc = h * 4 + c
                            ins = t.transpose(pst[sl][:, c * 128:(c + 1) * 128],
                                              xt[sl][:, cc * 128:(cc + 1) * 128], self.idn_f[:])
                        return ins
                    pdeps = [tl] + ([last_evac[sl]] if last_evac[sl] else [])
                    if h == 1:
                        pdeps.append(ev0)
                    ttok = S.op("T", tr, deps=pdeps, name=f"pro_tr{i}_{h}")
                    def evac(v, sl=sl, h=h):
                        return v.tensor_copy(ot[sl][:, h * 512:(h + 1) * 512], pst[sl][:])
                    sdeps = [ttok] + ([last_store[sl]] if last_store[sl] else [])
                    evtok = S.op("V", evac, deps=sdeps, name=f"pro_ev{i}_{h}")
                    if h == 0:
                        ev0 = evtok
                    th.append(evtok)
                last_evac[sl] = th[1]
                sttoks = []
                for c in range(8):
                    sttoks.append(S.dma("Y", f"pro_o{sl}",
                                        xA[c * 128:(c + 1) * 128, i * 128:(i + 1) * 128],
                                        ot[sl][:, c * 128:(c + 1) * 128], deps=[th[c // 4]]))
                last_store[sl] = sttoks[-1]
            S.barrier()

    def emit_epilogue(self, xN, out):
        S = self.S
        nc = self.nc
        seq = self.seq
        with contextlib.ExitStack() as ph:
            it = [ph.enter_context(nc.sbuf_tensor(f"epi_it{i}", [128, D], F32)) for i in range(2)]
            ot = [ph.enter_context(nc.sbuf_tensor(f"epi_ot{i}", [128, D], F32)) for i in range(2)]
            pst = [ph.enter_context(nc.psum_tensor(f"epi_ps{i}", [128, 512], F32)) for i in range(2)]
            last_evac = [None, None]
            last_store = [None, None]
            for i in range(seq // 128):
                sl = i % 2
                deps = [last_evac[sl]] if last_evac[sl] else []
                ltoks = []
                for c in range(8):
                    ltoks.append(S.dma("Y", f"epi_x{sl}", it[sl][:, c * 128:(c + 1) * 128],
                                       xN[c * 128:(c + 1) * 128, i * 128:(i + 1) * 128], deps=deps))
                tl = ltoks[-1]
                ev0 = None
                th = []
                for h in range(2):
                    def tr(t, sl=sl, h=h):
                        ins = None
                        for c in range(4):
                            cc = h * 4 + c
                            ins = t.transpose(pst[sl][:, c * 128:(c + 1) * 128],
                                              it[sl][:, cc * 128:(cc + 1) * 128], self.idn_f[:])
                        return ins
                    pdeps = [tl] + ([last_evac[sl]] if last_evac[sl] else [])
                    if h == 1:
                        pdeps.append(ev0)
                    ttok = S.op("T", tr, deps=pdeps, name=f"epi_tr{i}_{h}")
                    def evac(v, sl=sl, h=h):
                        return v.tensor_copy(ot[sl][:, (h * 4) * 128:(h * 4 + 4) * 128], pst[sl][:])
                    sdeps = [ttok] + ([last_store[sl]] if last_store[sl] else [])
                    evtok = S.op("V", evac, deps=sdeps, name=f"epi_ev{i}_{h}")
                    if h == 0:
                        ev0 = evtok
                    th.append(evtok)
                last_evac[sl] = th[1]
                last_store[sl] = S.dma("Y", f"epi_o{sl}", out[i * 128:(i + 1) * 128, :],
                                       ot[sl][:], deps=th)
            S.barrier()

    def emit_ffn0(self, xsrc, xdst, wbf, scol, tag):
        S, nc = self.S, self.nc
        TB = self.tb
        with contextlib.ExitStack() as ph:
            xsb = ph.enter_context(nc.sbuf_tensor(f"{tag}_x", [128, 8 * TB], F32))
            xa = [ph.enter_context(nc.sbuf_tensor(f"{tag}_xa{i}", [128, TB], F32)) for i in range(2)]
            x16 = ph.enter_context(nc.sbuf_tensor(f"{tag}_x16", [128, 8 * TB], BF16))
            sq16 = ph.enter_context(nc.sbuf_tensor(f"{tag}_sq", [128, 8 * TB], BF16))
            t16 = ph.enter_context(nc.sbuf_tensor(f"{tag}_t16", [128, 8 * TB], BF16))
            u16 = ph.enter_context(nc.sbuf_tensor(f"{tag}_u16", [128, 32 * TB], BF16))
            v16 = ph.enter_context(nc.sbuf_tensor(f"{tag}_v16", [128, 32 * TB], BF16))
            sA = [ph.enter_context(nc.sbuf_tensor(f"{tag}_sA{i}", [128, TB], BF16)) for i in range(2)]
            stat = ph.enter_context(nc.sbuf_tensor(f"{tag}_st", [1, 6 * TB], F32))
            wsl = {}
            for wname, wid in (("w1", 8), ("w2", 8), ("h1", 32), ("h2", 32), ("w3", 32)):
                wsl[wname] = [ph.enter_context(
                    nc.sbuf_tensor(f"{tag}_{wname}s{i}", [128, wid * 128], BF16)) for i in range(2)]
            ps_sum = ph.enter_context(nc.psum_tensor(f"{tag}_psum", [1, TB], F32))
            ps_sq = ph.enter_context(nc.psum_tensor(f"{tag}_psq", [1, TB], F32))
            ps_A = ph.enter_context(nc.psum_tensor(f"{tag}_psA", [128, TB], F32))
            ps_C = ph.enter_context(nc.psum_tensor(f"{tag}_psC", [128, TB], F32))
            ps = [ph.enter_context(nc.psum_tensor(f"{tag}_ps{i}", [128, TB], F32)) for i in range(2)]

            wtoks = {k: [None, None] for k in wsl}
            ltoks = {k: [None, None] for k in wsl}
            prev_x_readers = None
            prev_u16_mm = None
            prev_statreaders = []
            prev_norm = None
            prev_psAC = None
            prev_t16_mm = None
            prev_xa = [None, None]

            def loadw(wname, o, OC):
                sl = o % 2
                dep = [wtoks[wname][sl]] if wtoks[wname][sl] else []
                ltoks[wname][sl] = S.dma("Y", f"{tag}_{wname}{sl}", wsl[wname][sl][:],
                                         wbf[wname][o * 128:(o + 1) * 128, :], deps=dep)

            for b in range(self.nb):
                tsl = slice(b * TB, (b + 1) * TB)
                xdep = list(prev_x_readers) if prev_x_readers else []
                xts = [S.dma("Y", f"{tag}_xin", xsb[:, c * TB:(c + 1) * TB],
                             xsrc[c * 128:(c + 1) * 128, tsl], deps=xdep) for c in range(8)]
                xtok = xts[-1]
                cast_toks = []
                sq_toks = []
                for c in range(8):
                    dep = [xtok] + ([prev_norm[c]] if prev_norm else [])
                    cast_toks.append(S.op("V", lambda v, c=c: v.tensor_copy(
                        x16[:, c * TB:(c + 1) * TB], xsb[:, c * TB:(c + 1) * TB]), deps=dep))
                    sq_toks.append(S.op("S", lambda s, c=c: s.activation(
                        out=sq16[:, c * TB:(c + 1) * TB], in_=xsb[:, c * TB:(c + 1) * TB],
                        func=AF.Square), deps=dep))
                def mmsum(t):
                    ins = None
                    for c in range(8):
                        ins = t.matmul(ps_sum[:], self.ones_col_bf[:], x16[:, c * TB:(c + 1) * TB],
                                       start=(c == 0), stop=(c == 7))
                    return ins
                def mmsq(t):
                    ins = None
                    for c in range(8):
                        ins = t.matmul(ps_sq[:], self.ones_col_bf[:], sq16[:, c * TB:(c + 1) * TB],
                                       start=(c == 0), stop=(c == 7))
                    return ins
                sdep = cast_toks + sq_toks
                tsum = S.op("T", mmsum, deps=sdep, name=f"{tag}_lnsum{b}")
                tsq = S.op("T", mmsq, deps=sdep, name=f"{tag}_lnsq{b}")
                mu = stat[:, 0:TB]; ex2 = stat[:, TB:2 * TB]; var = stat[:, 2 * TB:3 * TB]
                std = stat[:, 3 * TB:4 * TB]; rinv = stat[:, 4 * TB:5 * TB]; nmr = stat[:, 5 * TB:6 * TB]
                v1 = S.op("V", lambda v: v.tensor_scalar_mul(mu, ps_sum[:], 1.0 / D),
                          deps=[tsum] + prev_statreaders)
                v2 = S.op("V", lambda v: v.tensor_scalar_mul(ex2, ps_sq[:], 1.0 / D), deps=[tsq])
                v3 = S.op("V", lambda v: v.scalar_tensor_tensor(
                    var, mu, 1.0, mu, op0=ALU.mult, op1=ALU.mult), deps=[v1])
                v4 = S.op("V", lambda v: v.tensor_sub(var, ex2, var), deps=[v2, v3])
                s5 = S.op("S", lambda s: s.activation(out=std, in_=var, func=AF.Sqrt, bias=EPS), deps=[v4])
                v6 = S.op("V", lambda v: v.reciprocal(rinv, std), deps=[s5])
                v7 = S.op("V", lambda v: v.scalar_tensor_tensor(
                    nmr, mu, -1.0, rinv, op0=ALU.mult, op1=ALU.mult), deps=[v6, v1])
                pdep = [prev_psAC] if prev_psAC else []
                tA = S.op("T", lambda t: t.matmul(ps_A[:], self.ones_row_f[:], rinv,
                                                  start=True, stop=True), deps=[v6] + pdep)
                tC = S.op("T", lambda t: t.matmul(ps_C[:], self.ones_row_f[:], nmr,
                                                  start=True, stop=True), deps=[v7] + pdep)
                norm_toks = []
                for c in range(8):
                    sl2 = c % 2
                    d1 = [tA, xtok] + ([prev_xa[sl2]] if prev_xa[sl2] else [])
                    n1 = S.op("V", lambda v, c=c: v.tensor_mul(
                        xa[c % 2][:], xsb[:, c * TB:(c + 1) * TB], ps_A[:]), deps=d1)
                    d2 = [tC, n1]
                    if prev_t16_mm is not None:
                        d2.append(prev_t16_mm)
                    n2 = S.op("V", lambda v, c=c: v.tensor_tensor(
                        t16[:, c * TB:(c + 1) * TB], xa[c % 2][:], ps_C[:],
                        op=ALU.add), deps=d2)
                    prev_xa[sl2] = n2
                    norm_toks.append(n2)
                prev_statreaders = [tA, tC]
                prev_norm = norm_toks
                prev_psAC = norm_toks[-1]
                xrtok = xtok

                evB = []
                mmB = []
                for o in range(32):
                    if b == 0 and o < 2:
                        loadw("w1", o, 32); loadw("w2", o, 32)
                    psl = o % 2
                    pdeps = [ltoks["w1"][o % 2]] + norm_toks
                    if len(evB) >= 2:
                        pdeps.append(evB[-2][1])
                    def mm1(t, o=o):
                        ins = None
                        sl = wsl["w1"][o % 2]
                        for k in range(8):
                            ins = t.matmul(ps[o % 2][:], sl[:, k * 128:(k + 1) * 128],
                                           t16[:, k * TB:(k + 1) * TB], start=(k == 0), stop=(k == 7))
                        return ins
                    ta = S.op("T", mm1, deps=pdeps, name=f"{tag}_B1_{b}_{o}")
                    sa = S.op("S", lambda s, o=o: s.activation(
                        out=sA[o % 2][:], in_=ps[o % 2][:], func=AF.Silu,
                        scale=self.col(scol["s1"][0], o)),
                        deps=[ta] + ([evB[-2][1]] if len(evB) >= 2 else []), name=f"{tag}_Bs_{b}_{o}")
                    def mm2(t, o=o):
                        ins = None
                        sl = wsl["w2"][o % 2]
                        for k in range(8):
                            ins = t.matmul(ps[o % 2][:], sl[:, k * 128:(k + 1) * 128],
                                           t16[:, k * TB:(k + 1) * TB], start=(k == 0), stop=(k == 7))
                        return ins
                    tb_ = S.op("T", mm2, deps=[sa, ltoks["w2"][o % 2]], name=f"{tag}_B2_{b}_{o}")
                    wtoks["w1"][o % 2] = ta
                    wtoks["w2"][o % 2] = tb_
                    if o + 2 < 32:
                        loadw("w1", o + 2, 32); loadw("w2", o + 2, 32)
                    elif b + 1 < self.nb and o + 2 < 34:
                        loadw("w1", o + 2 - 32, 32); loadw("w2", o + 2 - 32, 32)
                    udep = [tb_]
                    if b > 0 and o < 2:
                        udep.append(prev_u16_mm)
                    uev = S.op("V", lambda v, o=o: v.scalar_tensor_tensor(
                        u16[:, o * TB:(o + 1) * TB], ps[o % 2][:], self.col(scol["s2"][0], o),
                        sA[o % 2][:], op0=ALU.mult, op1=ALU.mult), deps=udep,
                        name=f"{tag}_Bu_{b}_{o}")
                    evB.append((ta, uev, sa))
                    mmB.append(tb_)
                evC = []
                lastB = [evB[-1][1], evB[-2][1]] + mmB[-2:]
                for o in range(32):
                    if o < 2:
                        sl = o % 2
                        dep = [wtoks["h1"][sl]] if wtoks["h1"][sl] else []
                        ltoks["h1"][sl] = S.dma("Y", f"{tag}_h1{sl}", wsl["h1"][sl][:],
                                                wbf["h1"][o * 128:(o + 1) * 128, :],
                                                deps=dep)
                    psl = o % 2
                    pdeps = [ltoks["h1"][psl]] + [x[1] for x in evB]
                    if len(evC) >= 2:
                        pdeps.append(evC[-2])
                    def mmc(t, o=o):
                        ins = None
                        sl = wsl["h1"][o % 2]
                        for k in range(32):
                            ins = t.matmul(ps[o % 2][:], sl[:, k * 128:(k + 1) * 128],
                                           u16[:, k * TB:(k + 1) * TB], start=(k == 0), stop=(k == 31))
                        return ins
                    tc = S.op("T", mmc, deps=pdeps, name=f"{tag}_C_{b}_{o}")
                    wtoks["h1"][psl] = tc
                    if o + 2 < 32:
                        sl = o % 2
                        ltoks["h1"][sl] = S.dma("Y", f"{tag}_h1{sl}", wsl["h1"][sl][:],
                                                wbf["h1"][(o + 2) * 128:(o + 3) * 128, :],
                                                deps=[tc])
                    ev = S.op("S", lambda s, o=o: s.activation(
                        out=v16[:, o * TB:(o + 1) * TB], in_=ps[o % 2][:], func=AF.Gelu,
                        scale=self.col(scol["h1s"][0], o)), deps=[tc], name=f"{tag}_Cg_{b}_{o}")
                    evC.append(ev)
                evD = []
                for o in range(32):
                    if o < 2:
                        sl = o % 2
                        dep = [wtoks["h2"][sl]] if wtoks["h2"][sl] else []
                        ltoks["h2"][sl] = S.dma("Y", f"{tag}_h2{sl}", wsl["h2"][sl][:],
                                                wbf["h2"][o * 128:(o + 1) * 128, :], deps=dep)
                    psl = o % 2
                    pdeps = [ltoks["h2"][psl]] + evC
                    pdeps += [x[1] for x in evB]
                    pdeps.append(evC[-1])
                    if len(evD) >= 2:
                        pdeps.append(evD[-2])
                    def mmd(t, o=o):
                        ins = None
                        sl = wsl["h2"][o % 2]
                        for k in range(32):
                            ins = t.matmul(ps[o % 2][:], sl[:, k * 128:(k + 1) * 128],
                                           v16[:, k * TB:(k + 1) * TB], start=(k == 0), stop=(k == 31))
                        return ins
                    td = S.op("T", mmd, deps=pdeps, name=f"{tag}_D_{b}_{o}")
                    wtoks["h2"][psl] = td
                    if o + 2 < 32:
                        sl = o % 2
                        ltoks["h2"][sl] = S.dma("Y", f"{tag}_h2{sl}", wsl["h2"][sl][:],
                                                wbf["h2"][(o + 2) * 128:(o + 3) * 128, :], deps=[td])
                    ev = S.op("S", lambda s, o=o: s.activation(
                        out=u16[:, o * TB:(o + 1) * TB], in_=ps[o % 2][:], func=AF.Gelu,
                        scale=self.col(scol["h2s"][0], o)), deps=[td], name=f"{tag}_Dg_{b}_{o}")
                    evD.append(ev)
                evE = []
                stE = []
                for o in range(8):
                    if o < 2:
                        sl = o % 2
                        dep = [wtoks["w3"][sl]] if wtoks["w3"][sl] else []
                        ltoks["w3"][sl] = S.dma("Y", f"{tag}_w3{sl}", wsl["w3"][sl][:],
                                                wbf["w3"][o * 128:(o + 1) * 128, :], deps=dep)
                    psl = o % 2
                    pdeps = [ltoks["w3"][psl]] + evD
                    if len(evE) >= 2:
                        pdeps.append(evE[-2])
                    def mme(t, o=o):
                        ins = None
                        sl = wsl["w3"][o % 2]
                        for k in range(32):
                            ins = t.matmul(ps[o % 2][:], sl[:, k * 128:(k + 1) * 128],
                                           u16[:, k * TB:(k + 1) * TB], start=(k == 0), stop=(k == 31))
                        return ins
                    te = S.op("T", mme, deps=pdeps, name=f"{tag}_E_{b}_{o}")
                    wtoks["w3"][psl] = te
                    if o + 2 < 8:
                        sl = o % 2
                        ltoks["w3"][sl] = S.dma("Y", f"{tag}_w3{sl}", wsl["w3"][sl][:],
                                                wbf["w3"][(o + 2) * 128:(o + 3) * 128, :], deps=[te])
                    ev = S.op("V", lambda v, o=o: v.scalar_tensor_tensor(
                        xsb[:, o * TB:(o + 1) * TB], ps[o % 2][:],
                        self.col(scol["w3sh"][0], o), xsb[:, o * TB:(o + 1) * TB],
                        op0=ALU.mult, op1=ALU.add),
                        deps=[te, xrtok, norm_toks[o], sq_toks[o], cast_toks[o]],
                        name=f"{tag}_Ee_{b}_{o}")
                    evE.append(ev)
                    stE.append(S.dma("Y", f"{tag}_xout", xdst[o * 128:(o + 1) * 128, tsl],
                                     xsb[:, o * TB:(o + 1) * TB], deps=[ev]))
                prev_x_readers = evE + [stE[-1]]
                prev_t16_mm = mmB[-1]
                prev_u16_mm = te
            S.barrier()


    def emit_qkv(self, xsrc, Qf, Kf, Vt, cfg, tag):
        S, nc = self.S, self.nc
        TB = self.tb
        mode = cfg["mode"]
        KV = cfg["kv"]
        ntile = self.seq // 128
        with contextlib.ExitStack() as ph:
            xsb = ph.enter_context(nc.sbuf_tensor(f"{tag}_x", [128, 8 * TB], F32))
            xa = [ph.enter_context(nc.sbuf_tensor(f"{tag}_xa{i}", [128, TB], F32)) for i in range(2)]
            x16 = ph.enter_context(nc.sbuf_tensor(f"{tag}_x16", [128, 8 * TB], BF16))
            sq16 = ph.enter_context(nc.sbuf_tensor(f"{tag}_sq", [128, 8 * TB], BF16))
            act16 = ph.enter_context(nc.sbuf_tensor(f"{tag}_a16", [128, 8 * TB], BF16))
            stat = ph.enter_context(nc.sbuf_tensor(f"{tag}_st", [1, 8 * TB], F32))
            wq_w = {"l0": 1024, "l1f": 3072}[mode]
            wk_w = {"l0": 256, "l1f": 0}[mode]
            wtot = wq_w + 2 * wk_w if mode == "l0" else 3072
            wsb = ph.enter_context(nc.sbuf_tensor(f"{tag}_w", [128, 8 * wtot], BF16))
            tabs = cfg["tabs"]
            tsl = [ph.enter_context(nc.sbuf_tensor(f"{tag}_tab{i}", [128, 4 * 1024], BF16))
                   for i in range(2)]
            rA = ph.enter_context(nc.sbuf_tensor(f"{tag}_rA", [128, 512], F32))
            rB = ph.enter_context(nc.sbuf_tensor(f"{tag}_rB", [128, 512], F32))
            qrot = ph.enter_context(nc.sbuf_tensor(f"{tag}_qrot", [128, 1024], BF16))
            krot = ph.enter_context(nc.sbuf_tensor(f"{tag}_krot", [128, 1024], BF16))
            qf_asm = ph.enter_context(nc.sbuf_tensor(f"{tag}_qasm", [128, 8 * 512], BF16))
            kf_asm = ph.enter_context(nc.sbuf_tensor(f"{tag}_kasm", [128, (2 if mode == "l0" else 8) * 512], BF16))
            vasm = ph.enter_context(nc.sbuf_tensor(f"{tag}_vasm", [128, KV * 65], BF16))
            if mode == "l1f":
                tsb = ph.enter_context(nc.sbuf_tensor(f"{tag}_t", [128, 8 * TB], F32))
                hb = ph.enter_context(nc.sbuf_tensor(f"{tag}_hb", [128, 8 * (TB + 2)], F32))
                ctmp = ph.enter_context(nc.sbuf_tensor(f"{tag}_ctmp", [128, TB], F32))
            ps_sum = ph.enter_context(nc.psum_tensor(f"{tag}_psum", [1, TB], F32))
            ps_sq = ph.enter_context(nc.psum_tensor(f"{tag}_psq", [1, TB], F32))
            ps_A = ph.enter_context(nc.psum_tensor(f"{tag}_psA", [128, TB], F32))
            ps_C = ph.enter_context(nc.psum_tensor(f"{tag}_psC", [128, TB], F32))
            ps = [ph.enter_context(nc.psum_tensor(f"{tag}_ps{i}", [128, 512], F32)) for i in range(2)]
            ps_tr = ph.enter_context(nc.psum_tensor(f"{tag}_ptr", [128, 512], BF16))

            if mode == "l0":
                for kc in range(8):
                    S.dma("Y", f"{tag}_wld", wsb[:, kc * wtot:kc * wtot + 1024],
                          cfg["wq"][kc * 128:(kc + 1) * 128, :])
                    S.dma("Y", f"{tag}_wld", wsb[:, kc * wtot + 1024:kc * wtot + 1280],
                          cfg["wk"][kc * 128:(kc + 1) * 128, :])
                    S.dma("Y", f"{tag}_wld", wsb[:, kc * wtot + 1280:kc * wtot + 1536],
                          cfg["wv"][kc * 128:(kc + 1) * 128, :])
            else:
                for kc in range(8):
                    S.dma("Y", f"{tag}_wld", wsb[:, kc * wtot:(kc + 1) * wtot],
                          cfg["wqkv"][kc * 128:(kc + 1) * 128, :])
            wld_tok = S.tok_now()
            vset = S.op("V", lambda v: v.memset(
                vasm[:].rearrange("p (k j) -> p k j", k=KV, j=65)[:, :, 64:65], 1.0))
            if mode == "l1f":
                hz = S.op("V", lambda v: v.memset(
                    hb[:].rearrange("p (c j) -> p c j", c=8, j=TB + 2)[:, :, 0:2], 0.0))
            S.barrier()

            prev_x_readers = None
            prev_statreaders = []
            prev_psAC = None
            prev_act_mm = None
            prev_xa = [None, None]
            prev_sliver = None
            prev_tsb_store = None
            self.prev_tab = [None, None]
            self.ps_free = [None, None]
            self.rope_scr = None
            self.rot_rd = None
            self.vst_last = None
            self.tr_ev = None
            self.asm_st = []
            for b in range(self.nb):
                tslice = slice(b * TB, (b + 1) * TB)
                xdep = list(prev_x_readers) if prev_x_readers else []
                xts = [S.dma("Y", f"{tag}_xin", xsb[:, c * TB:(c + 1) * TB],
                             xsrc[c * 128:(c + 1) * 128, tslice], deps=xdep) for c in range(8)]
                xtok = xts[-1]
                cast_toks, sq_toks = [], []
                for c in range(8):
                    dep = [xtok]
                    cast_toks.append(S.op("V", lambda v, c=c: v.tensor_copy(
                        x16[:, c * TB:(c + 1) * TB], xsb[:, c * TB:(c + 1) * TB]), deps=dep))
                    sq_toks.append(S.op("S", lambda s, c=c: s.activation(
                        out=sq16[:, c * TB:(c + 1) * TB], in_=xsb[:, c * TB:(c + 1) * TB],
                        func=AF.Square), deps=dep))
                def mmsum(t):
                    for c in range(8):
                        ins = t.matmul(ps_sum[:], self.ones_col_bf[:], x16[:, c * TB:(c + 1) * TB],
                                       start=(c == 0), stop=(c == 7))
                    return ins
                def mmsq(t):
                    for c in range(8):
                        ins = t.matmul(ps_sq[:], self.ones_col_bf[:], sq16[:, c * TB:(c + 1) * TB],
                                       start=(c == 0), stop=(c == 7))
                    return ins
                sdep = cast_toks + sq_toks
                tsum = S.op("T", mmsum, deps=sdep)
                tsq = S.op("T", mmsq, deps=sdep)
                mu = stat[:, 0:TB]; ex2 = stat[:, TB:2 * TB]; var = stat[:, 2 * TB:3 * TB]
                std = stat[:, 3 * TB:4 * TB]; rinv = stat[:, 4 * TB:5 * TB]; nmr = stat[:, 5 * TB:6 * TB]
                var2 = stat[:, 6 * TB:7 * TB]; nmr2 = stat[:, 7 * TB:8 * TB]
                v1 = S.op("V", lambda v: v.tensor_scalar_mul(mu, ps_sum[:], 1.0 / D),
                          deps=[tsum] + prev_statreaders)
                v2 = S.op("V", lambda v: v.tensor_scalar_mul(ex2, ps_sq[:], 1.0 / D), deps=[tsq])
                v3 = S.op("V", lambda v: v.scalar_tensor_tensor(
                    var, mu, 1.0, mu, op0=ALU.mult, op1=ALU.mult), deps=[v1])
                v4 = S.op("V", lambda v: v.tensor_sub(var, ex2, var), deps=[v2, v3])
                s5 = S.op("S", lambda s: s.activation(out=std, in_=var, func=AF.Sqrt, bias=EPS),
                          deps=[v4])
                v6 = S.op("V", lambda v: v.reciprocal(rinv, std), deps=[s5])
                v7 = S.op("V", lambda v: v.scalar_tensor_tensor(
                    nmr, mu, -1.0, rinv, op0=ALU.mult, op1=ALU.mult), deps=[v6, v1])
                pdep = [prev_psAC] if prev_psAC else []
                tA = S.op("T", lambda t: t.matmul(ps_A[:], self.ones_row_f[:], rinv,
                                                  start=True, stop=True), deps=[v6] + pdep)
                tC = S.op("T", lambda t: t.matmul(ps_C[:], self.ones_row_f[:], nmr,
                                                  start=True, stop=True), deps=[v7] + pdep)
                norm_toks = []
                if mode == "l0":
                    for c in range(8):
                        sl2 = c % 2
                        d1 = [tA, xtok] + ([prev_xa[sl2]] if prev_xa[sl2] else [])
                        n1 = S.op("V", lambda v, c=c: v.tensor_mul(
                            xa[c % 2][:], xsb[:, c * TB:(c + 1) * TB], ps_A[:]), deps=d1)
                        d2 = [tC, n1] + ([prev_act_mm] if prev_act_mm else [])
                        n2 = S.op("V", lambda v, c=c: v.tensor_tensor(
                            act16[:, c * TB:(c + 1) * TB], xa[c % 2][:], ps_C[:],
                            op=ALU.add), deps=d2)
                        prev_xa[sl2] = n2
                        norm_toks.append(n2)
                    statr = [tA, tC]
                else:
                    t_toks = []
                    for c in range(8):
                        sl2 = c % 2
                        d1 = [tA, xtok] + ([prev_xa[sl2]] if prev_xa[sl2] else [])
                        n1 = S.op("V", lambda v, c=c: v.tensor_mul(
                            xa[c % 2][:], xsb[:, c * TB:(c + 1) * TB], ps_A[:]), deps=d1)
                        d2 = [tC, n1] + ([prev_tsb_store] if b > 0 else [])
                        n2 = S.op("V", lambda v, c=c: v.tensor_tensor(
                            tsb[:, c * TB:(c + 1) * TB], xa[c % 2][:], ps_C[:],
                            op=ALU.add), deps=d2)
                        prev_xa[sl2] = n2
                        t_toks.append(n2)
                    tst = [S.dma("Y", f"{tag}_tout", cfg["tdst"][c * 128:(c + 1) * 128, tslice],
                                 tsb[:, c * TB:(c + 1) * TB], deps=[t_toks[c]]) for c in range(8)]
                    prev_tsb_store = tst[-1]
                    e1 = S.op("V", lambda v: v.tensor_scalar(
                        var2, var, EPS, EPS, op0=ALU.add, op1=ALU.mult), deps=[v4, tA])
                    e2 = S.op("V", lambda v: v.tensor_add(var2, var, var2), deps=[e1])
                    s6 = S.op("S", lambda s: s.activation(out=std, in_=var2, func=AF.Sqrt),
                              deps=[e2, v6])
                    e3 = S.op("V", lambda v: v.reciprocal(rinv, std), deps=[s6, tA])
                    e4 = S.op("V", lambda v: v.scalar_tensor_tensor(
                        nmr2, mu, -1.0, rinv, op0=ALU.mult, op1=ALU.mult), deps=[e3])
                    ndep = t_toks + [e3]
                    tA2 = S.op("T", lambda t: t.matmul(ps_A[:], self.ones_row_f[:], rinv,
                                                       start=True, stop=True), deps=ndep)
                    tC2 = S.op("T", lambda t: t.matmul(ps_C[:], self.ones_row_f[:], nmr2,
                                                       start=True, stop=True), deps=[e4] + ndep)
                    hw_c0 = cfg["ln1w"][0]
                    hb_c0 = cfg["ln1b"][0]
                    conv_toks = []
                    for c in range(8):
                        sl2 = c % 2
                        hoff = c * (TB + 2)
                        d1 = [tA2, xtok, t_toks[c]] + ([prev_xa[sl2]] if prev_xa[sl2] else [])
                        n1 = S.op("V", lambda v, c=c: v.tensor_mul(
                            xa[c % 2][:], xsb[:, c * TB:(c + 1) * TB], ps_A[:]), deps=d1)
                        d2 = [tC2, n1] + ([prev_sliver[c]] if prev_sliver else [])
                        n2 = S.op("V", lambda v, c=c, hoff=hoff: v.tensor_tensor(
                            hb[:, hoff + 2:hoff + 2 + TB], xa[c % 2][:], ps_C[:],
                            op=ALU.add), deps=d2)
                        prev_xa[sl2] = n2
                        n3 = S.op("V", lambda v, c=c, hoff=hoff: v.tensor_scalar(
                            hb[:, hoff + 2:hoff + 2 + TB], hb[:, hoff + 2:hoff + 2 + TB],
                            self.col(hw_c0, c), self.col(hb_c0, c),
                            op0=ALU.mult, op1=ALU.add), deps=[n2])
                        norm_toks.append(n3)
                    statr = [tA, tC, tA2, tC2]
                    sliver_toks = []
                    conv_out = []
                    for c in range(8):
                        hoff = c * (TB + 2)
                        o1 = S.op("V", lambda v, c=c, hoff=hoff: v.tensor_scalar_mul(
                            ctmp[:], hb[:, hoff:hoff + TB], self.col(cfg["cw0"][0], c)),
                            deps=[norm_toks[c]] + ([conv_out[-1]] if conv_out else []))
                        o2 = S.op("V", lambda v, c=c, hoff=hoff: v.scalar_tensor_tensor(
                            ctmp[:], hb[:, hoff + 1:hoff + 1 + TB], self.col(cfg["cw1"][0], c),
                            ctmp[:], op0=ALU.mult, op1=ALU.add), deps=[o1])
                        o3dep = [o2] + ([prev_act_mm] if prev_act_mm else [])
                        o3 = S.op("V", lambda v, c=c, hoff=hoff: v.scalar_tensor_tensor(
                            act16[:, c * TB:(c + 1) * TB], hb[:, hoff + 2:hoff + 2 + TB],
                            self.col(cfg["cw2"][0], c), ctmp[:],
                            op0=ALU.mult, op1=ALU.add), deps=o3dep)
                        sv = S.op("V", lambda v, c=c, hoff=hoff: v.tensor_copy(
                            hb[:, hoff:hoff + 2], hb[:, hoff + TB:hoff + TB + 2]), deps=[o3])
                        sliver_toks.append(sv)
                        conv_out.append(o3)
                    norm_toks = conv_out
                    prev_sliver = sliver_toks
                prev_statreaders = statr
                prev_psAC = norm_toks[-1] if mode == "l0" else norm_toks[-1]
                for tt in range(TB // 128):
                    gtile = b * (TB // 128) + tt
                    sl = tt % 2
                    tdep = [self.prev_tab[sl]] if self.prev_tab[sl] else []
                    tabtok = None
                    for ti, tab in enumerate(tabs):
                        tabtok = S.dma("Y", f"{tag}_tab{sl}", tsl[sl][:, ti * 1024:(ti + 1) * 1024],
                                       tab[gtile * 128:(gtile + 1) * 128, :], deps=tdep)
                    tabtile = tsl[sl]
                    def actT(kc, tt=tt):
                        return act16[:, kc * TB + tt * 128: kc * TB + tt * 128 + 128]
                    if mode == "l0":
                        groups = [("q", 0, 512, 0), ("q", 512, 512, 1),
                                  ("k", 1024, 256, 0), ("v", 1280, 256, 0)]
                    else:
                        groups = [("q", 0, 512, 0), ("q", 512, 512, 1),
                                  ("k", 1024, 512, 0), ("k", 1536, 512, 1),
                                  ("v", 2048, 512, 0), ("v", 2560, 512, 1)]
                    rope_q = [None, None]
                    rope_k = [None, None]
                    vev_tok = None
                    for gi, (kind, ooff, width, half) in enumerate(groups):
                        psl = gi % 2
                        mdeps = list(norm_toks) + list(wld_tok)
                        if self.ps_free[psl] is not None:
                            mdeps.append(self.ps_free[psl])
                        def mmg(t, ooff=ooff, width=width, psl=psl, tt=tt):
                            for kc in range(8):
                                ins = t.matmul(ps[psl][:, 0:width],
                                               actT(kc, tt),
                                               wsb[:, kc * wtot + ooff:kc * wtot + ooff + width],
                                               start=(kc == 0), stop=(kc == 7))
                            return ins
                        gt = S.op("T", mmg, deps=mdeps, name=f"{tag}_qkv_{b}_{tt}_{kind}{half}")
                        if kind in ("q", "k"):
                            is_q = kind == "q"
                            ctab = (0 if is_q else 2) * 1024
                            stab = (1 if is_q else 3) * 1024
                            dst = qrot if is_q else krot
                            toff = half * 512
                            e1dep = [gt, tabtok] + ([self.rope_scr] if self.rope_scr else [])
                            e1 = S.op("V", lambda v, psl=psl, ctab=ctab, toff=toff, width=width, tb_=tabtile:
                                      v.tensor_mul(rA[:, 0:width], ps[psl][:, 0:width],
                                                   tb_[:, ctab + toff:ctab + toff + width]), deps=e1dep)
                            e2 = S.op("V", lambda v, psl=psl, stab=stab, toff=toff, width=width, tb_=tabtile:
                                      v.tensor_tensor(
                                rB[:, 0:width].rearrange("p (bk h j) -> p bk h j", h=2, j=32),
                                ps[psl][:, 0:width].rearrange("p (bk h j) -> p bk h j", h=2, j=32)[:, :, ::-1, :],
                                tb_[:, stab + toff:stab + toff + width].rearrange("p (bk h j) -> p bk h j", h=2, j=32),
                                op=ALU.mult), deps=[gt, tabtok, e1])
                            e3dep = [e2] + ([self.rot_rd] if self.rot_rd else [])
                            e3 = S.op("V", lambda v, dst=dst, toff=toff, width=width:
                                      v.tensor_add(dst[:, toff:toff + width],
                                                   rA[:, 0:width], rB[:, 0:width]), deps=e3dep)
                            self.rope_scr = e3
                            self.ps_free[psl] = e2
                            if is_q:
                                rope_q[half] = e3
                            else:
                                rope_k[half] = e3
                        else:
                            koff = half * 8 if mode == "l1f" else 0
                            nk = width // 64
                            evdep = [gt, vset] + ([self.vst_last] if self.vst_last else [])
                            ev = S.op("V", lambda v, psl=psl, koff=koff, nk=nk, width=width:
                                      v.tensor_copy(
                                vasm[:].rearrange("p (k j) -> p k j", k=KV, j=65)[:, koff:koff + nk, 0:64],
                                ps[psl][:, 0:width].rearrange("p (k j) -> p k j", k=nk, j=64)),
                                deps=evdep)
                            self.ps_free[psl] = ev
                            vev_tok = ev
                    vst = [S.dma("Y", f"{tag}_vst",
                                 Vt[(k * ntile + gtile) * 128:(k * ntile + gtile + 1) * 128, :],
                                 vasm[:, k * 65:(k + 1) * 65], deps=[vev_tok]) for k in range(KV)]
                    self.vst_last = vst[-1]
                    def emit_tr(srcrot, nchunk, asm, rdeps, tt=tt):
                        evts = []
                        for hgrp in range((nchunk + 3) // 4):
                            lo, hi = hgrp * 4, min(nchunk, hgrp * 4 + 4)
                            tdeps = list(rdeps) + ([self.tr_ev] if self.tr_ev else [])
                            def trf(t, srcrot=srcrot, lo=lo, hi=hi):
                                for c in range(lo, hi):
                                    ins = t.transpose(ps_tr[:, (c - lo) * 128:(c - lo + 1) * 128],
                                                      srcrot[:, c * 128:(c + 1) * 128], self.idn_bf[:])
                                return ins
                            trt = S.op("T", trf, deps=tdeps)
                            evdeps = [trt] + list(self.asm_st)
                            def evf(v, asm=asm, lo=lo, hi=hi, tt=tt):
                                return v.tensor_copy(
                                    asm[:].rearrange("p (c q) -> p c q", q=512)[:, lo:hi, tt * 128:tt * 128 + 128],
                                    ps_tr[:, 0:(hi - lo) * 128].rearrange("p (c q) -> p c q", q=128))
                            evt = S.op("V", evf, deps=evdeps)
                            self.tr_ev = evt
                            evts.append(evt)
                        return evts
                    qev = emit_tr(qrot, 8, qf_asm, [t for t in rope_q if t])
                    kev = emit_tr(krot, 2 if mode == "l0" else 8, kf_asm, [t for t in rope_k if t])
                    self.rot_rd = qev[-1]
                    self.prev_tab[sl] = e2
                    self.prev_tab[sl] = qev[-1]
                    if tt == TB // 128 - 1:
                        qsts = [S.dma("Y", f"{tag}_qst", Qf[c * 128:(c + 1) * 128, tslice],
                                      qf_asm[:, c * 512:(c + 1) * 512], deps=[qev[-1]])
                                for c in range(8)]
                        nkc_k = 2 if mode == "l0" else 8
                        ksts = [S.dma("Y", f"{tag}_kst", Kf[c * 128:(c + 1) * 128, tslice],
                                      kf_asm[:, c * 512:(c + 1) * 512], deps=[kev[-1]])
                                for c in range(nkc_k)]
                        self.asm_st = [qsts[-1], ksts[-1]]
                prev_act_mm = ("T", S.ecnt["T"])
                prev_x_readers = norm_toks + [cast_toks[-1], sq_toks[-1]]
            S.barrier()

    def emit_attn_core(self, Qf, Kf, Vt, Of, causal, KV, tag, maskq=None, maskt=None):
        S, nc = self.S, self.nc
        seq = self.seq
        ntile = seq // 128
        NQB = seq // 512
        HPK = 16 // KV
        units = [(kv, kv * HPK + hh, qb) for kv in range(KV)
                 for hh in range(HPK) for qb in range(NQB)]
        gsize = HPK * NQB
        with contextlib.ExitStack() as ph:
            katt = [ph.enter_context(nc.sbuf_tensor(f"{tag}_katt{i}", [128, seq], BF16)) for i in range(2)]
            qatt = [ph.enter_context(nc.sbuf_tensor(f"{tag}_qatt{i}", [128, 512], BF16)) for i in range(2)]
            q128 = [ph.enter_context(nc.sbuf_tensor(f"{tag}_q128{i}", [128, 512], BF16)) for i in range(2)]
            vh = [ph.enter_context(nc.sbuf_tensor(f"{tag}_vh{i}", [128, ntile * 65], BF16)) for i in range(2)]
            nm4 = [ph.enter_context(nc.sbuf_tensor(f"{tag}_nm4{i}", [128, 4], BF16)) for i in range(2)]
            nmsb = [ph.enter_context(nc.sbuf_tensor(f"{tag}_nmsb{i}", [4, 128], BF16)) for i in range(2)]
            mxall = ph.enter_context(nc.sbuf_tensor(f"{tag}_mxall", [128, 16], F32))
            pt16 = [ph.enter_context(nc.sbuf_tensor(f"{tag}_pt{i}", [128, 512], BF16)) for i in range(2)]
            osb = ph.enter_context(nc.sbuf_tensor(f"{tag}_osb", [64, 512], F32))
            o16 = ph.enter_context(nc.sbuf_tensor(f"{tag}_o16", [64, 512], BF16))
            l16row = ph.enter_context(nc.sbuf_tensor(f"{tag}_l16row", [1, 512], BF16))
            rsb64 = ph.enter_context(nc.sbuf_tensor(f"{tag}_rsb64", [64, 512], F32))
            if causal:
                mq = ph.enter_context(nc.sbuf_tensor(f"{tag}_mq", [128, 4 * 512], F32))
                mt = ph.enter_context(nc.sbuf_tensor(f"{tag}_mt", [128, 4 * 512], F32))
            psS = [ph.enter_context(nc.psum_tensor(f"{tag}_psS{i}", [128, 512], F32)) for i in range(2)]
            psT = [ph.enter_context(nc.psum_tensor(f"{tag}_psT{i}", [128, 512], F32)) for i in range(2)]
            psO = [ph.enter_context(nc.psum_tensor(f"{tag}_psO{i}", [65, 512], F32)) for i in range(2)]
            psB = ph.enter_context(nc.psum_tensor(f"{tag}_psB", [64, 512], F32))
            psNM = ph.enter_context(nc.psum_tensor(f"{tag}_psNM", [4, 128], BF16))
            if causal:
                for i in range(4):
                    S.dma("Y", f"{tag}_mld", mq[:, i * 512:(i + 1) * 512], maskq[i * 128:(i + 1) * 128, :])
                    S.dma("Y", f"{tag}_mld", mt[:, i * 512:(i + 1) * 512], maskt[i * 128:(i + 1) * 128, :])
            kz = [S.op("V", lambda v, i=i: v.memset(katt[i][:], 0.0)) for i in range(2)]
            qz = [S.op("V", lambda v, i=i: v.memset(qatt[i][:], 0.0)) for i in range(2)]
            q2z = [S.op("V", lambda v, i=i: v.memset(q128[i][:], 0.0)) for i in range(2)]
            S.barrier()

            st_state = {}
            self.ac_kv = [None, None]
            self.ac_kv_users = [None, None]
            self.ac_q_users = [None, None]
            self.ac_psS = [None, None]
            self.ac_psT = [None, None]
            self.ac_pt = [None, None]
            self.ac_nm = [None, None]
            self.ac_oev = None
            self.ac_ost = None
            self.ac_rfin = None
            self.ac_psO = [None, None]

            def buildA(i):
                kv, h, qb = units[i]
                ksl = kv % 2
                usl = i % 2
                if i % gsize == 0:
                    kdeps = [self.ac_kv_users[ksl]] if self.ac_kv_users[ksl] else []
                    kdeps += [kz[ksl]]
                    ktok = S.dma("Y", f"{tag}_kld{ksl}", katt[ksl][0:64, :],
                                 Kf[kv * 64:(kv + 1) * 64, :], deps=kdeps)
                    kone = S.op("V", lambda v: v.memset(katt[ksl][64:65, :], 1.0), deps=kdeps)
                    vtok = [S.dma("Y", f"{tag}_vld{ksl}", vh[ksl][:, kc * 65:(kc + 1) * 65],
                                  Vt[(kv * ntile + kc) * 128:(kv * ntile + kc + 1) * 128, :],
                                  deps=kdeps) for kc in range(ntile)][-1]
                    self.ac_kv[ksl] = (ktok, kone, vtok)
                ktok, kone, vtok = self.ac_kv[ksl]
                nkb = (qb + 1) if causal else NQB
                qdep = [self.ac_q_users[usl], qz[usl], q2z[usl]] if self.ac_q_users[usl] \
                    else [qz[usl], q2z[usl]]
                qtok = S.dma("Y", f"{tag}_qld{usl}", qatt[usl][0:64, :],
                             Qf[h * 64:(h + 1) * 64, qb * 512:(qb + 1) * 512], deps=qdep)
                qtok2 = S.dma("Y", f"{tag}_q2ld{usl}", q128[usl][0:64, :],
                              Qf[h * 64:(h + 1) * 64, qb * 512:(qb + 1) * 512], deps=qdep)
                nm_writes = []
                red_toks = []
                thunks = []

                def piece_parts(qt, kb):
                    psl = self.ac_nredg % 2
                    self.ac_nredg += 1
                    sdeps = [qtok, qtok2, ktok, kone]
                    if self.ac_psS[psl]:
                        sdeps.append(self.ac_psS[psl])

                    def safn(t, qt=qt, kb=kb, psl=psl):
                        return t.matmul(
                            psS[psl][:], q128[usl][:, qt * 128:(qt + 1) * 128],
                            katt[ksl][:, kb * 512:(kb + 1) * 512], start=True, stop=True)

                    def post(smt, qt=qt, kb=kb, psl=psl):
                        rdep = [smt]
                        if causal and kb == qb:
                            mk = S.op("V", lambda v, psl=psl, qt=qt: v.tensor_tensor(
                                psS[psl][:], psS[psl][:], mq[:, qt * 512:(qt + 1) * 512],
                                op=ALU.add), deps=[smt], name=f"{tag}_Smask")
                            rdep = [mk]
                        if kb == 0 and qt == 0 and self.ac_rfin:
                            rdep.append(self.ac_rfin)
                        red = S.op("V", lambda v, psl=psl, kb=kb: v.reduce_max(
                            mxall[:, kb:kb + 1], psS[psl][:], axis=AX.X), deps=rdep,
                            name=f"{tag}_red")
                        self.ac_psS[psl] = red
                        red_toks.append(red)
                        if kb == nkb - 1:
                            nmdep = list(red_toks) + ([self.ac_nm[usl]] if self.ac_nm[usl] else [])
                            red_toks.clear()
                            rfin = S.op("V", lambda v: v.reduce_max(
                                mxall[:, 15:16], mxall[:, 0:nkb], axis=AX.X), deps=nmdep)
                            nmw = S.op("V", lambda v, qt=qt: v.tensor_scalar_mul(
                                nm4[usl][:, qt:qt + 1], mxall[:, 15:16], -1.0), deps=[rfin])
                            self.ac_rfin = nmw
                            nm_writes.append(nmw)
                        if qt == 3 and kb == nkb - 1:
                            ntr = S.op("T", lambda t: t.transpose(
                                psNM[:], nm4[usl][:], self.idn_bf[:]), deps=nm_writes +
                                ([self.ac_nm[usl]] if self.ac_nm[usl] else []), name=f"{tag}_nmtr")
                            nev = S.op("V", lambda v: v.tensor_copy(nmsb[usl][:], psNM[:]),
                                       deps=[ntr], name=f"{tag}_nmev")
                            self.ac_nm[usl] = nev
                            nmd = S.dma("G", f"{tag}_nmdma{usl}", qatt[usl][64:65, 0:512],
                                        nmsb[usl][:], deps=[nev, qtok])
                            st_state[i] = (qtok, nmd, ktok, kone, vtok)
                    return safn, sdeps, post

                for qt in range(4):
                    for kb in range(nkb):
                        thunks.append(lambda qt=qt, kb=kb: piece_parts(qt, kb))
                return thunks

            def emitB(i, athunks):
                kv, h, qb = units[i]
                ksl = kv % 2
                usl = i % 2
                sl2 = i % 2
                qtok, nmd, ktok, kone, vtok = st_state.pop(i)
                nkc = 4 * (qb + 1) if causal else ntile
                exp_toks = [None] * nkc
                pv_last = None
                adone = 0
                na = len(athunks)

                def emitPVSA(kc_pv, parts):
                    nonlocal pv_last
                    fns = []
                    deps = []
                    if kc_pv is not None:
                        psl = kc_pv % 2
                        deps += [exp_toks[kc_pv], vtok]
                        if kc_pv == 0 and self.ac_psO[sl2]:
                            deps.append(self.ac_psO[sl2])

                        def pvfn(t, kc=kc_pv, psl=psl):
                            return t.matmul(
                                psO[sl2][:], vh[ksl][:, kc * 65:(kc + 1) * 65], pt16[psl][:],
                                start=(kc == 0), stop=(kc == nkc - 1))
                        fns.append(pvfn)
                    for (safn, sadeps, _) in parts:
                        fns.append(safn)
                        deps += sadeps
                    if not fns:
                        return
                    tok = S.op("T", lambda t, fns=tuple(fns): [f(t) for f in fns][-1],
                               deps=deps, name=f"{tag}_PVSA")
                    if kc_pv is not None:
                        self.ac_pt[kc_pv % 2] = tok
                        pv_last = tok
                    for (_, _, post) in parts:
                        post(tok)

                for kc in range(nkc):
                    psl = kc % 2
                    stdeps = [nmd, ktok, kone]
                    if self.ac_psT[psl]:
                        stdeps.append(self.ac_psT[psl])
                    stt_ = S.op("T", lambda t, kc=kc, psl=psl: t.matmul(
                        psT[psl][:], katt[ksl][:, kc * 128:(kc + 1) * 128], qatt[usl][:],
                        start=True, stop=True), deps=stdeps, name=f"{tag}_ST")
                    edep = [stt_]
                    if causal and kc >= 4 * qb:
                        mk = S.op("V", lambda v, psl=psl, kc=kc, qb=qb: v.tensor_tensor(
                            psT[psl][:], psT[psl][:], mt[:, (kc - 4 * qb) * 512:(kc - 4 * qb + 1) * 512],
                            op=ALU.add), deps=[stt_], name=f"{tag}_Tmask")
                        edep = [mk]
                    if self.ac_pt[psl]:
                        edep.append(self.ac_pt[psl])
                    ex = S.op("S", lambda s, psl=psl: s.activation(
                        out=pt16[psl][:], in_=psT[psl][:], func=AF.Exp), deps=edep,
                        name=f"{tag}_exp")
                    self.ac_psT[psl] = ex
                    exp_toks[kc] = ex
                    parts = []
                    while adone * nkc < (kc + 1) * na:
                        parts.append(athunks[adone]())
                        adone += 1
                    emitPVSA(kc - 1 if kc > 0 else None, parts)
                parts = [athunks[j]() for j in range(adone, na)]
                adone = na
                emitPVSA(nkc - 1, parts)
                l1 = S.op("S", lambda s: s.copy(l16row[:], psO[sl2][64:65, :]),
                          deps=[pv_last] + ([self.ac_oev] if self.ac_oev else []), name=f"{tag}_lrow")
                oc = S.op("S", lambda s: s.copy(osb[:], psO[sl2][0:64, :]),
                          deps=[pv_last] + ([self.ac_oev] if self.ac_oev else []), name=f"{tag}_ocopy")
                bmm = S.op("T", lambda t: t.matmul(psB[:], self.ones_row_bf[0:1, 0:64],
                                                   l16row[:], start=True, stop=True),
                           deps=[l1] + ([self.ac_oev] if self.ac_oev else []), name=f"{tag}_bmm")
                rec = S.op("V", lambda v: v.reciprocal(rsb64[:], psB[:]),
                           deps=[bmm] + ([self.ac_oev] if self.ac_oev else []), name=f"{tag}_rec")
                oev = S.op("V", lambda v: v.tensor_tensor(o16[:], osb[:], rsb64[:], op=ALU.mult),
                           deps=[rec, oc] + ([self.ac_ost] if self.ac_ost else []), name=f"{tag}_oev")
                ost = S.dma("Y", f"{tag}_ost", Of[h * 64:(h + 1) * 64, qb * 512:(qb + 1) * 512],
                            o16[:], deps=[oev])
                self.ac_oev = oev
                self.ac_ost = ost
                self.ac_psO[sl2] = oev
                self.ac_q_users[usl] = pv_last
                self.ac_kv_users[ksl] = pv_last

            self.ac_nredg = 0
            for th in buildA(0):
                safn, sadeps, post = th()
                tok = S.op("T", lambda t, safn=safn: safn(t), deps=sadeps,
                           name=f"{tag}_S0")
                post(tok)
            for i in range(len(units)):
                athunks = buildA(i + 1) if i + 1 < len(units) else []
                emitB(i, athunks)
            S.barrier()


    def emit_moe(self, zsrc, xsrc, xdst, wbf, cols, tag):
        S, nc = self.S, self.nc
        TB = self.tb
        with contextlib.ExitStack() as ph:
            zsb = ph.enter_context(nc.sbuf_tensor(f"{tag}_z", [128, 8 * TB], F32))
            xsb = ph.enter_context(nc.sbuf_tensor(f"{tag}_x", [128, 8 * TB], F32))
            xa = [ph.enter_context(nc.sbuf_tensor(f"{tag}_xa{i}", [128, TB], F32)) for i in range(2)]
            z16 = ph.enter_context(nc.sbuf_tensor(f"{tag}_z16", [128, 8 * TB], BF16))
            sq16 = ph.enter_context(nc.sbuf_tensor(f"{tag}_sq", [128, 8 * TB], BF16))
            h16 = ph.enter_context(nc.sbuf_tensor(f"{tag}_h16", [128, 8 * TB], BF16))
            he16 = ph.enter_context(nc.sbuf_tensor(f"{tag}_he", [128, 8 * TB], BF16))
            gl16 = ph.enter_context(nc.sbuf_tensor(f"{tag}_gl", [128, 32 * TB], BF16))
            macc = ph.enter_context(nc.sbuf_tensor(f"{tag}_macc", [128, 8 * TB], F32))
            sgb = [ph.enter_context(nc.sbuf_tensor(f"{tag}_sg{i}", [128, TB], BF16)) for i in range(2)]
            msb = ph.enter_context(nc.sbuf_tensor(f"{tag}_msb", [128, 4 * TB], BF16))
            m4 = ph.enter_context(nc.sbuf_tensor(f"{tag}_m4", [128, 16], BF16))
            mT = ph.enter_context(nc.sbuf_tensor(f"{tag}_mT", [4, 512], BF16))
            mrow = ph.enter_context(nc.sbuf_tensor(f"{tag}_mrow", [1, 4 * TB], BF16))
            rmax = ph.enter_context(nc.sbuf_tensor(f"{tag}_rmax", [128, 1], F32))
            stat = ph.enter_context(nc.sbuf_tensor(f"{tag}_st", [1, 6 * TB], F32))
            rsb = ph.enter_context(nc.sbuf_tensor(f"{tag}_rw", [128, 32], BF16))
            wsl = {"wv": [ph.enter_context(nc.sbuf_tensor(f"{tag}_wv{i}", [128, 1024], BF16)) for i in range(4)],
                   "pj": [ph.enter_context(nc.sbuf_tensor(f"{tag}_pj{i}", [128, 4096], BF16)) for i in range(4)]}
            WQE = ("Y", "Y", "Y", "Y")
            ps_sum = ph.enter_context(nc.psum_tensor(f"{tag}_psum", [1, TB], F32))
            ps_sq = ph.enter_context(nc.psum_tensor(f"{tag}_psq", [1, TB], F32))
            ps_A = ph.enter_context(nc.psum_tensor(f"{tag}_psA", [128, TB], F32))
            ps_C = ph.enter_context(nc.psum_tensor(f"{tag}_psC", [128, TB], F32))
            ps = [ph.enter_context(nc.psum_tensor(f"{tag}_ps{i}", [128, TB], F32)) for i in range(2)]
            ps_r = ph.enter_context(nc.psum_tensor(f"{tag}_psr", [128, 4], F32))
            ps_nm = ph.enter_context(nc.psum_tensor(f"{tag}_psnm", [4, 128], BF16))
            for kc in range(8):
                S.dma("Y", f"{tag}_rld", rsb[:, kc * 4:(kc + 1) * 4],
                      wbf["router"][kc * 128:(kc + 1) * 128, :])
            S.barrier()
            prev_x_readers = None
            prev_statreaders = []
            prev_psAC = None
            prev_xa = [None, None]
            prev_h_mm = None
            prev_he_mm = None
            prev_gl_mm = None
            prev_m4 = None
            prev_mT = None
            prev_mrow_rd = None
            prev_msb_rd = None
            prev_sg = [None, None]
            wtoks = {"wv": [None] * 4, "pj": [None] * 4}
            ltoks = {"wv": [None] * 4, "pj": [None] * 4}
            prev_macc_st = None
            for b in range(self.nb):
                tslice = slice(b * TB, (b + 1) * TB)
                xdep = list(prev_x_readers) if prev_x_readers else []
                zts = [S.dma("Y", f"{tag}_zin", zsb[:, c * TB:(c + 1) * TB],
                             zsrc[c * 128:(c + 1) * 128, tslice], deps=xdep) for c in range(8)]
                xts = [S.dma("Y", f"{tag}_xin", xsb[:, c * TB:(c + 1) * TB],
                             xsrc[c * 128:(c + 1) * 128, tslice], deps=xdep) for c in range(8)]
                ztok, xtok = zts[-1], xts[-1]
                cast_toks, sq_toks = [], []
                for c in range(8):
                    cast_toks.append(S.op("V", lambda v, c=c: v.tensor_copy(
                        z16[:, c * TB:(c + 1) * TB], zsb[:, c * TB:(c + 1) * TB]), deps=[ztok]))
                    sq_toks.append(S.op("S", lambda s, c=c: s.activation(
                        out=sq16[:, c * TB:(c + 1) * TB], in_=zsb[:, c * TB:(c + 1) * TB],
                        func=AF.Square), deps=[ztok]))
                def mmsum(t):
                    for c in range(8):
                        ins = t.matmul(ps_sum[:], self.ones_col_bf[:], z16[:, c * TB:(c + 1) * TB],
                                       start=(c == 0), stop=(c == 7))
                    return ins
                def mmsq(t):
                    for c in range(8):
                        ins = t.matmul(ps_sq[:], self.ones_col_bf[:], sq16[:, c * TB:(c + 1) * TB],
                                       start=(c == 0), stop=(c == 7))
                    return ins
                sdep = cast_toks + sq_toks
                tsum = S.op("T", mmsum, deps=sdep)
                tsq = S.op("T", mmsq, deps=sdep)
                mu = stat[:, 0:TB]; ex2 = stat[:, TB:2 * TB]; var = stat[:, 2 * TB:3 * TB]
                std = stat[:, 3 * TB:4 * TB]; rinv = stat[:, 4 * TB:5 * TB]; nmr = stat[:, 5 * TB:6 * TB]
                v1 = S.op("V", lambda v: v.tensor_scalar_mul(mu, ps_sum[:], 1.0 / D),
                          deps=[tsum] + prev_statreaders)
                v2 = S.op("V", lambda v: v.tensor_scalar_mul(ex2, ps_sq[:], 1.0 / D), deps=[tsq])
                v3 = S.op("V", lambda v: v.scalar_tensor_tensor(
                    var, mu, 1.0, mu, op0=ALU.mult, op1=ALU.mult), deps=[v1])
                v4 = S.op("V", lambda v: v.tensor_sub(var, ex2, var), deps=[v2, v3])
                s5 = S.op("S", lambda s: s.activation(out=std, in_=var, func=AF.Sqrt, bias=EPS),
                          deps=[v4])
                v6 = S.op("V", lambda v: v.reciprocal(rinv, std), deps=[s5])
                v7 = S.op("V", lambda v: v.scalar_tensor_tensor(
                    nmr, mu, -1.0, rinv, op0=ALU.mult, op1=ALU.mult), deps=[v6, v1])
                pdep = [prev_psAC] if prev_psAC else []
                tA = S.op("T", lambda t: t.matmul(ps_A[:], self.ones_row_f[:], rinv,
                                                  start=True, stop=True), deps=[v6] + pdep)
                tC = S.op("T", lambda t: t.matmul(ps_C[:], self.ones_row_f[:], nmr,
                                                  start=True, stop=True), deps=[v7] + pdep)
                norm_toks = []
                for c in range(8):
                    sl2 = c % 2
                    d1 = [tA, ztok] + ([prev_xa[sl2]] if prev_xa[sl2] else [])
                    n1 = S.op("V", lambda v, c=c: v.tensor_mul(
                        xa[c % 2][:], zsb[:, c * TB:(c + 1) * TB], ps_A[:]), deps=d1)
                    d2 = [tC, n1] + ([prev_h_mm] if prev_h_mm else [])
                    n2 = S.op("V", lambda v, c=c: v.tensor_tensor(
                        xa[c % 2][:], xa[c % 2][:], ps_C[:], op=ALU.add), deps=d2)
                    n3 = S.op("V", lambda v, c=c: v.tensor_scalar(
                        h16[:, c * TB:(c + 1) * TB], xa[c % 2][:],
                        self.col(cols["ln2w"][0], c), self.col(cols["ln2b"][0], c),
                        op0=ALU.mult, op1=ALU.add), deps=[n2])
                    prev_xa[sl2] = n3
                    norm_toks.append(n3)
                prev_statreaders = [tA, tC]
                prev_psAC = norm_toks[-1]
                m4w = []
                for tt in range(TB // 128):
                    rdep = norm_toks + ([prev_m4] if prev_m4 else [])
                    def mmr(t, tt=tt):
                        for kc in range(8):
                            ins = t.matmul(ps_r[:], h16[:, kc * TB + tt * 128:kc * TB + tt * 128 + 128],
                                           rsb[:, kc * 4:(kc + 1) * 4], start=(kc == 0), stop=(kc == 7))
                        return ins
                    rt = S.op("T", mmr, deps=rdep, name=f"{tag}_router")
                    rm = S.op("V", lambda v: v.reduce_max(rmax[:], ps_r[:], axis=AX.X), deps=[rt])
                    mw = S.op("V", lambda v, tt=tt: v.tensor_scalar(
                        m4[:, tt * 4:(tt + 1) * 4], ps_r[:], rmax[:, 0:1], None,
                        op0=ALU.is_ge), deps=[rm])
                    prev_m4 = mw
                    m4w.append(mw)
                mTev = []
                for tt in range(TB // 128):
                    trd = [m4w[tt]] + ([prev_mT] if prev_mT else [])
                    ttr = S.op("T", lambda t, tt=tt: t.transpose(
                        ps_nm[:], m4[:, tt * 4:(tt + 1) * 4], self.idn_bf[:]), deps=trd)
                    tev = S.op("V", lambda v, tt=tt: v.tensor_copy(
                        mT[:, tt * 128:(tt + 1) * 128], ps_nm[:]), deps=[ttr] +
                        ([prev_mrow_rd] if prev_mrow_rd else []))
                    prev_mT = tev
                    mTev.append(tev)
                mrd = S.dma("G", f"{tag}_mrow", mrow[0:1, :], mT[:],
                            deps=mTev + [("T", S.ecnt["T"])])
                prev_mrow_rd = None
                exp_toks = []
                psA_rd = prev_psAC
                for e in range(4):
                    bdep = [mrd, psA_rd] + ([exp_toks[-1][0]] if exp_toks else [])
                    bc = S.op("T", lambda t, e=e: t.matmul(
                        ps_A[:], self.ones_row_bf[:],
                        mrow[0:1, e * TB:(e + 1) * TB], start=True, stop=True),
                        deps=bdep, name=f"{tag}_mbc")
                    mev = S.op("V", lambda v, e=e: v.tensor_copy(
                        msb[:, e * TB:(e + 1) * TB], ps_A[:]), deps=[bc] +
                        ([prev_msb_rd] if prev_msb_rd else []))
                    psA_rd = mev
                    hetoks = []
                    for c in range(8):
                        hdep = [mev, norm_toks[c]] + ([prev_he_mm] if prev_he_mm else [])
                        hetoks.append(S.op("V", lambda v, c=c, e=e: v.tensor_mul(
                            he16[:, c * TB:(c + 1) * TB], h16[:, c * TB:(c + 1) * TB],
                            msb[:, e * TB:(e + 1) * TB]), deps=hdep))
                    gl_toks = []
                    lastmm = None
                    prev_sg_tok = getattr(self, "_moe_prev_sg", None)
                    for ci in range(32):
                        for half, o in ((0, ci), (1, 32 + ci)):
                            sl = (2 * ci + half) % 4
                            dep = [wtoks["wv"][sl]] if wtoks["wv"][sl] else []
                            ltoks["wv"][sl] = S.dma(WQE[sl], f"{tag}_wvl{sl}", wsl["wv"][sl][:],
                                                    wbf["wv"][e][o * 128:(o + 1) * 128, :], deps=dep)
                            pdeps = hetoks + [ltoks["wv"][sl]]
                            psl = half
                            if half == 0 and prev_sg_tok is not None:
                                pdeps.append(prev_sg_tok)
                            if half == 1 and len(gl_toks) >= 1:
                                pdeps.append(gl_toks[-1])
                            def mmwv(t, o=o, psl=psl, sl=sl):
                                for k in range(8):
                                    ins = t.matmul(ps[psl][:], wsl["wv"][sl][:, k * 128:(k + 1) * 128],
                                                   he16[:, k * TB:(k + 1) * TB], start=(k == 0),
                                                   stop=(k == 7))
                                return ins
                            mm = S.op("T", mmwv, deps=pdeps, name=f"{tag}_wv{e}_{o}")
                            wtoks["wv"][sl] = mm
                            if half == 0:
                                sdep2 = [mm] + ([prev_sg[ci % 2]] if prev_sg[ci % 2] else [])
                                sg = S.op("S", lambda s, ci=ci: s.activation(
                                    out=sgb[ci % 2][:], in_=ps[0][:], func=AF.Silu), deps=sdep2)
                                prev_sg[ci % 2] = None
                                prev_sg_tok = sg
                                self._moe_prev_sg = sg
                                gmm = mm
                            else:
                                gdep = [sg, mm] + ([prev_gl_mm] if prev_gl_mm and ci == 0 else [])
                                gl = S.op("V", lambda v, ci=ci: v.tensor_tensor(
                                    gl16[:, ci * TB:(ci + 1) * TB], sgb[ci % 2][:], ps[1][:],
                                    op=ALU.mult), deps=gdep)
                                prev_sg[ci % 2] = gl
                                gl_toks.append(gl)
                    ev_toks = []
                    for o in range(8):
                        sl = o % 4
                        psl = o % 2
                        dep = [wtoks["pj"][sl]] if wtoks["pj"][sl] else []
                        ltoks["pj"][sl] = S.dma(WQE[sl], f"{tag}_pjl{sl}", wsl["pj"][sl][:],
                                                wbf["pj"][e][o * 128:(o + 1) * 128, :], deps=dep)
                        pdeps = gl_toks + [ltoks["pj"][sl]]
                        if len(ev_toks) >= 2:
                            pdeps.append(ev_toks[-2])
                        def mmpj(t, o=o, sl=sl, psl=psl):
                            for k in range(32):
                                ins = t.matmul(ps[psl][:], wsl["pj"][sl][:, k * 128:(k + 1) * 128],
                                               gl16[:, k * TB:(k + 1) * TB], start=(k == 0),
                                               stop=(k == 31))
                            return ins
                        mm = S.op("T", mmpj, deps=pdeps, name=f"{tag}_pj{e}_{o}")
                        wtoks["pj"][sl] = mm
                        if e == 0:
                            mdep = [mm] + ([prev_macc_st] if prev_macc_st else [])
                            ev = S.op("V", lambda v, o=o, psl=psl: v.tensor_copy(
                                macc[:, o * TB:(o + 1) * TB], ps[psl][:]), deps=mdep)
                        else:
                            ev = S.op("V", lambda v, o=o, psl=psl: v.tensor_tensor(
                                macc[:, o * TB:(o + 1) * TB], ps[psl][:],
                                macc[:, o * TB:(o + 1) * TB], op=ALU.add), deps=[mm, exp_toks[-1][1][o]])
                        ev_toks.append(ev)
                    exp_toks.append((mm, ev_toks))
                    prev_he_mm = ("T", S.ecnt["T"])
                    prev_gl_mm = ("T", S.ecnt["T"])
                prev_msb_rd = ("V", S.ecnt["V"])
                prev_psAC = psA_rd
                stE = []
                evE = []
                for c in range(8):
                    e1 = S.op("V", lambda v, c=c: v.scalar_tensor_tensor(
                        xsb[:, c * TB:(c + 1) * TB], zsb[:, c * TB:(c + 1) * TB], 0.5,
                        xsb[:, c * TB:(c + 1) * TB], op0=ALU.mult, op1=ALU.add),
                        deps=[xtok, ztok, norm_toks[c], exp_toks[-1][1][c], cast_toks[c], sq_toks[c]])
                    e2 = S.op("V", lambda v, c=c: v.scalar_tensor_tensor(
                        xsb[:, c * TB:(c + 1) * TB], macc[:, c * TB:(c + 1) * TB], 0.5,
                        xsb[:, c * TB:(c + 1) * TB], op0=ALU.mult, op1=ALU.add), deps=[e1])
                    evE.append(e2)
                    stE.append(S.dma("Y", f"{tag}_xout", xdst[c * 128:(c + 1) * 128, tslice],
                                     xsb[:, c * TB:(c + 1) * TB], deps=[e2]))
                prev_macc_st = evE[-1]
                prev_x_readers = evE + [stE[-1]]
                prev_h_mm = ("T", S.ecnt["T"])
                prev_m4 = mTev[-1]
                prev_mrow_rd = ("T", S.ecnt["T"])
            S.barrier()

    def emit_proj_resid(self, srcs, w_pre, resid_src, xdst, tag, bias_col=None, oc_n=8):
        S, nc = self.S, self.nc
        TB = self.tb
        kc_n = sum(n for _, n in srcs)
        with contextlib.ExitStack() as ph:
            asb = ph.enter_context(nc.sbuf_tensor(f"{tag}_a", [128, kc_n * TB], BF16))
            xsb = ph.enter_context(nc.sbuf_tensor(f"{tag}_x", [128, oc_n * TB], F32))
            wsl = [ph.enter_context(nc.sbuf_tensor(f"{tag}_w{i}", [128, kc_n * 128], BF16))
                   for i in range(2)]
            ps = [ph.enter_context(nc.psum_tensor(f"{tag}_ps{i}", [128, TB], F32)) for i in range(2)]
            wtoks = [None, None]
            ltoks = [None, None]
            prev_rd = None
            prev_wr = None
            for b in range(self.nb):
                tslice = slice(b * TB, (b + 1) * TB)
                adep = [prev_rd] if prev_rd else []
                ats = []
                cbase = 0
                for (ap, nch) in srcs:
                    for c in range(nch):
                        ats.append(S.dma("Y", f"{tag}_ain", asb[:, (cbase + c) * TB:(cbase + c + 1) * TB],
                                         ap[c * 128:(c + 1) * 128, tslice], deps=adep))
                    cbase += nch
                atok = ats[-1]
                xdep = [prev_wr] if prev_wr else []
                xts = [S.dma("Y", f"{tag}_xin", xsb[:, c * TB:(c + 1) * TB],
                             resid_src[c * 128:(c + 1) * 128, tslice], deps=xdep)
                       for c in range(oc_n)]
                xtok = xts[-1]
                evs = []
                for o in range(oc_n):
                    sl = o % 2
                    if b == 0 and o < 2:
                        dep = [wtoks[sl]] if wtoks[sl] else []
                        ltoks[sl] = S.dma("Y", f"{tag}_wld{sl}", wsl[sl][:],
                                          w_pre[o * 128:(o + 1) * 128, :], deps=dep)
                    pdeps = [ltoks[sl], atok]
                    if len(evs) >= 2:
                        pdeps.append(evs[-2])
                    def mmg(t, o=o, sl=sl):
                        for k in range(kc_n):
                            ins = t.matmul(ps[sl][:], wsl[sl][:, k * 128:(k + 1) * 128],
                                           asb[:, k * TB:(k + 1) * TB], start=(k == 0),
                                           stop=(k == kc_n - 1))
                        return ins
                    te = S.op("T", mmg, deps=pdeps, name=f"{tag}_mm{b}_{o}")
                    wtoks[sl] = te
                    nxt = o + 2
                    if nxt < oc_n or b + 1 < self.nb:
                        wo = nxt % oc_n
                        ltoks[sl] = S.dma("Y", f"{tag}_wld{sl}", wsl[sl][:],
                                          w_pre[wo * 128:(wo + 1) * 128, :], deps=[te])
                    if bias_col is not None:
                        ev = S.op("V", lambda v, o=o, sl=sl: v.scalar_tensor_tensor(
                            xsb[:, o * TB:(o + 1) * TB], ps[sl][:], self.col(bias_col[0], o),
                            xsb[:, o * TB:(o + 1) * TB], op0=ALU.add, op1=ALU.add),
                            deps=[te, xtok], name=f"{tag}_ev{b}_{o}")
                    else:
                        ev = S.op("V", lambda v, o=o, sl=sl: v.scalar_tensor_tensor(
                            xsb[:, o * TB:(o + 1) * TB], ps[sl][:], 1.0,
                            xsb[:, o * TB:(o + 1) * TB], op0=ALU.mult, op1=ALU.add),
                            deps=[te, xtok], name=f"{tag}_ev{b}_{o}")
                    evs.append(ev)
                    last_st = S.dma("Y", f"{tag}_xout", xdst[o * 128:(o + 1) * 128, tslice],
                                    xsb[:, o * TB:(o + 1) * TB], deps=[ev])
                prev_rd = te
                prev_wr = last_st
            S.barrier()

    def emit_l1a_proj(self, xsrc, Qf, Kf, Af, Bf, Vtok, w, cols, tag):
        S, nc = self.S, self.nc
        TB = self.tb
        with contextlib.ExitStack() as ph:
            xsb = ph.enter_context(nc.sbuf_tensor(f"{tag}_x", [128, 8 * TB], F32))
            xa = [ph.enter_context(nc.sbuf_tensor(f"{tag}_xa{i}", [128, TB], F32)) for i in range(2)]
            x16 = ph.enter_context(nc.sbuf_tensor(f"{tag}_x16", [128, 8 * TB], BF16))
            sq16 = ph.enter_context(nc.sbuf_tensor(f"{tag}_sq", [128, 8 * TB], BF16))
            t16 = ph.enter_context(nc.sbuf_tensor(f"{tag}_t16", [128, 8 * TB], BF16))
            stat = ph.enter_context(nc.sbuf_tensor(f"{tag}_st", [1, 6 * TB], F32))
            out16 = ph.enter_context(nc.sbuf_tensor(f"{tag}_o16", [128, TB], BF16))
            vout = ph.enter_context(nc.sbuf_tensor(f"{tag}_vout", [128, 1024], BF16))
            wvsb = ph.enter_context(nc.sbuf_tensor(f"{tag}_wv", [128, 8 * 1024], BF16))
            bvsb = ph.enter_context(nc.sbuf_tensor(f"{tag}_bv", [1, 1024], F32))
            bv16 = ph.enter_context(nc.sbuf_tensor(f"{tag}_bv16", [1, 1024], BF16))
            wsl = [ph.enter_context(nc.sbuf_tensor(f"{tag}_w{i}", [128, 1024], BF16)) for i in range(2)]
            ps_sum = ph.enter_context(nc.psum_tensor(f"{tag}_psum", [1, TB], F32))
            ps_sq = ph.enter_context(nc.psum_tensor(f"{tag}_psq", [1, TB], F32))
            ps_A = ph.enter_context(nc.psum_tensor(f"{tag}_psA", [128, TB], F32))
            ps_C = ph.enter_context(nc.psum_tensor(f"{tag}_psC", [128, TB], F32))
            ps = [ph.enter_context(nc.psum_tensor(f"{tag}_ps{i}", [128, TB], F32)) for i in range(2)]
            for kc in range(8):
                S.dma("Y", f"{tag}_wvl", wvsb[:, kc * 1024:(kc + 1) * 1024],
                      w["wv_pl"][kc * 128:(kc + 1) * 128, :])
            bvl = S.dma("Y", f"{tag}_bvl", bvsb[:], w["bv_row"][:])
            bvc = S.op("V", lambda v: v.tensor_copy(bv16[:], bvsb[:]), deps=[bvl])
            S.barrier()
            projs = [("q", w["wq"], Qf, cols.get("bq32"), True),
                     ("k", w["wk"], Kf, cols.get("bk"), False),
                     ("a", w["wm1a"], Af, None, False),
                     ("b", w["wm1b"], Bf, None, False)]
            wtoks = [None, None]
            ltoks = [None, None]
            prev_x_readers = None
            prev_statreaders = []
            prev_psAC = None
            prev_xa = [None, None]
            prev_t_mm = None
            prev_o16 = None
            prev_vout = None
            for b in range(self.nb):
                tslice = slice(b * TB, (b + 1) * TB)
                xdep = list(prev_x_readers) if prev_x_readers else []
                xts = [S.dma("Y", f"{tag}_xin", xsb[:, c * TB:(c + 1) * TB],
                             xsrc[c * 128:(c + 1) * 128, tslice], deps=xdep) for c in range(8)]
                xtok = xts[-1]
                cast_toks, sq_toks = [], []
                for c in range(8):
                    cast_toks.append(S.op("V", lambda v, c=c: v.tensor_copy(
                        x16[:, c * TB:(c + 1) * TB], xsb[:, c * TB:(c + 1) * TB]), deps=[xtok]))
                    sq_toks.append(S.op("S", lambda s, c=c: s.activation(
                        out=sq16[:, c * TB:(c + 1) * TB], in_=xsb[:, c * TB:(c + 1) * TB],
                        func=AF.Square), deps=[xtok]))
                def mmsum(t):
                    for c in range(8):
                        ins = t.matmul(ps_sum[:], self.ones_col_bf[:], x16[:, c * TB:(c + 1) * TB],
                                       start=(c == 0), stop=(c == 7))
                    return ins
                def mmsq(t):
                    for c in range(8):
                        ins = t.matmul(ps_sq[:], self.ones_col_bf[:], sq16[:, c * TB:(c + 1) * TB],
                                       start=(c == 0), stop=(c == 7))
                    return ins
                sdep = cast_toks + sq_toks
                tsum = S.op("T", mmsum, deps=sdep)
                tsq = S.op("T", mmsq, deps=sdep)
                mu = stat[:, 0:TB]; ex2 = stat[:, TB:2 * TB]; var = stat[:, 2 * TB:3 * TB]
                std = stat[:, 3 * TB:4 * TB]; rinv = stat[:, 4 * TB:5 * TB]; nmr = stat[:, 5 * TB:6 * TB]
                v1 = S.op("V", lambda v: v.tensor_scalar_mul(mu, ps_sum[:], 1.0 / D),
                          deps=[tsum] + prev_statreaders)
                v2 = S.op("V", lambda v: v.tensor_scalar_mul(ex2, ps_sq[:], 1.0 / D), deps=[tsq])
                v3 = S.op("V", lambda v: v.scalar_tensor_tensor(
                    var, mu, 1.0, mu, op0=ALU.mult, op1=ALU.mult), deps=[v1])
                v4 = S.op("V", lambda v: v.tensor_sub(var, ex2, var), deps=[v2, v3])
                s5 = S.op("S", lambda s: s.activation(out=std, in_=var, func=AF.Sqrt, bias=EPS),
                          deps=[v4])
                v6 = S.op("V", lambda v: v.reciprocal(rinv, std), deps=[s5])
                v7 = S.op("V", lambda v: v.scalar_tensor_tensor(
                    nmr, mu, -1.0, rinv, op0=ALU.mult, op1=ALU.mult), deps=[v6, v1])
                pdep = [prev_psAC] if prev_psAC else []
                tA = S.op("T", lambda t: t.matmul(ps_A[:], self.ones_row_f[:], rinv,
                                                  start=True, stop=True), deps=[v6] + pdep)
                tC = S.op("T", lambda t: t.matmul(ps_C[:], self.ones_row_f[:], nmr,
                                                  start=True, stop=True), deps=[v7] + pdep)
                norm_toks = []
                for c in range(8):
                    sl2 = c % 2
                    d1 = [tA, xtok] + ([prev_xa[sl2]] if prev_xa[sl2] else [])
                    n1 = S.op("V", lambda v, c=c: v.tensor_mul(
                        xa[c % 2][:], xsb[:, c * TB:(c + 1) * TB], ps_A[:]), deps=d1)
                    d2 = [tC, n1] + ([prev_t_mm] if prev_t_mm else [])
                    n2 = S.op("V", lambda v, c=c: v.tensor_tensor(
                        t16[:, c * TB:(c + 1) * TB], xa[c % 2][:], ps_C[:], op=ALU.add), deps=d2)
                    prev_xa[sl2] = n2
                    norm_toks.append(n2)
                prev_statreaders = [tA, tC]
                prev_psAC = norm_toks[-1]
                evs = []
                for (pn, wpre, dst, bc, scaled) in projs:
                    for o in range(8):
                        sl = o % 2
                        dep = [wtoks[sl]] if wtoks[sl] else []
                        ltoks[sl] = S.dma("Y", f"{tag}_wld{sl}", wsl[sl][:],
                                          wpre[o * 128:(o + 1) * 128, :], deps=dep)
                        pdeps = [ltoks[sl]] + norm_toks
                        if len(evs) >= 2:
                            pdeps.append(evs[-2])
                        def mmg(t, o=o, sl=sl):
                            for k in range(8):
                                ins = t.matmul(ps[sl][:], wsl[sl][:, k * 128:(k + 1) * 128],
                                               t16[:, k * TB:(k + 1) * TB], start=(k == 0),
                                               stop=(k == 7))
                            return ins
                        te = S.op("T", mmg, deps=pdeps, name=f"{tag}_{pn}{b}_{o}")
                        wtoks[sl] = te
                        edep = [te] + ([prev_o16] if prev_o16 else [])
                        if bc is not None and scaled:
                            ev = S.op("S", lambda s, o=o, sl=sl, bc=bc: s.activation(
                                out=out16[:], in_=ps[sl][:], func=AF.Identity,
                                bias=self.col(bc[0], o), scale=1.0 / 32), deps=edep)
                        elif bc is not None:
                            ev = S.op("S", lambda s, o=o, sl=sl, bc=bc: s.activation(
                                out=out16[:], in_=ps[sl][:], func=AF.Identity,
                                bias=self.col(bc[0], o)), deps=edep)
                        else:
                            ev = S.op("S", lambda s, o=o, sl=sl: s.copy(
                                out16[:], ps[sl][:]), deps=edep)
                        st = S.dma("Y", f"{tag}_pst", dst[o * 128:(o + 1) * 128, tslice],
                                   out16[:], deps=[ev])
                        prev_o16 = st
                        evs.append(ev)
                for tt in range(TB // 128):
                    gtile = b * (TB // 128) + tt
                    for half in range(2):
                        psl = half
                        pdeps = norm_toks + [bvc, evs[-1], evs[-2]]
                        if prev_vout:
                            pdeps.append(prev_vout)
                        def mmv(t, tt=tt, half=half, psl=psl):
                            ins = t.matmul(ps[psl][:, 0:512], self.ones_row_bf[0:1, :],
                                           bv16[0:1, half * 512:(half + 1) * 512],
                                           start=True, stop=False)
                            for kc in range(8):
                                ins = t.matmul(ps[psl][:, 0:512],
                                               t16[:, kc * TB + tt * 128:kc * TB + tt * 128 + 128],
                                               wvsb[:, kc * 1024 + half * 512:kc * 1024 + (half + 1) * 512],
                                               start=False, stop=(kc == 7))
                            return ins
                        tv = S.op("T", mmv, deps=pdeps, name=f"{tag}_v{b}_{tt}_{half}")
                        vev = S.op("V", lambda v, half=half, psl=psl: v.tensor_copy(
                            vout[:, half * 512:(half + 1) * 512], ps[psl][:, 0:512]),
                            deps=[tv] + ([prev_vout] if prev_vout else []))
                        evs.append(vev)
                    vst = S.dma("Y", f"{tag}_vst", Vtok[gtile * 128:(gtile + 1) * 128, :],
                                vout[:], deps=[evs[-1], evs[-2]])
                    prev_vout = vst
                prev_t_mm = ("T", S.ecnt["T"])
                prev_x_readers = norm_toks + [cast_toks[-1], sq_toks[-1]]
            S.barrier()

    def emit_l1a_core(self, Qf, Kf, Vtok, Gf, w, tag):
        S, nc = self.S, self.nc
        seq = self.seq
        NQB = seq // 512
        with contextlib.ExitStack() as ph:
            kres = ph.enter_context(nc.sbuf_tensor(f"{tag}_k", [128, 8 * seq], BF16))
            qblk = ph.enter_context(nc.sbuf_tensor(f"{tag}_q", [128, 8 * 512], BF16))
            ssb = ph.enter_context(nc.sbuf_tensor(f"{tag}_s", [128, seq], F32))
            p16 = ph.enter_context(nc.sbuf_tensor(f"{tag}_p", [128, seq], BF16))
            m16 = ph.enter_context(nc.sbuf_tensor(f"{tag}_m", [128, seq], BF16))
            pt = ph.enter_context(nc.sbuf_tensor(f"{tag}_pt", [128, (seq // 128) * 512], BF16))
            t8 = ph.enter_context(nc.sbuf_tensor(f"{tag}_t8", [128, 8], F32))
            negm = ph.enter_context(nc.sbuf_tensor(f"{tag}_ngm", [128, 1], F32))
            lsum = ph.enter_context(nc.sbuf_tensor(f"{tag}_l", [128, 1], F32))
            linv = ph.enter_context(nc.sbuf_tensor(f"{tag}_li", [128, 1], F32))
            mq = ph.enter_context(nc.sbuf_tensor(f"{tag}_mq", [128, 4 * 512], F32))
            g16 = ph.enter_context(nc.sbuf_tensor(f"{tag}_g16", [128, 512], BF16))
            vsl = [ph.enter_context(nc.sbuf_tensor(f"{tag}_vs{i}", [128, 1024], BF16)) for i in range(2)]
            psS = [ph.enter_context(nc.psum_tensor(f"{tag}_psS{i}", [128, 512], F32)) for i in range(2)]
            psD = [ph.enter_context(nc.psum_tensor(f"{tag}_psD{i}", [128, 512], F32)) for i in range(4)]
            psTr = ph.enter_context(nc.psum_tensor(f"{tag}_ptr", [128, 512], BF16))
            for kc in range(8):
                S.dma("Y", f"{tag}_kld", kres[:, kc * seq:(kc + 1) * seq], Kf[kc * 128:(kc + 1) * 128, :])
            for i in range(4):
                S.dma("Y", f"{tag}_mld", mq[:, i * 512:(i + 1) * 512], w["maskq"][i * 128:(i + 1) * 128, :])
            S.barrier()
            prev_q_rd = None
            prev_s = [None, None]
            prev_ssb_rd = []
            prev_p_rd = None
            prev_pt_rd = None
            prev_tr = None
            prev_g = None
            prev_gst = None
            vltoks = [None, None]
            vwtoks = [None, None]
            for qb in range(NQB):
                qdep = [prev_q_rd] if prev_q_rd else []
                qts = [S.dma("Y", f"{tag}_qld", qblk[:, c * 512:(c + 1) * 512],
                             Qf[c * 128:(c + 1) * 128, qb * 512:(qb + 1) * 512], deps=qdep)
                       for c in range(8)]
                qtok = qts[-1]
                nkc = 4 * (qb + 1)
                tr_toks = []
                for qtl in range(4):
                    qt = qb * 4 + qtl
                    L = qt * 128 + 128
                    nkb = (L + 511) // 512
                    ev_toks = []
                    for kb in range(nkb):
                        wdt = min(512, L - kb * 512) if kb == nkb - 1 else 512
                        wdt = 512
                        psl = kb % 2
                        sdeps = [qtok] + ([prev_s[psl]] if prev_s[psl] else [])
                        def mms(t, qtl=qtl, kb=kb, psl=psl):
                            for c in range(8):
                                ins = t.matmul(psS[psl][:], qblk[:, c * 512 + qtl * 128:c * 512 + qtl * 128 + 128],
                                               kres[:, c * seq + kb * 512:c * seq + kb * 512 + 512],
                                               start=(c == 0), stop=(c == 7))
                            return ins
                        ts_ = S.op("T", mms, deps=sdeps, name=f"{tag}_S{qt}_{kb}")
                        if kb == nkb - 1:
                            ev = S.op("V", lambda v, psl=psl, kb=kb, qtl=qtl: v.tensor_tensor(
                                ssb[:, kb * 512:kb * 512 + 512], psS[psl][:],
                                mq[:, qtl * 512:(qtl + 1) * 512], op=ALU.add),
                                deps=[ts_] + (prev_ssb_rd or []), name=f"{tag}_Sev")
                        else:
                            ev = S.op("S", lambda s, psl=psl, kb=kb: s.copy(
                                ssb[:, kb * 512:kb * 512 + 512], psS[psl][:]),
                                deps=[ts_] + (prev_ssb_rd or []), name=f"{tag}_Sev")
                        prev_s[psl] = ev
                        ev_toks.append(ev)
                    t1 = S.op("V", lambda v, L=L: v.max(out=t8[:], in_=ssb[:, 0:L]),
                              deps=ev_toks, name=f"{tag}_top8")
                    t2 = S.op("V", lambda v: v.tensor_scalar_mul(negm[:], t8[:, 0:1], -1.0), deps=[t1])
                    t3dep = [t2] + ([prev_p_rd] if prev_p_rd else [])
                    t3 = S.op("S", lambda s, L=L: s.activation(
                        out=p16[:, 0:L], in_=ssb[:, 0:L], func=AF.Exp, bias=negm[:, 0:1]),
                        deps=t3dep, name=f"{tag}_exp")
                    t4 = S.op("V", lambda v, L=L: v.tensor_scalar(
                        m16[:, 0:L], ssb[:, 0:L], t8[:, 7:8], None, op0=ALU.is_ge),
                        deps=[t1] + ([prev_p_rd] if prev_p_rd else []), name=f"{tag}_thr")
                    t5 = S.op("V", lambda v, L=L: v.tensor_mul(p16[:, 0:L], p16[:, 0:L], m16[:, 0:L]),
                              deps=[t3, t4], name=f"{tag}_pm")
                    t6 = S.op("V", lambda v, L=L: v.reduce_sum(lsum[:], p16[:, 0:L], axis=AX.X),
                              deps=[t5], name=f"{tag}_lsum")
                    t7 = S.op("V", lambda v: v.reciprocal(linv[:], lsum[:]), deps=[t6])
                    t8_ = S.op("V", lambda v, L=L: v.tensor_scalar_mul(p16[:, 0:L], p16[:, 0:L],
                                                                       linv[:, 0:1]),
                               deps=[t7], name=f"{tag}_pn")
                    nblk = L // 128
                    for grp in range((nblk + 3) // 4):
                        lo, hi = grp * 4, min(nblk, grp * 4 + 4)
                        tdeps = [t8_] + ([prev_tr] if prev_tr else [])
                        def trf(t, lo=lo, hi=hi):
                            for c in range(lo, hi):
                                ins = t.transpose(psTr[:, (c - lo) * 128:(c - lo + 1) * 128],
                                                  p16[:, c * 128:(c + 1) * 128], self.idn_bf[:])
                            return ins
                        trt = S.op("T", trf, deps=tdeps)
                        evdeps = [trt] + ([prev_pt_rd] if prev_pt_rd else [])
                        def evf(v, lo=lo, hi=hi, qtl=qtl):
                            return v.tensor_copy(
                                pt[:].rearrange("p (c q) -> p c q", q=512)[:, lo:hi, qtl * 128:qtl * 128 + 128],
                                psTr[:, 0:(hi - lo) * 128].rearrange("p (c q) -> p c q", q=128))
                        evt = S.op("V", evf, deps=evdeps)
                        prev_tr = evt
                        tr_toks.append(evt)
                    prev_ssb_rd = [t4, t3]
                    prev_p_rd = ("T", S.ecnt["T"])
                for dd in range(1, 4):
                    kc = 4 * qb + dd
                    zt = S.op("V", lambda v, kc=kc, dd=dd: v.memset(
                        pt[:, kc * 512:kc * 512 + dd * 128], 0.0),
                        deps=[prev_pt_rd] if prev_pt_rd else [])
                    tr_toks.append(zt)
                gsts = []
                for half in range(2):
                    pv_toks = []
                    for kc in range(nkc):
                        sl = kc % 2
                        vdep = [vwtoks[sl]] if vwtoks[sl] else []
                        vltoks[sl] = S.dma("Y", f"{tag}_vld{sl}", vsl[sl][:],
                                           Vtok[kc * 128:(kc + 1) * 128, :], deps=vdep)
                        mdeps = [vltoks[sl]] + tr_toks
                        if kc == 0:
                            mdeps += [prev_g] if prev_g else []
                        def mpv(t, kc=kc, sl=sl, half=half, nkc=nkc):
                            for dh in range(4):
                                ins = t.matmul(psD[dh][:], vsl[sl][:, (half * 4 + dh) * 128:(half * 4 + dh + 1) * 128],
                                               pt[:, kc * 512:(kc + 1) * 512],
                                               start=(kc == 0), stop=(kc == nkc - 1))
                            return ins
                        pv = S.op("T", mpv, deps=mdeps, name=f"{tag}_pv{qb}_{half}_{kc}")
                        vwtoks[sl] = pv
                        pv_toks.append(pv)
                    for dh in range(4):
                        gdep = [pv_toks[-1]] + ([prev_gst] if prev_gst else [])
                        gev = S.op("S", lambda s, dh=dh: s.copy(g16[:], psD[dh][:]), deps=gdep,
                                   name=f"{tag}_gev")
                        gst = S.dma("Y", f"{tag}_gst",
                                    Gf[(half * 4 + dh) * 128:(half * 4 + dh + 1) * 128,
                                       qb * 512:(qb + 1) * 512], g16[:], deps=[gev])
                        prev_gst = gst
                        gsts.append(gst)
                        prev_g = gev
                prev_q_rd = ("T", S.ecnt["T"])
                prev_pt_rd = ("T", S.ecnt["T"])
            S.barrier()

    def emit_l1a_local(self, Af, Bf, Lf, w, cols, tag):
        S, nc = self.S, self.nc
        TB = self.tb
        with contextlib.ExitStack() as ph:
            asb = ph.enter_context(nc.sbuf_tensor(f"{tag}_a", [128, 8 * TB], BF16))
            bsb = ph.enter_context(nc.sbuf_tensor(f"{tag}_b", [128, 8 * (TB + 4)], BF16))
            zb = ph.enter_context(nc.sbuf_tensor(f"{tag}_zb", [128, TB], F32))
            sw = ph.enter_context(nc.sbuf_tensor(f"{tag}_sw", [128, TB], F32))
            acc = ph.enter_context(nc.sbuf_tensor(f"{tag}_acc", [128, TB], F32))
            sm16 = ph.enter_context(nc.sbuf_tensor(f"{tag}_sm", [128, 8 * TB], BF16))
            o16 = ph.enter_context(nc.sbuf_tensor(f"{tag}_o", [128, TB], BF16))
            wsl = [ph.enter_context(nc.sbuf_tensor(f"{tag}_w{i}", [128, 1024], BF16)) for i in range(2)]
            ps = [ph.enter_context(nc.psum_tensor(f"{tag}_ps{i}", [128, TB], F32)) for i in range(2)]
            S.barrier()
            wtoks = [None, None]
            ltoks = [None, None]
            prev_sm_mm = None
            prev_rd = None
            prev_ost = None
            for b in range(self.nb):
                tslice = slice(b * TB, (b + 1) * TB)
                adep = [prev_rd] if prev_rd else []
                ats = [S.dma("Y", f"{tag}_ain", asb[:, c * TB:(c + 1) * TB],
                             Af[c * 128:(c + 1) * 128, tslice], deps=adep) for c in range(8)]
                bts = []
                zpads = []
                for c in range(8):
                    boff = c * (TB + 4)
                    if b == 0:
                        zpads.append(S.op("V", lambda v, boff=boff: v.memset(
                            bsb[:, boff:boff + 4], 0.0), deps=adep))
                        bts.append(S.dma("Y", f"{tag}_bin", bsb[:, boff + 4:boff + 4 + TB],
                                         Bf[c * 128:(c + 1) * 128, tslice], deps=adep))
                    else:
                        bts.append(S.dma("Y", f"{tag}_bin", bsb[:, boff:boff + 4 + TB],
                                         Bf[c * 128:(c + 1) * 128, b * TB - 4:(b + 1) * TB],
                                         deps=adep))
                atok, btok = ats[-1], bts[-1]
                sm_toks = []
                for c in range(8):
                    boff = c * (TB + 4)
                    accdep = [atok, btok] + zpads + (sm_toks[-1:] if sm_toks else [])
                    prevacc = None
                    for wd in range(1, 5):
                        z1 = S.op("V", lambda v, c=c, boff=boff, wd=wd: v.tensor_tensor(
                            zb[:], asb[:, c * TB:(c + 1) * TB],
                            bsb[:, boff + 4 - wd:boff + 4 - wd + TB], op=ALU.add),
                            deps=accdep + ([prevacc] if prevacc else []))
                        if wd == 1:
                            sv = S.op("S", lambda s, c=c: s.activation(
                                out=acc[:], in_=zb[:], func=AF.Silu,
                                bias=self.col(cols["bm1"][0], c)), deps=[z1])
                            prevacc = sv
                        else:
                            sv = S.op("S", lambda s, c=c: s.activation(
                                out=sw[:], in_=zb[:], func=AF.Silu,
                                bias=self.col(cols["bm1"][0], c)), deps=[z1, prevacc])
                            av = S.op("V", lambda v: v.tensor_add(acc[:], acc[:], sw[:]),
                                      deps=[sv])
                            prevacc = av
                    smdep = [prevacc] + ([prev_sm_mm] if prev_sm_mm else [])
                    smt = S.op("V", lambda v, c=c: v.tensor_scalar_mul(
                        sm16[:, c * TB:(c + 1) * TB], acc[:], 0.25), deps=smdep)
                    sm_toks.append(smt)
                evs = []
                for o in range(8):
                    sl = o % 2
                    dep = [wtoks[sl]] if wtoks[sl] else []
                    ltoks[sl] = S.dma("Y", f"{tag}_wld{sl}", wsl[sl][:],
                                      w["wm2"][o * 128:(o + 1) * 128, :], deps=dep)
                    pdeps = [ltoks[sl]] + sm_toks
                    if len(evs) >= 2:
                        pdeps.append(evs[-2])
                    def mmg(t, o=o, sl=sl):
                        for k in range(8):
                            ins = t.matmul(ps[sl][:], wsl[sl][:, k * 128:(k + 1) * 128],
                                           sm16[:, k * TB:(k + 1) * TB], start=(k == 0), stop=(k == 7))
                        return ins
                    te = S.op("T", mmg, deps=pdeps, name=f"{tag}_mm{b}_{o}")
                    wtoks[sl] = te
                    ev = S.op("S", lambda s, o=o, sl=sl: s.activation(
                        out=o16[:], in_=ps[sl][:], func=AF.Identity,
                        bias=self.col(cols["bm2"][0], o)),
                        deps=[te] + ([prev_ost] if prev_ost else []))
                    st = S.dma("Y", f"{tag}_lst", Lf[o * 128:(o + 1) * 128, tslice], o16[:],
                               deps=[ev])
                    prev_ost = st
                    evs.append(ev)
                prev_sm_mm = te
                prev_rd = te
            S.barrier()



def build_program(seq=4096, tb=512, debug_outs=()):
    mk = MK(seq, tb)
    mk.debug_outs = list(debug_outs)
    S = mk.S
    x_in = mk.din("x", [seq, D], F32)
    out = mk.dout("out", [seq, D], F32)
    xA = mk.dscratch("xA", [D, seq], F32, debug=True)
    xB = mk.dscratch("xB", [D, seq], F32, debug=True)

    mk.emit_consts()
    S.barrier()
    return mk, x_in, out, xA, xB


def build_l0ffn_weights(mk, I):
    wbf = {}
    w1T = I["L0_ffn_w1_w"].T
    w2T = I["L0_ffn_w2_w"].T
    h1T = I["L0_ffn_h1_w"].T
    h2T = I["L0_ffn_h2_w"].T
    w3T = I["L0_ffn_w3_w"].T
    for name, w in (("w1", w1T), ("w2", w2T), ("h1", h1T), ("h2", h2T), ("w3", w3T)):
        wbf[name], _ = mk.precast("l0f_" + name, prechunk(np.asarray(w, np.float32)))
    scol = {}
    scol["s1"] = mk.load_cols("s1", colsplit(np.asarray(I["L0_ffn_w1_s"], np.float32)))
    scol["s2"] = mk.load_cols("s2", colsplit(np.asarray(I["L0_ffn_w2_s"], np.float32)))
    scol["h1s"] = mk.load_cols("h1s", colsplit(np.asarray(I["L0_ffn_h1_s"], np.float32)))
    scol["h2s"] = mk.load_cols("h2s", colsplit(np.asarray(I["L0_ffn_h2_s"], np.float32)))
    c0, _ = mk.load_cols("w3sh", colsplit(np.asarray(I["L0_ffn_w3_s"], np.float32)))
    mk.S.barrier()
    mk.S.op("V", lambda v: v.tensor_scalar_mul(
        mk.colsb[:, c0:c0 + 8], mk.colsb[:, c0:c0 + 8], 0.5))
    scol["w3sh"] = (c0, None)
    return wbf, scol


def rope_tables(seq):
    inv = 10000.0 ** (-np.arange(32, dtype=np.float64) / 32.0)
    fr = np.arange(seq, dtype=np.float64)[:, None] * inv[None, :]
    c = np.cos(fr); s = np.sin(fr)
    C64 = np.concatenate([c, c], 1)
    S64 = np.concatenate([-s, s], 1)
    CT = np.tile(C64, (1, 16))
    ST = np.tile(S64, (1, 16))
    f = lambda a: np.ascontiguousarray(a.astype(NBF))
    return f(CT / 8.0), f(ST / 8.0), f(CT), f(ST)


def causal_masks():
    i = np.arange(128)[:, None]
    j = np.arange(512)[None, :]
    mq = np.concatenate([np.where(j <= i + o * 128, 0.0, -1e30).astype(np.float32)
                         for o in range(4)], 0)
    mt = np.concatenate([np.where(d * 128 + i <= j, 0.0, -1e30).astype(np.float32)
                         for d in range(4)], 0)
    return np.ascontiguousarray(mq), np.ascontiguousarray(mt)


def build_l0attn_weights(mk, I):
    w = {}
    w["wq"], _ = mk.precast("l0a_wq", np.asarray(I["L0_attn_w_q"], np.float32))
    w["wk"], _ = mk.precast("l0a_wk", np.asarray(I["L0_attn_w_k"], np.float32))
    w["wv"], _ = mk.precast("l0a_wv", np.asarray(I["L0_attn_w_v"], np.float32))
    w["wo"], _ = mk.precast("l0a_wo", prechunk(np.asarray(I["L0_attn_w_o"], np.float32)))
    cq, sq, ck, sk = rope_tables(mk.seq)
    for nm, a in (("cq", cq), ("sq", sq), ("ck", ck), ("sk", sk)):
        ap = mk.din("c_" + nm, list(a.shape), BF16)
        mk.inputs["c_" + nm] = a
        w[nm] = ap
    return w


def build_l1ffn_weights(mk, I):
    w = {}
    w["attn_w"], _ = mk.precast("l1f_aw", np.asarray(I["L1_ffn_attn_w"], np.float32))
    w["proj"], _ = mk.precast("l1f_pw", prechunk(np.asarray(I["L1_ffn_proj_w"], np.float32)))
    w["router"], _ = mk.precast("l1f_rt", np.asarray(I["L1_ffn_router_w"], np.float32))
    w["wv"] = []
    w["pj"] = []
    for e in range(4):
        a, _ = mk.precast(f"l1f_wv{e}", prechunk(np.asarray(I["L1_ffn_expert_wv"][e], np.float32)))
        w["wv"].append(a)
        a, _ = mk.precast(f"l1f_pj{e}", prechunk(np.asarray(I["L1_ffn_expert_proj"][e], np.float32)))
        w["pj"].append(a)
    cols = {}
    cols["ln1w"] = mk.load_cols("l1f_ln1w", colsplit(np.asarray(I["L1_ffn_ln1_w"], np.float32)))
    cols["ln1b"] = mk.load_cols("l1f_ln1b", colsplit(np.asarray(I["L1_ffn_ln1_b"], np.float32)))
    cols["ln2w"] = mk.load_cols("l1f_ln2w", colsplit(np.asarray(I["L1_ffn_ln2_w"], np.float32)))
    cols["ln2b"] = mk.load_cols("l1f_ln2b", colsplit(np.asarray(I["L1_ffn_ln2_b"], np.float32)))
    cw = np.asarray(I["L1_ffn_conv_w"], np.float32)
    for i in range(3):
        cols[f"cw{i}"] = mk.load_cols(f"l1f_cw{i}", colsplit(np.ascontiguousarray(cw[:, 0, i])))
    mq, mt = causal_masks()
    for nm, a in (("maskq", mq), ("maskt", mt)):
        ap = mk.din("c_" + nm, list(a.shape), F32)
        mk.inputs["c_" + nm] = a
        w[nm] = ap
    return w, cols


def emit_l1ffn(mk, xsrc, xdst, w1f, c1f, wa, Qf, Kf, Vt, Of, Tf, Zf, tag):
    cfg = {"mode": "l1f", "kv": 16, "wqkv": w1f["attn_w"],
           "tabs": (wa["cq"], wa["sq"], wa["ck"], wa["sk"]),
           "ln1w": c1f["ln1w"], "ln1b": c1f["ln1b"],
           "cw0": c1f["cw0"], "cw1": c1f["cw1"], "cw2": c1f["cw2"], "tdst": Tf}
    mk.emit_qkv(xsrc, Qf, Kf, Vt, cfg, tag + "q")
    mk.emit_attn_core(Qf, Kf, Vt, Of, causal=True, KV=16, tag=tag + "c",
                      maskq=w1f["maskq"], maskt=w1f["maskt"])
    mk.emit_proj_resid([(Of, 8)], w1f["proj"], Tf, Zf, tag + "p")
    mk.emit_moe(Zf, xsrc, xdst, {"router": w1f["router"], "wv": w1f["wv"], "pj": w1f["pj"]},
                c1f, tag + "m")


def build_l1attn_weights(mk, I):
    w = {}
    w["wq"], _ = mk.precast("l1a_wq", prechunk(np.asarray(I["L1_attn_Wq"], np.float32)))
    w["wk"], _ = mk.precast("l1a_wk", prechunk(np.asarray(I["L1_attn_Wk"], np.float32)))
    wm1 = np.asarray(I["L1_attn_Wm1"], np.float32)
    w["wm1a"], _ = mk.precast("l1a_m1a", prechunk(np.ascontiguousarray(wm1[0:1024])))
    w["wm1b"], _ = mk.precast("l1a_m1b", prechunk(np.ascontiguousarray(wm1[1024:2048])))
    w["wm2"], _ = mk.precast("l1a_m2", prechunk(np.asarray(I["L1_attn_Wm2"], np.float32)))
    w["wo"], _ = mk.precast("l1a_wo", prechunk(np.asarray(I["L1_attn_Wo"], np.float32)))
    w["wv_pl"], _ = mk.precast("l1a_wv", np.asarray(I["L1_attn_Wv"], np.float32))
    bv = np.ascontiguousarray(np.asarray(I["L1_attn_bv"], np.float32).reshape(1, 1024))
    w["bv_row"] = mk.din("l1a_bv", [1, 1024], F32)
    mk.inputs["l1a_bv"] = bv
    cols = {}
    c0, _ = mk.load_cols("l1a_bq32", colsplit(np.asarray(I["L1_attn_bq"], np.float32)))
    mk.S.barrier()
    mk.S.op("V", lambda v: v.tensor_scalar_mul(
        mk.colsb[:, c0:c0 + 8], mk.colsb[:, c0:c0 + 8], 1.0 / 32))
    cols["bq32"] = (c0, None)
    cols["bk"] = mk.load_cols("l1a_bk", colsplit(np.asarray(I["L1_attn_bk"], np.float32)))
    cols["bm1"] = mk.load_cols("l1a_bm1", colsplit(np.asarray(I["L1_attn_bm1"], np.float32)))
    cols["bm2"] = mk.load_cols("l1a_bm2", colsplit(np.asarray(I["L1_attn_bm2"], np.float32)))
    cols["bo"] = mk.load_cols("l1a_bo", colsplit(np.asarray(I["L1_attn_bo"], np.float32)))
    mk.fconst(1.0 / 32)
    return w, cols


def build_all(inputs, seq=4096, tb=512, debug_outs=(), upto=99):
    mk, x_in, out, xA, xB = build_program(seq=seq, tb=tb, debug_outs=debug_outs)
    wbf, scol = build_l0ffn_weights(mk, inputs)
    wa = build_l0attn_weights(mk, inputs)
    w1f, c1f = (None, None)
    if upto >= 4:
        w1f, c1f = build_l1ffn_weights(mk, inputs)
    ntile = seq // 128
    Qf = mk.dscratch("Qf", [D, seq], BF16, debug=True)
    Kf = mk.dscratch("Kf", [D, seq], BF16, debug=True)
    Vt = mk.dscratch("Vt", [16 * ntile * 128, 65], BF16, debug=True)
    Of = mk.dscratch("Of", [D, seq], BF16, debug=True)
    Tf = mk.dscratch("Tf", [D, seq], F32, debug=True)
    Zf = mk.dscratch("Zf", [D, seq], F32, debug=True)
    S = mk.S
    S.barrier()
    mk.emit_prologue(x_in, xA)
    mk.emit_ffn0(xA, xB, wbf, scol, "f0a")
    final = xB
    if upto >= 2:
        cfg = {"mode": "l0", "kv": 4, "wq": wa["wq"], "wk": wa["wk"], "wv": wa["wv"],
               "tabs": (wa["cq"], wa["sq"], wa["ck"], wa["sk"])}
        mk.emit_qkv(xB, Qf, Kf, Vt, cfg, "qk0")
        mk.emit_attn_core(Qf, Kf, Vt, Of, causal=False, KV=4, tag="ac0")
        mk.emit_proj_resid([(Of, 8)], wa["wo"], xB, xA, "pr0")
        final = xA
    if upto >= 3:
        mk.emit_ffn0(xA, xB, wbf, scol, "f0b")
        final = xB
    if upto >= 4:
        emit_l1ffn(mk, xB, xA, w1f, c1f, wa, Qf, Kf, Vt, Of, Tf, Zf, "l1a")
        final = xA
    if upto >= 5:
        w1a, c1a = build_l1attn_weights(mk, inputs)
        Af = mk.dscratch("Af", [D, seq], BF16, debug=True)
        Bf = mk.dscratch("Bf", [D, seq], BF16, debug=True)
        Vtok = mk.dscratch("Vtok", [seq, D], BF16, debug=True)
        Gf = mk.dscratch("Gf", [D, seq], BF16, debug=True)
        Lf = mk.dscratch("Lf", [D, seq], BF16, debug=True)
        S.barrier()
        mk.emit_l1a_proj(xA, Qf, Kf, Af, Bf, Vtok, w1a, c1a, "laP")
        mk.emit_l1a_core(Qf, Kf, Vtok, Gf, {"maskq": w1f["maskq"]}, "laC")
        mk.emit_l1a_local(Af, Bf, Lf, w1a, c1a, "laL")
        mk.emit_proj_resid([(Lf, 8), (Gf, 8)], w1a["wo"], xA, xB, "laF",
                           bias_col=c1a["bo"])
        final = xB
    if upto >= 6:
        emit_l1ffn(mk, xB, xA, w1f, c1f, wa, Qf, Kf, Vt, Of, Tf, Zf, "l1b")
        final = xA
    mk.emit_epilogue(final, out)
    S.finish()
    return mk


def kernel(**inputs):
    seq = inputs["L0_attn_x"].shape[1]
    mk = build_all(inputs, seq=seq, tb=512)
    x = np.ascontiguousarray(np.asarray(inputs["L0_attn_x"], np.float32))
    in_maps = []
    for c in range(8):
        m = dict(mk.inputs)
        m["x"] = np.ascontiguousarray(x[c])
        in_maps.append(m)
    res = run_bass_kernel_spmd(mk.nc, in_maps, core_ids=list(range(8)))
    return np.stack([r["out"] for r in res.results], axis=0)


# version 13
# speedup vs baseline: 1.1667x; 1.1667x over previous
import contextlib
import numpy as np
import ml_dtypes
import concourse.bass as bass
import concourse.mybir as mybir
from concourse.bass_utils import run_bass_kernel_spmd

F32 = mybir.dt.float32
BF16 = mybir.dt.bfloat16
AF = mybir.ActivationFunctionType
ALU = mybir.AluOpType
AX = mybir.AxisListType
NBF = ml_dtypes.bfloat16

D = 1024
DF = 4096
EPS = 1e-5


class Sched:
    ENGS = ("Y", "G", "T", "S", "V")

    def __init__(self, nc, es):
        self.nc = nc
        self.esem = {}
        self.ecnt = {e: 0 for e in self.ENGS}
        self.streams = {}
        self.ops = []
        self.hw = {e: {} for e in self.ENGS}
        self.es = es
        for e in self.ENGS:
            self.esem[e] = es.enter_context(nc.semaphore(f"p{e}"))
        self.phase_deps = ()

    def stream(self, name, cls):
        if not hasattr(self, "_spool"):
            self._spool = {"H": [], "G": []}
            self._sems = []
            self._scnt = []
        if name in self.streams:
            idx = self.streams[name]
            assert self._sems[idx][1] == cls, f"stream {name} class mismatch"
            return idx
        if self._spool[cls]:
            idx = self._spool[cls].pop()
        else:
            idx = len(self._sems)
            sem = self.es.enter_context(self.nc.semaphore(f"dma{idx}"))
            self._sems.append((sem, cls))
            self._scnt.append(0)
        self.streams[name] = idx
        return idx

    def op(self, e, fn, deps=(), name=""):
        self.ecnt[e] += 1
        tok = (e, self.ecnt[e])
        self.ops.append((e, fn, tuple(deps) + tuple(self.phase_deps), tok, name))
        return tok

    def dma(self, e, stream, out, in_, deps=(), name="", **kw):
        cls = "G" if e == "G" else "H"
        idx = self.stream(stream, cls)
        self._scnt[idx] += 16
        tok = (("d", idx), self._scnt[idx])
        def fn(eng, out=out, in_=in_, kw=kw):
            return eng.dma_start(out=out, in_=in_, **kw)
        self.ops.append((e, fn, tuple(deps) + tuple(self.phase_deps), tok,
                         name or f"dma:{stream}"))
        return tok

    def tok_now(self):
        toks = [(e, self.ecnt[e]) for e in self.ENGS if self.ecnt[e] > 0]
        if hasattr(self, "_scnt"):
            toks += [(("d", i), c) for i, c in enumerate(self._scnt) if c > 0]
        return tuple(toks)

    def _sem_of(self, key):
        if key in self.esem:
            return self.esem[key]
        return self._sems[key[1]][0]

    def flush(self):
        nc = self.nc
        ops = self.ops
        self.ops = []
        if not ops:
            return
        with nc.Block() as block:
            meths = {"Y": block.sync, "G": block.gpsimd, "T": block.tensor,
                     "S": block.scalar, "V": block.vector}
            for ekey in self.ENGS:
                eops = [o for o in ops if o[0] == ekey]
                if not eops:
                    continue
                def body(eng, eops=eops, ekey=ekey):
                    hw = self.hw[ekey]
                    for (_, fn, deps, tok, name) in eops:
                        for (dk, dv) in deps:
                            if dk == ekey and tok[0] == ekey and dv >= tok[1]:
                                raise RuntimeError(f"self-dep future {name}")
                            if hw.get(dk, 0) < dv:
                                eng.wait_ge(self._sem_of(dk), dv)
                                hw[dk] = dv
                        ins = fn(eng)
                        if tok[0] in self.esem:
                            ins.then_inc(self.esem[ekey], 1)
                        else:
                            ins.then_inc(self._sem_of(tok[0]), 16)
                meths[ekey](body)

    def barrier(self):
        self.flush()
        self.phase_deps = self.tok_now()
        if hasattr(self, "_spool"):
            for name, idx in self.streams.items():
                self._spool[self._sems[idx][1]].append(idx)
            self.streams = {}

    def finish(self, extra_deps=()):
        self.flush()
        toks = self.tok_now() + tuple(extra_deps)
        nc = self.nc
        with nc.Block() as block:
            def body(eng):
                for (dk, dv) in toks:
                    eng.wait_ge(self._sem_of(dk), dv)
            block.sync(body)



def prechunk(w_lhsT):
    d_in, d_out = w_lhsT.shape
    KC, OC = d_in // 128, d_out // 128
    w = w_lhsT.reshape(KC, 128, OC, 128).transpose(2, 1, 0, 3).reshape(OC * 128, KC * 128)
    return np.ascontiguousarray(w)

def colsplit(vec):
    return np.ascontiguousarray(vec.reshape(-1, 128))



class MK:
    def __init__(self, seq=4096, tb=512):
        self.seq = seq
        self.tb = tb
        self.nb = seq // tb
        nc = bass.Bass("TRN2", target_bir_lowering=False)
        self.nc = nc
        self.es = contextlib.ExitStack()
        self.S = Sched(nc, self.es)
        self.inputs = {}
        self.dram = {}
        self.debug_outs = []

    def din(self, name, shape, dtype=F32):
        ap = self.nc.dram_tensor(name, list(shape), dtype, kind="ExternalInput").ap()
        self.dram[name] = ap
        return ap

    def dscratch(self, name, shape, dtype=F32, debug=False):
        kind = "ExternalOutput" if (debug and name in self.debug_outs) else "Internal"
        ap = self.nc.dram_tensor(name, list(shape), dtype, kind=kind).ap()
        self.dram[name] = ap
        return ap

    def dout(self, name, shape, dtype=F32):
        ap = self.nc.dram_tensor(name, list(shape), dtype, kind="ExternalOutput").ap()
        self.dram[name] = ap
        return ap

    def emit_consts(self):
        nc, S = self.nc, self.S
        es = self.es
        self.ones_col_bf = es.enter_context(nc.sbuf_tensor("ones_col_bf", [128, 1], BF16))
        self.ones_row_f = es.enter_context(nc.sbuf_tensor("ones_row_f", [1, 128], F32))
        self.ones_row_bf = es.enter_context(nc.sbuf_tensor("ones_row_bf", [1, 128], BF16))
        self.idn_bf = es.enter_context(nc.sbuf_tensor("idn_bf", [128, 128], BF16))
        self.idn_f = es.enter_context(nc.sbuf_tensor("idn_f", [128, 128], F32))
        idn_host = self.din("c_idn", [128, 128], F32)
        S.op("V", lambda v: v.memset(self.ones_col_bf[:], 1.0))
        S.op("V", lambda v: v.memset(self.ones_row_f[:], 1.0))
        S.op("V", lambda v: v.memset(self.ones_row_bf[:], 1.0))
        t1 = S.dma("Y", "consts", self.idn_f[:], idn_host[:])
        t2 = S.dma("G", "gconsts", self.idn_bf[:], idn_host[:])
        self.inputs["c_idn"] = np.eye(128, dtype=np.float32)
        self.colsb = es.enter_context(nc.sbuf_tensor("colsb", [128, 256], F32))
        self.ncols = 0
        self.fcsb = es.enter_context(nc.sbuf_tensor("fcsb", [128, 16], F32))
        self.nfc = 0
        self.fcmap = {}
        for v in (EPS,):
            self.fconst(v)

    def fconst(self, val):
        if val in self.fcmap:
            return self.fcmap[val]
        i = self.nfc
        assert i < 16
        self.nfc += 1
        ap = self.fcsb[:, i:i + 1]
        self.S.op("V", lambda v, ap=ap, val=val: v.memset(ap, float(val)))
        self.nc.const_aps.register(F32, float(val), ap)
        self.fcmap[val] = ap
        return ap

    def load_cols(self, name, host_oc128):
        OC = host_oc128.shape[0]
        assert host_oc128.shape == (OC, 128)
        iname = "c_" + name
        ap = self.din(iname, [OC, 128], F32)
        self.inputs[iname] = np.ascontiguousarray(host_oc128.astype(np.float32))
        c0 = self.ncols
        assert c0 + OC <= 256
        self.ncols += OC
        for o in range(OC):
            self.S.dma("Y", "consts", self.colsb[:, c0 + o:c0 + o + 1], ap[o:o + 1, :])
        return c0, None

    def col(self, c0, o):
        return self.colsb[:, c0 + o:c0 + o + 1]

    def precast(self, name, host_arr, chunk_elems=4 << 20):
        arr = np.ascontiguousarray(host_arr.astype(np.float32))
        shp = list(arr.shape)
        ap_in = self.din(name, shp, F32)
        self.inputs[name] = arr
        ap_bf = self.dscratch("b_" + name, shp, BF16, debug=True)
        n = int(np.prod(shp))
        flat_in = ap_in.rearrange("a b -> (a b)") if len(shp) == 2 else ap_in
        flat_out = ap_bf.rearrange("a b -> (a b)") if len(shp) == 2 else ap_bf
        toks = []
        for i in range(0, n, chunk_elems):
            j = min(n, i + chunk_elems)
            toks.append(self.S.dma("G", "cast_" + name, flat_out[i:j], flat_in[i:j]))
        return ap_bf, toks

    def emit_prologue(self, x_in, xA):
        S = self.S
        nc = self.nc
        seq = self.seq
        with contextlib.ExitStack() as ph:
            xt = [ph.enter_context(nc.sbuf_tensor(f"pro_xt{i}", [128, D], F32)) for i in range(2)]
            ot = [ph.enter_context(nc.sbuf_tensor(f"pro_ot{i}", [128, D], F32)) for i in range(2)]
            pst = [ph.enter_context(nc.psum_tensor(f"pro_ps{i}", [128, 512], F32)) for i in range(2)]
            last_evac = [None, None]
            last_store = [None, None]
            for i in range(seq // 128):
                sl = i % 2
                deps = [last_evac[sl]] if last_evac[sl] else []
                tl = S.dma("Y", f"pro_x{sl}", xt[sl][:], x_in[i * 128:(i + 1) * 128, :], deps=deps)
                th = []
                ev0 = None
                for h in range(2):
                    def tr(t, sl=sl, h=h):
                        ins = None
                        for c in range(4):
                            cc = h * 4 + c
                            ins = t.transpose(pst[sl][:, c * 128:(c + 1) * 128],
                                              xt[sl][:, cc * 128:(cc + 1) * 128], self.idn_f[:])
                        return ins
                    pdeps = [tl] + ([last_evac[sl]] if last_evac[sl] else [])
                    if h == 1:
                        pdeps.append(ev0)
                    ttok = S.op("T", tr, deps=pdeps, name=f"pro_tr{i}_{h}")
                    def evac(v, sl=sl, h=h):
                        return v.tensor_copy(ot[sl][:, h * 512:(h + 1) * 512], pst[sl][:])
                    sdeps = [ttok] + ([last_store[sl]] if last_store[sl] else [])
                    evtok = S.op("V", evac, deps=sdeps, name=f"pro_ev{i}_{h}")
                    if h == 0:
                        ev0 = evtok
                    th.append(evtok)
                last_evac[sl] = th[1]
                sttoks = []
                for c in range(8):
                    sttoks.append(S.dma("Y", f"pro_o{sl}",
                                        xA[c * 128:(c + 1) * 128, i * 128:(i + 1) * 128],
                                        ot[sl][:, c * 128:(c + 1) * 128], deps=[th[c // 4]]))
                last_store[sl] = sttoks[-1]
            S.barrier()

    def emit_epilogue(self, xN, out):
        S = self.S
        nc = self.nc
        seq = self.seq
        with contextlib.ExitStack() as ph:
            it = [ph.enter_context(nc.sbuf_tensor(f"epi_it{i}", [128, D], F32)) for i in range(2)]
            ot = [ph.enter_context(nc.sbuf_tensor(f"epi_ot{i}", [128, D], F32)) for i in range(2)]
            pst = [ph.enter_context(nc.psum_tensor(f"epi_ps{i}", [128, 512], F32)) for i in range(2)]
            last_evac = [None, None]
            last_store = [None, None]
            for i in range(seq // 128):
                sl = i % 2
                deps = [last_evac[sl]] if last_evac[sl] else []
                ltoks = []
                for c in range(8):
                    ltoks.append(S.dma("Y", f"epi_x{sl}", it[sl][:, c * 128:(c + 1) * 128],
                                       xN[c * 128:(c + 1) * 128, i * 128:(i + 1) * 128], deps=deps))
                tl = ltoks[-1]
                ev0 = None
                th = []
                for h in range(2):
                    def tr(t, sl=sl, h=h):
                        ins = None
                        for c in range(4):
                            cc = h * 4 + c
                            ins = t.transpose(pst[sl][:, c * 128:(c + 1) * 128],
                                              it[sl][:, cc * 128:(cc + 1) * 128], self.idn_f[:])
                        return ins
                    pdeps = [tl] + ([last_evac[sl]] if last_evac[sl] else [])
                    if h == 1:
                        pdeps.append(ev0)
                    ttok = S.op("T", tr, deps=pdeps, name=f"epi_tr{i}_{h}")
                    def evac(v, sl=sl, h=h):
                        return v.tensor_copy(ot[sl][:, (h * 4) * 128:(h * 4 + 4) * 128], pst[sl][:])
                    sdeps = [ttok] + ([last_store[sl]] if last_store[sl] else [])
                    evtok = S.op("V", evac, deps=sdeps, name=f"epi_ev{i}_{h}")
                    if h == 0:
                        ev0 = evtok
                    th.append(evtok)
                last_evac[sl] = th[1]
                last_store[sl] = S.dma("Y", f"epi_o{sl}", out[i * 128:(i + 1) * 128, :],
                                       ot[sl][:], deps=th)
            S.barrier()

    def emit_ffn0(self, xsrc, xdst, wbf, scol, tag):
        S, nc = self.S, self.nc
        TB = self.tb
        with contextlib.ExitStack() as ph:
            xsb = ph.enter_context(nc.sbuf_tensor(f"{tag}_x", [128, 8 * TB], F32))
            xa = [ph.enter_context(nc.sbuf_tensor(f"{tag}_xa{i}", [128, TB], F32)) for i in range(2)]
            x16 = ph.enter_context(nc.sbuf_tensor(f"{tag}_x16", [128, 8 * TB], BF16))
            sq16 = ph.enter_context(nc.sbuf_tensor(f"{tag}_sq", [128, 8 * TB], BF16))
            t16 = ph.enter_context(nc.sbuf_tensor(f"{tag}_t16", [128, 8 * TB], BF16))
            u16 = ph.enter_context(nc.sbuf_tensor(f"{tag}_u16", [128, 32 * TB], BF16))
            v16 = ph.enter_context(nc.sbuf_tensor(f"{tag}_v16", [128, 32 * TB], BF16))
            sA = [ph.enter_context(nc.sbuf_tensor(f"{tag}_sA{i}", [128, TB], BF16)) for i in range(2)]
            stat = ph.enter_context(nc.sbuf_tensor(f"{tag}_st", [1, 6 * TB], F32))
            wsl = {}
            for wname, wid in (("w1", 8), ("w2", 8), ("h1", 32), ("h2", 32), ("w3", 32)):
                wsl[wname] = [ph.enter_context(
                    nc.sbuf_tensor(f"{tag}_{wname}s{i}", [128, wid * 128], BF16)) for i in range(2)]
            ps_sum = ph.enter_context(nc.psum_tensor(f"{tag}_psum", [1, TB], F32))
            ps_sq = ph.enter_context(nc.psum_tensor(f"{tag}_psq", [1, TB], F32))
            ps_A = ph.enter_context(nc.psum_tensor(f"{tag}_psA", [128, TB], F32))
            ps_C = ph.enter_context(nc.psum_tensor(f"{tag}_psC", [128, TB], F32))
            ps = [ph.enter_context(nc.psum_tensor(f"{tag}_ps{i}", [128, TB], F32)) for i in range(2)]

            wtoks = {k: [None, None] for k in wsl}
            ltoks = {k: [None, None] for k in wsl}
            prev_x_readers = None
            prev_u16_mm = None
            prev_statreaders = []
            prev_norm = None
            prev_psAC = None
            prev_t16_mm = None
            prev_xa = [None, None]

            def loadw(wname, o, OC):
                sl = o % 2
                dep = [wtoks[wname][sl]] if wtoks[wname][sl] else []
                ltoks[wname][sl] = S.dma("Y", f"{tag}_{wname}{sl}", wsl[wname][sl][:],
                                         wbf[wname][o * 128:(o + 1) * 128, :], deps=dep)

            for b in range(self.nb):
                tsl = slice(b * TB, (b + 1) * TB)
                xdep = list(prev_x_readers) if prev_x_readers else []
                xts = [S.dma("Y", f"{tag}_xin", xsb[:, c * TB:(c + 1) * TB],
                             xsrc[c * 128:(c + 1) * 128, tsl], deps=xdep) for c in range(8)]
                xtok = xts[-1]
                cast_toks = []
                sq_toks = []
                for c in range(8):
                    dep = [xtok] + ([prev_norm[c]] if prev_norm else [])
                    cast_toks.append(S.op("V", lambda v, c=c: v.tensor_copy(
                        x16[:, c * TB:(c + 1) * TB], xsb[:, c * TB:(c + 1) * TB]), deps=dep))
                    sq_toks.append(S.op("S", lambda s, c=c: s.activation(
                        out=sq16[:, c * TB:(c + 1) * TB], in_=xsb[:, c * TB:(c + 1) * TB],
                        func=AF.Square), deps=dep))
                def mmsum(t):
                    ins = None
                    for c in range(8):
                        ins = t.matmul(ps_sum[:], self.ones_col_bf[:], x16[:, c * TB:(c + 1) * TB],
                                       start=(c == 0), stop=(c == 7))
                    return ins
                def mmsq(t):
                    ins = None
                    for c in range(8):
                        ins = t.matmul(ps_sq[:], self.ones_col_bf[:], sq16[:, c * TB:(c + 1) * TB],
                                       start=(c == 0), stop=(c == 7))
                    return ins
                sdep = cast_toks + sq_toks
                tsum = S.op("T", mmsum, deps=sdep, name=f"{tag}_lnsum{b}")
                tsq = S.op("T", mmsq, deps=sdep, name=f"{tag}_lnsq{b}")
                mu = stat[:, 0:TB]; ex2 = stat[:, TB:2 * TB]; var = stat[:, 2 * TB:3 * TB]
                std = stat[:, 3 * TB:4 * TB]; rinv = stat[:, 4 * TB:5 * TB]; nmr = stat[:, 5 * TB:6 * TB]
                v1 = S.op("V", lambda v: v.tensor_scalar_mul(mu, ps_sum[:], 1.0 / D),
                          deps=[tsum] + prev_statreaders)
                v2 = S.op("V", lambda v: v.tensor_scalar_mul(ex2, ps_sq[:], 1.0 / D), deps=[tsq])
                v3 = S.op("V", lambda v: v.scalar_tensor_tensor(
                    var, mu, 1.0, mu, op0=ALU.mult, op1=ALU.mult), deps=[v1])
                v4 = S.op("V", lambda v: v.tensor_sub(var, ex2, var), deps=[v2, v3])
                s5 = S.op("S", lambda s: s.activation(out=std, in_=var, func=AF.Sqrt, bias=EPS), deps=[v4])
                v6 = S.op("V", lambda v: v.reciprocal(rinv, std), deps=[s5])
                v7 = S.op("V", lambda v: v.scalar_tensor_tensor(
                    nmr, mu, -1.0, rinv, op0=ALU.mult, op1=ALU.mult), deps=[v6, v1])
                pdep = [prev_psAC] if prev_psAC else []
                tA = S.op("T", lambda t: t.matmul(ps_A[:], self.ones_row_f[:], rinv,
                                                  start=True, stop=True), deps=[v6] + pdep)
                tC = S.op("T", lambda t: t.matmul(ps_C[:], self.ones_row_f[:], nmr,
                                                  start=True, stop=True), deps=[v7] + pdep)
                norm_toks = []
                for c in range(8):
                    sl2 = c % 2
                    d1 = [tA, xtok] + ([prev_xa[sl2]] if prev_xa[sl2] else [])
                    n1 = S.op("V", lambda v, c=c: v.tensor_mul(
                        xa[c % 2][:], xsb[:, c * TB:(c + 1) * TB], ps_A[:]), deps=d1)
                    d2 = [tC, n1]
                    if prev_t16_mm is not None:
                        d2.append(prev_t16_mm)
                    n2 = S.op("V", lambda v, c=c: v.tensor_tensor(
                        t16[:, c * TB:(c + 1) * TB], xa[c % 2][:], ps_C[:],
                        op=ALU.add), deps=d2)
                    prev_xa[sl2] = n2
                    norm_toks.append(n2)
                prev_statreaders = [tA, tC]
                prev_norm = norm_toks
                prev_psAC = norm_toks[-1]
                xrtok = xtok

                evB = []
                mmB = []
                for o in range(32):
                    if b == 0 and o < 2:
                        loadw("w1", o, 32); loadw("w2", o, 32)
                    psl = o % 2
                    pdeps = [ltoks["w1"][o % 2]] + norm_toks
                    if len(evB) >= 2:
                        pdeps.append(evB[-2][1])
                    def mm1(t, o=o):
                        ins = None
                        sl = wsl["w1"][o % 2]
                        for k in range(8):
                            ins = t.matmul(ps[o % 2][:], sl[:, k * 128:(k + 1) * 128],
                                           t16[:, k * TB:(k + 1) * TB], start=(k == 0), stop=(k == 7))
                        return ins
                    ta = S.op("T", mm1, deps=pdeps, name=f"{tag}_B1_{b}_{o}")
                    sa = S.op("S", lambda s, o=o: s.activation(
                        out=sA[o % 2][:], in_=ps[o % 2][:], func=AF.Silu,
                        scale=self.col(scol["s1"][0], o)),
                        deps=[ta] + ([evB[-2][1]] if len(evB) >= 2 else []), name=f"{tag}_Bs_{b}_{o}")
                    def mm2(t, o=o):
                        ins = None
                        sl = wsl["w2"][o % 2]
                        for k in range(8):
                            ins = t.matmul(ps[o % 2][:], sl[:, k * 128:(k + 1) * 128],
                                           t16[:, k * TB:(k + 1) * TB], start=(k == 0), stop=(k == 7))
                        return ins
                    tb_ = S.op("T", mm2, deps=[sa, ltoks["w2"][o % 2]], name=f"{tag}_B2_{b}_{o}")
                    wtoks["w1"][o % 2] = ta
                    wtoks["w2"][o % 2] = tb_
                    if o + 2 < 32:
                        loadw("w1", o + 2, 32); loadw("w2", o + 2, 32)
                    elif b + 1 < self.nb and o + 2 < 34:
                        loadw("w1", o + 2 - 32, 32); loadw("w2", o + 2 - 32, 32)
                    udep = [tb_]
                    if b > 0 and o < 2:
                        udep.append(prev_u16_mm)
                    uev = S.op("V", lambda v, o=o: v.scalar_tensor_tensor(
                        u16[:, o * TB:(o + 1) * TB], ps[o % 2][:], self.col(scol["s2"][0], o),
                        sA[o % 2][:], op0=ALU.mult, op1=ALU.mult), deps=udep,
                        name=f"{tag}_Bu_{b}_{o}")
                    evB.append((ta, uev, sa))
                    mmB.append(tb_)
                evC = []
                lastB = [evB[-1][1], evB[-2][1]] + mmB[-2:]
                for o in range(32):
                    if o < 2:
                        sl = o % 2
                        dep = [wtoks["h1"][sl]] if wtoks["h1"][sl] else []
                        ltoks["h1"][sl] = S.dma("Y", f"{tag}_h1{sl}", wsl["h1"][sl][:],
                                                wbf["h1"][o * 128:(o + 1) * 128, :],
                                                deps=dep)
                    psl = o % 2
                    pdeps = [ltoks["h1"][psl]] + [x[1] for x in evB]
                    if len(evC) >= 2:
                        pdeps.append(evC[-2])
                    def mmc(t, o=o):
                        ins = None
                        sl = wsl["h1"][o % 2]
                        for k in range(32):
                            ins = t.matmul(ps[o % 2][:], sl[:, k * 128:(k + 1) * 128],
                                           u16[:, k * TB:(k + 1) * TB], start=(k == 0), stop=(k == 31))
                        return ins
                    tc = S.op("T", mmc, deps=pdeps, name=f"{tag}_C_{b}_{o}")
                    wtoks["h1"][psl] = tc
                    if o + 2 < 32:
                        sl = o % 2
                        ltoks["h1"][sl] = S.dma("Y", f"{tag}_h1{sl}", wsl["h1"][sl][:],
                                                wbf["h1"][(o + 2) * 128:(o + 3) * 128, :],
                                                deps=[tc])
                    ev = S.op("S", lambda s, o=o: s.activation(
                        out=v16[:, o * TB:(o + 1) * TB], in_=ps[o % 2][:], func=AF.Gelu,
                        scale=self.col(scol["h1s"][0], o)), deps=[tc], name=f"{tag}_Cg_{b}_{o}")
                    evC.append(ev)
                evD = []
                for o in range(32):
                    if o < 2:
                        sl = o % 2
                        dep = [wtoks["h2"][sl]] if wtoks["h2"][sl] else []
                        ltoks["h2"][sl] = S.dma("Y", f"{tag}_h2{sl}", wsl["h2"][sl][:],
                                                wbf["h2"][o * 128:(o + 1) * 128, :], deps=dep)
                    psl = o % 2
                    pdeps = [ltoks["h2"][psl]] + evC
                    pdeps += [x[1] for x in evB]
                    pdeps.append(evC[-1])
                    if len(evD) >= 2:
                        pdeps.append(evD[-2])
                    def mmd(t, o=o):
                        ins = None
                        sl = wsl["h2"][o % 2]
                        for k in range(32):
                            ins = t.matmul(ps[o % 2][:], sl[:, k * 128:(k + 1) * 128],
                                           v16[:, k * TB:(k + 1) * TB], start=(k == 0), stop=(k == 31))
                        return ins
                    td = S.op("T", mmd, deps=pdeps, name=f"{tag}_D_{b}_{o}")
                    wtoks["h2"][psl] = td
                    if o + 2 < 32:
                        sl = o % 2
                        ltoks["h2"][sl] = S.dma("Y", f"{tag}_h2{sl}", wsl["h2"][sl][:],
                                                wbf["h2"][(o + 2) * 128:(o + 3) * 128, :], deps=[td])
                    ev = S.op("S", lambda s, o=o: s.activation(
                        out=u16[:, o * TB:(o + 1) * TB], in_=ps[o % 2][:], func=AF.Gelu,
                        scale=self.col(scol["h2s"][0], o)), deps=[td], name=f"{tag}_Dg_{b}_{o}")
                    evD.append(ev)
                evE = []
                stE = []
                for o in range(8):
                    if o < 2:
                        sl = o % 2
                        dep = [wtoks["w3"][sl]] if wtoks["w3"][sl] else []
                        ltoks["w3"][sl] = S.dma("Y", f"{tag}_w3{sl}", wsl["w3"][sl][:],
                                                wbf["w3"][o * 128:(o + 1) * 128, :], deps=dep)
                    psl = o % 2
                    pdeps = [ltoks["w3"][psl]] + evD
                    if len(evE) >= 2:
                        pdeps.append(evE[-2])
                    def mme(t, o=o):
                        ins = None
                        sl = wsl["w3"][o % 2]
                        for k in range(32):
                            ins = t.matmul(ps[o % 2][:], sl[:, k * 128:(k + 1) * 128],
                                           u16[:, k * TB:(k + 1) * TB], start=(k == 0), stop=(k == 31))
                        return ins
                    te = S.op("T", mme, deps=pdeps, name=f"{tag}_E_{b}_{o}")
                    wtoks["w3"][psl] = te
                    if o + 2 < 8:
                        sl = o % 2
                        ltoks["w3"][sl] = S.dma("Y", f"{tag}_w3{sl}", wsl["w3"][sl][:],
                                                wbf["w3"][(o + 2) * 128:(o + 3) * 128, :], deps=[te])
                    ev = S.op("V", lambda v, o=o: v.scalar_tensor_tensor(
                        xsb[:, o * TB:(o + 1) * TB], ps[o % 2][:],
                        self.col(scol["w3sh"][0], o), xsb[:, o * TB:(o + 1) * TB],
                        op0=ALU.mult, op1=ALU.add),
                        deps=[te, xrtok, norm_toks[o], sq_toks[o], cast_toks[o]],
                        name=f"{tag}_Ee_{b}_{o}")
                    evE.append(ev)
                    stE.append(S.dma("Y", f"{tag}_xout", xdst[o * 128:(o + 1) * 128, tsl],
                                     xsb[:, o * TB:(o + 1) * TB], deps=[ev]))
                prev_x_readers = evE + [stE[-1]]
                prev_t16_mm = mmB[-1]
                prev_u16_mm = te
            S.barrier()


    def emit_qkv(self, xsrc, Qf, Kf, Vt, cfg, tag):
        S, nc = self.S, self.nc
        TB = self.tb
        mode = cfg["mode"]
        KV = cfg["kv"]
        ntile = self.seq // 128
        with contextlib.ExitStack() as ph:
            xsb = ph.enter_context(nc.sbuf_tensor(f"{tag}_x", [128, 8 * TB], F32))
            xa = [ph.enter_context(nc.sbuf_tensor(f"{tag}_xa{i}", [128, TB], F32)) for i in range(2)]
            x16 = ph.enter_context(nc.sbuf_tensor(f"{tag}_x16", [128, 8 * TB], BF16))
            sq16 = ph.enter_context(nc.sbuf_tensor(f"{tag}_sq", [128, 8 * TB], BF16))
            act16 = ph.enter_context(nc.sbuf_tensor(f"{tag}_a16", [128, 8 * TB], BF16))
            stat = ph.enter_context(nc.sbuf_tensor(f"{tag}_st", [1, 8 * TB], F32))
            wq_w = {"l0": 1024, "l1f": 3072}[mode]
            wk_w = {"l0": 256, "l1f": 0}[mode]
            wtot = wq_w + 2 * wk_w if mode == "l0" else 3072
            wsb = ph.enter_context(nc.sbuf_tensor(f"{tag}_w", [128, 8 * wtot], BF16))
            tabs = cfg["tabs"]
            tsl = [ph.enter_context(nc.sbuf_tensor(f"{tag}_tab{i}", [128, 4 * 1024], BF16))
                   for i in range(2)]
            rA = ph.enter_context(nc.sbuf_tensor(f"{tag}_rA", [128, 512], F32))
            rB = ph.enter_context(nc.sbuf_tensor(f"{tag}_rB", [128, 512], F32))
            qrot = ph.enter_context(nc.sbuf_tensor(f"{tag}_qrot", [128, 1024], BF16))
            krot = ph.enter_context(nc.sbuf_tensor(f"{tag}_krot", [128, 1024], BF16))
            qf_asm = ph.enter_context(nc.sbuf_tensor(f"{tag}_qasm", [128, 8 * 512], BF16))
            kf_asm = ph.enter_context(nc.sbuf_tensor(f"{tag}_kasm", [128, (2 if mode == "l0" else 8) * 512], BF16))
            vasm = ph.enter_context(nc.sbuf_tensor(f"{tag}_vasm", [128, KV * 65], BF16))
            if mode == "l1f":
                tsb = ph.enter_context(nc.sbuf_tensor(f"{tag}_t", [128, 8 * TB], F32))
                hb = ph.enter_context(nc.sbuf_tensor(f"{tag}_hb", [128, 8 * (TB + 2)], F32))
                ctmp = ph.enter_context(nc.sbuf_tensor(f"{tag}_ctmp", [128, TB], F32))
            ps_sum = ph.enter_context(nc.psum_tensor(f"{tag}_psum", [1, TB], F32))
            ps_sq = ph.enter_context(nc.psum_tensor(f"{tag}_psq", [1, TB], F32))
            ps_A = ph.enter_context(nc.psum_tensor(f"{tag}_psA", [128, TB], F32))
            ps_C = ph.enter_context(nc.psum_tensor(f"{tag}_psC", [128, TB], F32))
            ps = [ph.enter_context(nc.psum_tensor(f"{tag}_ps{i}", [128, 512], F32)) for i in range(2)]
            ps_tr = ph.enter_context(nc.psum_tensor(f"{tag}_ptr", [128, 512], BF16))

            if mode == "l0":
                for kc in range(8):
                    S.dma("Y", f"{tag}_wld", wsb[:, kc * wtot:kc * wtot + 1024],
                          cfg["wq"][kc * 128:(kc + 1) * 128, :])
                    S.dma("Y", f"{tag}_wld", wsb[:, kc * wtot + 1024:kc * wtot + 1280],
                          cfg["wk"][kc * 128:(kc + 1) * 128, :])
                    S.dma("Y", f"{tag}_wld", wsb[:, kc * wtot + 1280:kc * wtot + 1536],
                          cfg["wv"][kc * 128:(kc + 1) * 128, :])
            else:
                for kc in range(8):
                    S.dma("Y", f"{tag}_wld", wsb[:, kc * wtot:(kc + 1) * wtot],
                          cfg["wqkv"][kc * 128:(kc + 1) * 128, :])
            wld_tok = S.tok_now()
            vset = S.op("V", lambda v: v.memset(
                vasm[:].rearrange("p (k j) -> p k j", k=KV, j=65)[:, :, 64:65], 1.0))
            if mode == "l1f":
                hz = S.op("V", lambda v: v.memset(
                    hb[:].rearrange("p (c j) -> p c j", c=8, j=TB + 2)[:, :, 0:2], 0.0))
            S.barrier()

            prev_x_readers = None
            prev_statreaders = []
            prev_psAC = None
            prev_act_mm = None
            prev_xa = [None, None]
            prev_sliver = None
            prev_tsb_store = None
            self.prev_tab = [None, None]
            self.ps_free = [None, None]
            self.rope_scr = None
            self.rot_rd = None
            self.vst_last = None
            self.tr_ev = None
            self.asm_st = []
            for b in range(self.nb):
                tslice = slice(b * TB, (b + 1) * TB)
                xdep = list(prev_x_readers) if prev_x_readers else []
                xts = [S.dma("Y", f"{tag}_xin", xsb[:, c * TB:(c + 1) * TB],
                             xsrc[c * 128:(c + 1) * 128, tslice], deps=xdep) for c in range(8)]
                xtok = xts[-1]
                cast_toks, sq_toks = [], []
                for c in range(8):
                    dep = [xtok]
                    cast_toks.append(S.op("V", lambda v, c=c: v.tensor_copy(
                        x16[:, c * TB:(c + 1) * TB], xsb[:, c * TB:(c + 1) * TB]), deps=dep))
                    sq_toks.append(S.op("S", lambda s, c=c: s.activation(
                        out=sq16[:, c * TB:(c + 1) * TB], in_=xsb[:, c * TB:(c + 1) * TB],
                        func=AF.Square), deps=dep))
                def mmsum(t):
                    for c in range(8):
                        ins = t.matmul(ps_sum[:], self.ones_col_bf[:], x16[:, c * TB:(c + 1) * TB],
                                       start=(c == 0), stop=(c == 7))
                    return ins
                def mmsq(t):
                    for c in range(8):
                        ins = t.matmul(ps_sq[:], self.ones_col_bf[:], sq16[:, c * TB:(c + 1) * TB],
                                       start=(c == 0), stop=(c == 7))
                    return ins
                sdep = cast_toks + sq_toks
                tsum = S.op("T", mmsum, deps=sdep)
                tsq = S.op("T", mmsq, deps=sdep)
                mu = stat[:, 0:TB]; ex2 = stat[:, TB:2 * TB]; var = stat[:, 2 * TB:3 * TB]
                std = stat[:, 3 * TB:4 * TB]; rinv = stat[:, 4 * TB:5 * TB]; nmr = stat[:, 5 * TB:6 * TB]
                var2 = stat[:, 6 * TB:7 * TB]; nmr2 = stat[:, 7 * TB:8 * TB]
                v1 = S.op("V", lambda v: v.tensor_scalar_mul(mu, ps_sum[:], 1.0 / D),
                          deps=[tsum] + prev_statreaders)
                v2 = S.op("V", lambda v: v.tensor_scalar_mul(ex2, ps_sq[:], 1.0 / D), deps=[tsq])
                v3 = S.op("V", lambda v: v.scalar_tensor_tensor(
                    var, mu, 1.0, mu, op0=ALU.mult, op1=ALU.mult), deps=[v1])
                v4 = S.op("V", lambda v: v.tensor_sub(var, ex2, var), deps=[v2, v3])
                s5 = S.op("S", lambda s: s.activation(out=std, in_=var, func=AF.Sqrt, bias=EPS),
                          deps=[v4])
                v6 = S.op("V", lambda v: v.reciprocal(rinv, std), deps=[s5])
                v7 = S.op("V", lambda v: v.scalar_tensor_tensor(
                    nmr, mu, -1.0, rinv, op0=ALU.mult, op1=ALU.mult), deps=[v6, v1])
                pdep = [prev_psAC] if prev_psAC else []
                tA = S.op("T", lambda t: t.matmul(ps_A[:], self.ones_row_f[:], rinv,
                                                  start=True, stop=True), deps=[v6] + pdep)
                tC = S.op("T", lambda t: t.matmul(ps_C[:], self.ones_row_f[:], nmr,
                                                  start=True, stop=True), deps=[v7] + pdep)
                norm_toks = []
                if mode == "l0":
                    for c in range(8):
                        sl2 = c % 2
                        d1 = [tA, xtok] + ([prev_xa[sl2]] if prev_xa[sl2] else [])
                        n1 = S.op("V", lambda v, c=c: v.tensor_mul(
                            xa[c % 2][:], xsb[:, c * TB:(c + 1) * TB], ps_A[:]), deps=d1)
                        d2 = [tC, n1] + ([prev_act_mm] if prev_act_mm else [])
                        n2 = S.op("V", lambda v, c=c: v.tensor_tensor(
                            act16[:, c * TB:(c + 1) * TB], xa[c % 2][:], ps_C[:],
                            op=ALU.add), deps=d2)
                        prev_xa[sl2] = n2
                        norm_toks.append(n2)
                    statr = [tA, tC]
                else:
                    t_toks = []
                    for c in range(8):
                        sl2 = c % 2
                        d1 = [tA, xtok] + ([prev_xa[sl2]] if prev_xa[sl2] else [])
                        n1 = S.op("V", lambda v, c=c: v.tensor_mul(
                            xa[c % 2][:], xsb[:, c * TB:(c + 1) * TB], ps_A[:]), deps=d1)
                        d2 = [tC, n1] + ([prev_tsb_store] if b > 0 else [])
                        n2 = S.op("V", lambda v, c=c: v.tensor_tensor(
                            tsb[:, c * TB:(c + 1) * TB], xa[c % 2][:], ps_C[:],
                            op=ALU.add), deps=d2)
                        prev_xa[sl2] = n2
                        t_toks.append(n2)
                    tst = [S.dma("Y", f"{tag}_tout", cfg["tdst"][c * 128:(c + 1) * 128, tslice],
                                 tsb[:, c * TB:(c + 1) * TB], deps=[t_toks[c]]) for c in range(8)]
                    prev_tsb_store = tst[-1]
                    e1 = S.op("V", lambda v: v.tensor_scalar(
                        var2, var, EPS, EPS, op0=ALU.add, op1=ALU.mult), deps=[v4, tA])
                    e2 = S.op("V", lambda v: v.tensor_add(var2, var, var2), deps=[e1])
                    s6 = S.op("S", lambda s: s.activation(out=std, in_=var2, func=AF.Sqrt),
                              deps=[e2, v6])
                    e3 = S.op("V", lambda v: v.reciprocal(rinv, std), deps=[s6, tA])
                    e4 = S.op("V", lambda v: v.scalar_tensor_tensor(
                        nmr2, mu, -1.0, rinv, op0=ALU.mult, op1=ALU.mult), deps=[e3])
                    ndep = t_toks + [e3]
                    tA2 = S.op("T", lambda t: t.matmul(ps_A[:], self.ones_row_f[:], rinv,
                                                       start=True, stop=True), deps=ndep)
                    tC2 = S.op("T", lambda t: t.matmul(ps_C[:], self.ones_row_f[:], nmr2,
                                                       start=True, stop=True), deps=[e4] + ndep)
                    hw_c0 = cfg["ln1w"][0]
                    hb_c0 = cfg["ln1b"][0]
                    conv_toks = []
                    for c in range(8):
                        sl2 = c % 2
                        hoff = c * (TB + 2)
                        d1 = [tA2, xtok, t_toks[c]] + ([prev_xa[sl2]] if prev_xa[sl2] else [])
                        n1 = S.op("V", lambda v, c=c: v.tensor_mul(
                            xa[c % 2][:], xsb[:, c * TB:(c + 1) * TB], ps_A[:]), deps=d1)
                        d2 = [tC2, n1] + ([prev_sliver[c]] if prev_sliver else [])
                        n2 = S.op("V", lambda v, c=c, hoff=hoff: v.tensor_tensor(
                            hb[:, hoff + 2:hoff + 2 + TB], xa[c % 2][:], ps_C[:],
                            op=ALU.add), deps=d2)
                        prev_xa[sl2] = n2
                        n3 = S.op("V", lambda v, c=c, hoff=hoff: v.tensor_scalar(
                            hb[:, hoff + 2:hoff + 2 + TB], hb[:, hoff + 2:hoff + 2 + TB],
                            self.col(hw_c0, c), self.col(hb_c0, c),
                            op0=ALU.mult, op1=ALU.add), deps=[n2])
                        norm_toks.append(n3)
                    statr = [tA, tC, tA2, tC2]
                    sliver_toks = []
                    conv_out = []
                    for c in range(8):
                        hoff = c * (TB + 2)
                        o1 = S.op("V", lambda v, c=c, hoff=hoff: v.tensor_scalar_mul(
                            ctmp[:], hb[:, hoff:hoff + TB], self.col(cfg["cw0"][0], c)),
                            deps=[norm_toks[c]] + ([conv_out[-1]] if conv_out else []))
                        o2 = S.op("V", lambda v, c=c, hoff=hoff: v.scalar_tensor_tensor(
                            ctmp[:], hb[:, hoff + 1:hoff + 1 + TB], self.col(cfg["cw1"][0], c),
                            ctmp[:], op0=ALU.mult, op1=ALU.add), deps=[o1])
                        o3dep = [o2] + ([prev_act_mm] if prev_act_mm else [])
                        o3 = S.op("V", lambda v, c=c, hoff=hoff: v.scalar_tensor_tensor(
                            act16[:, c * TB:(c + 1) * TB], hb[:, hoff + 2:hoff + 2 + TB],
                            self.col(cfg["cw2"][0], c), ctmp[:],
                            op0=ALU.mult, op1=ALU.add), deps=o3dep)
                        sv = S.op("V", lambda v, c=c, hoff=hoff: v.tensor_copy(
                            hb[:, hoff:hoff + 2], hb[:, hoff + TB:hoff + TB + 2]), deps=[o3])
                        sliver_toks.append(sv)
                        conv_out.append(o3)
                    norm_toks = conv_out
                    prev_sliver = sliver_toks
                prev_statreaders = statr
                prev_psAC = norm_toks[-1] if mode == "l0" else norm_toks[-1]
                for tt in range(TB // 128):
                    gtile = b * (TB // 128) + tt
                    sl = tt % 2
                    tdep = [self.prev_tab[sl]] if self.prev_tab[sl] else []
                    tabtok = None
                    for ti, tab in enumerate(tabs):
                        tabtok = S.dma("Y", f"{tag}_tab{sl}", tsl[sl][:, ti * 1024:(ti + 1) * 1024],
                                       tab[gtile * 128:(gtile + 1) * 128, :], deps=tdep)
                    tabtile = tsl[sl]
                    def actT(kc, tt=tt):
                        return act16[:, kc * TB + tt * 128: kc * TB + tt * 128 + 128]
                    if mode == "l0":
                        groups = [("q", 0, 512, 0), ("q", 512, 512, 1),
                                  ("k", 1024, 256, 0), ("v", 1280, 256, 0)]
                    else:
                        groups = [("q", 0, 512, 0), ("q", 512, 512, 1),
                                  ("k", 1024, 512, 0), ("k", 1536, 512, 1),
                                  ("v", 2048, 512, 0), ("v", 2560, 512, 1)]
                    rope_q = [None, None]
                    rope_k = [None, None]
                    vev_tok = None
                    for gi, (kind, ooff, width, half) in enumerate(groups):
                        psl = gi % 2
                        mdeps = list(norm_toks) + list(wld_tok)
                        if self.ps_free[psl] is not None:
                            mdeps.append(self.ps_free[psl])
                        def mmg(t, ooff=ooff, width=width, psl=psl, tt=tt):
                            for kc in range(8):
                                ins = t.matmul(ps[psl][:, 0:width],
                                               actT(kc, tt),
                                               wsb[:, kc * wtot + ooff:kc * wtot + ooff + width],
                                               start=(kc == 0), stop=(kc == 7))
                            return ins
                        gt = S.op("T", mmg, deps=mdeps, name=f"{tag}_qkv_{b}_{tt}_{kind}{half}")
                        if kind in ("q", "k"):
                            is_q = kind == "q"
                            ctab = (0 if is_q else 2) * 1024
                            stab = (1 if is_q else 3) * 1024
                            dst = qrot if is_q else krot
                            toff = half * 512
                            e1dep = [gt, tabtok] + ([self.rope_scr] if self.rope_scr else [])
                            e1 = S.op("V", lambda v, psl=psl, ctab=ctab, toff=toff, width=width, tb_=tabtile:
                                      v.tensor_mul(rA[:, 0:width], ps[psl][:, 0:width],
                                                   tb_[:, ctab + toff:ctab + toff + width]), deps=e1dep)
                            e2 = S.op("V", lambda v, psl=psl, stab=stab, toff=toff, width=width, tb_=tabtile:
                                      v.tensor_tensor(
                                rB[:, 0:width].rearrange("p (bk h j) -> p bk h j", h=2, j=32),
                                ps[psl][:, 0:width].rearrange("p (bk h j) -> p bk h j", h=2, j=32)[:, :, ::-1, :],
                                tb_[:, stab + toff:stab + toff + width].rearrange("p (bk h j) -> p bk h j", h=2, j=32),
                                op=ALU.mult), deps=[gt, tabtok, e1])
                            e3dep = [e2] + ([self.rot_rd] if self.rot_rd else [])
                            e3 = S.op("V", lambda v, dst=dst, toff=toff, width=width:
                                      v.tensor_add(dst[:, toff:toff + width],
                                                   rA[:, 0:width], rB[:, 0:width]), deps=e3dep)
                            self.rope_scr = e3
                            self.ps_free[psl] = e2
                            if is_q:
                                rope_q[half] = e3
                            else:
                                rope_k[half] = e3
                        else:
                            koff = half * 8 if mode == "l1f" else 0
                            nk = width // 64
                            evdep = [gt, vset] + ([self.vst_last] if self.vst_last else [])
                            ev = S.op("V", lambda v, psl=psl, koff=koff, nk=nk, width=width:
                                      v.tensor_copy(
                                vasm[:].rearrange("p (k j) -> p k j", k=KV, j=65)[:, koff:koff + nk, 0:64],
                                ps[psl][:, 0:width].rearrange("p (k j) -> p k j", k=nk, j=64)),
                                deps=evdep)
                            self.ps_free[psl] = ev
                            vev_tok = ev
                    vst = [S.dma("Y", f"{tag}_vst",
                                 Vt[(k * ntile + gtile) * 128:(k * ntile + gtile + 1) * 128, :],
                                 vasm[:, k * 65:(k + 1) * 65], deps=[vev_tok]) for k in range(KV)]
                    self.vst_last = vst[-1]
                    def emit_tr(srcrot, nchunk, asm, rdeps, tt=tt):
                        evts = []
                        for hgrp in range((nchunk + 3) // 4):
                            lo, hi = hgrp * 4, min(nchunk, hgrp * 4 + 4)
                            tdeps = list(rdeps) + ([self.tr_ev] if self.tr_ev else [])
                            def trf(t, srcrot=srcrot, lo=lo, hi=hi):
                                for c in range(lo, hi):
                                    ins = t.transpose(ps_tr[:, (c - lo) * 128:(c - lo + 1) * 128],
                                                      srcrot[:, c * 128:(c + 1) * 128], self.idn_bf[:])
                                return ins
                            trt = S.op("T", trf, deps=tdeps)
                            evdeps = [trt] + list(self.asm_st)
                            def evf(v, asm=asm, lo=lo, hi=hi, tt=tt):
                                return v.tensor_copy(
                                    asm[:].rearrange("p (c q) -> p c q", q=512)[:, lo:hi, tt * 128:tt * 128 + 128],
                                    ps_tr[:, 0:(hi - lo) * 128].rearrange("p (c q) -> p c q", q=128))
                            evt = S.op("V", evf, deps=evdeps)
                            self.tr_ev = evt
                            evts.append(evt)
                        return evts
                    qev = emit_tr(qrot, 8, qf_asm, [t for t in rope_q if t])
                    kev = emit_tr(krot, 2 if mode == "l0" else 8, kf_asm, [t for t in rope_k if t])
                    self.rot_rd = qev[-1]
                    self.prev_tab[sl] = e2
                    self.prev_tab[sl] = qev[-1]
                    if tt == TB // 128 - 1:
                        qsts = [S.dma("Y", f"{tag}_qst", Qf[c * 128:(c + 1) * 128, tslice],
                                      qf_asm[:, c * 512:(c + 1) * 512], deps=[qev[-1]])
                                for c in range(8)]
                        nkc_k = 2 if mode == "l0" else 8
                        ksts = [S.dma("Y", f"{tag}_kst", Kf[c * 128:(c + 1) * 128, tslice],
                                      kf_asm[:, c * 512:(c + 1) * 512], deps=[kev[-1]])
                                for c in range(nkc_k)]
                        self.asm_st = [qsts[-1], ksts[-1]]
                prev_act_mm = ("T", S.ecnt["T"])
                prev_x_readers = norm_toks + [cast_toks[-1], sq_toks[-1]]
            S.barrier()

    def emit_attn_core(self, Qf, Kf, Vt, Of, causal, KV, tag, maskq=None, maskt=None):
        S, nc = self.S, self.nc
        seq = self.seq
        ntile = seq // 128
        NQB = seq // 512
        HPK = 16 // KV
        units = [(kv, kv * HPK + hh, qb) for kv in range(KV)
                 for hh in range(HPK) for qb in range(NQB)]
        gsize = HPK * NQB
        with contextlib.ExitStack() as ph:
            katt = [ph.enter_context(nc.sbuf_tensor(f"{tag}_katt{i}", [128, seq], BF16)) for i in range(2)]
            qatt = [ph.enter_context(nc.sbuf_tensor(f"{tag}_qatt{i}", [128, 512], BF16)) for i in range(2)]
            q128 = [ph.enter_context(nc.sbuf_tensor(f"{tag}_q128{i}", [128, 512], BF16)) for i in range(2)]
            vh = [ph.enter_context(nc.sbuf_tensor(f"{tag}_vh{i}", [128, ntile * 65], BF16)) for i in range(2)]
            nm4 = [ph.enter_context(nc.sbuf_tensor(f"{tag}_nm4{i}", [128, 4], BF16)) for i in range(2)]
            nmsb = [ph.enter_context(nc.sbuf_tensor(f"{tag}_nmsb{i}", [4, 128], BF16)) for i in range(2)]
            mxall = ph.enter_context(nc.sbuf_tensor(f"{tag}_mxall", [128, 32], F32))
            pt16 = [ph.enter_context(nc.sbuf_tensor(f"{tag}_pt{i}", [128, 512], BF16)) for i in range(2)]
            osb = ph.enter_context(nc.sbuf_tensor(f"{tag}_osb", [64, 512], F32))
            o16 = ph.enter_context(nc.sbuf_tensor(f"{tag}_o16", [64, 512], BF16))
            l16row = ph.enter_context(nc.sbuf_tensor(f"{tag}_l16row", [1, 512], BF16))
            rsb64 = ph.enter_context(nc.sbuf_tensor(f"{tag}_rsb64", [64, 512], F32))
            if causal:
                mq = ph.enter_context(nc.sbuf_tensor(f"{tag}_mq", [128, 4 * 512], F32))
                mt = ph.enter_context(nc.sbuf_tensor(f"{tag}_mt", [128, 4 * 512], F32))
            psS = [ph.enter_context(nc.psum_tensor(f"{tag}_psS{i}", [128, 512], F32)) for i in range(2)]
            psT = [ph.enter_context(nc.psum_tensor(f"{tag}_psT{i}", [128, 512], F32)) for i in range(2)]
            psO = [ph.enter_context(nc.psum_tensor(f"{tag}_psO{i}", [65, 512], F32)) for i in range(2)]
            psB = ph.enter_context(nc.psum_tensor(f"{tag}_psB", [64, 512], F32))
            psNM = ph.enter_context(nc.psum_tensor(f"{tag}_psNM", [4, 128], BF16))
            if causal:
                for i in range(4):
                    S.dma("Y", f"{tag}_mld", mq[:, i * 512:(i + 1) * 512], maskq[i * 128:(i + 1) * 128, :])
                    S.dma("Y", f"{tag}_mld", mt[:, i * 512:(i + 1) * 512], maskt[i * 128:(i + 1) * 128, :])
            kz = [S.op("V", lambda v, i=i: v.memset(katt[i][:], 0.0)) for i in range(2)]
            qz = [S.op("V", lambda v, i=i: v.memset(qatt[i][:], 0.0)) for i in range(2)]
            q2z = [S.op("V", lambda v, i=i: v.memset(q128[i][:], 0.0)) for i in range(2)]
            S.barrier()

            st_state = {}
            self.ac_kv = [None, None]
            self.ac_kv_users = [None, None]
            self.ac_q_users = [None, None]
            self.ac_psS = [None, None]
            self.ac_psT = [None, None]
            self.ac_pt = [None, None]
            self.ac_nm = [None, None]
            self.ac_oev = None
            self.ac_ost = None
            self.ac_rfin = None
            self.ac_psO = [None, None]

            def buildA(i):
                kv, h, qb = units[i]
                ksl = kv % 2
                usl = i % 2
                if i % gsize == 0:
                    kdeps = [self.ac_kv_users[ksl]] if self.ac_kv_users[ksl] else []
                    kdeps += [kz[ksl]]
                    ktok = S.dma("Y", f"{tag}_kld{ksl}", katt[ksl][0:64, :],
                                 Kf[kv * 64:(kv + 1) * 64, :], deps=kdeps)
                    kone = S.op("V", lambda v: v.memset(katt[ksl][64:65, :], 1.0), deps=kdeps)
                    vtok = [S.dma("Y", f"{tag}_vld{ksl}", vh[ksl][:, kc * 65:(kc + 1) * 65],
                                  Vt[(kv * ntile + kc) * 128:(kv * ntile + kc + 1) * 128, :],
                                  deps=kdeps) for kc in range(ntile)][-1]
                    self.ac_kv[ksl] = (ktok, kone, vtok)
                ktok, kone, vtok = self.ac_kv[ksl]
                nkb = (qb + 1) if causal else NQB
                qdep = [self.ac_q_users[usl], qz[usl], q2z[usl]] if self.ac_q_users[usl] \
                    else [qz[usl], q2z[usl]]
                qtok = S.dma("Y", f"{tag}_qld{usl}", qatt[usl][0:64, :],
                             Qf[h * 64:(h + 1) * 64, qb * 512:(qb + 1) * 512], deps=qdep)
                qtok2 = S.dma("Y", f"{tag}_q2ld{usl}", q128[usl][0:64, :],
                              Qf[h * 64:(h + 1) * 64, qb * 512:(qb + 1) * 512], deps=qdep)
                nm_writes = []
                red_toks = []
                thunks = []

                def piece_parts(qt, kb):
                    psl = self.ac_nredg % 2
                    self.ac_nredg += 1
                    sdeps = [qtok, qtok2, ktok, kone]
                    if self.ac_psS[psl]:
                        sdeps.append(self.ac_psS[psl])

                    def safn(t, qt=qt, kb=kb, psl=psl):
                        return t.matmul(
                            psS[psl][:], q128[usl][:, qt * 128:(qt + 1) * 128],
                            katt[ksl][:, kb * 512:(kb + 1) * 512], start=True, stop=True)

                    def post(smt, qt=qt, kb=kb, psl=psl):
                        base = 16 * (qt % 2)
                        rdep = [smt]
                        if causal and kb == qb:
                            mk = S.op("V", lambda v, psl=psl, qt=qt: v.tensor_tensor(
                                psS[psl][:], psS[psl][:], mq[:, qt * 512:(qt + 1) * 512],
                                op=ALU.add), deps=[smt], name=f"{tag}_Smask")
                            rdep = [mk]
                        red = S.op("V", lambda v, psl=psl, kb=kb, base=base: v.reduce_max(
                            mxall[:, base + kb:base + kb + 1], psS[psl][:], axis=AX.X),
                            deps=rdep, name=f"{tag}_red")
                        self.ac_psS[psl] = red
                        red_toks.append(red)
                        if kb == nkb - 1:
                            nmdep = list(red_toks) + ([self.ac_nm[usl]] if self.ac_nm[usl] else [])
                            red_toks.clear()
                            rfin = S.op("V", lambda v, base=base: v.reduce_max(
                                mxall[:, base + 15:base + 16], mxall[:, base:base + nkb],
                                axis=AX.X), deps=nmdep)
                            nmw = S.op("V", lambda v, qt=qt, base=base: v.tensor_scalar_mul(
                                nm4[usl][:, qt:qt + 1], mxall[:, base + 15:base + 16], -1.0),
                                deps=[rfin])
                            self.ac_rfin = nmw
                            nm_writes.append(nmw)
                        if qt == 3 and kb == nkb - 1:
                            ntr = S.op("T", lambda t: t.transpose(
                                psNM[:], nm4[usl][:], self.idn_bf[:]), deps=nm_writes +
                                ([self.ac_nm[usl]] if self.ac_nm[usl] else []), name=f"{tag}_nmtr")
                            nev = S.op("V", lambda v: v.tensor_copy(nmsb[usl][:], psNM[:]),
                                       deps=[ntr], name=f"{tag}_nmev")
                            self.ac_nm[usl] = nev
                            nmd = S.dma("G", f"{tag}_nmdma{usl}", qatt[usl][64:65, 0:512],
                                        nmsb[usl][:], deps=[nev, qtok])
                            st_state[i] = (qtok, nmd, ktok, kone, vtok)
                    return safn, sdeps, post

                for qt in range(4):
                    for kb in range(nkb):
                        thunks.append(lambda qt=qt, kb=kb: piece_parts(qt, kb))
                return thunks

            def emitB(i, athunks):
                kv, h, qb = units[i]
                ksl = kv % 2
                usl = i % 2
                sl2 = i % 2
                qtok, nmd, ktok, kone, vtok = st_state.pop(i)
                nkc = 4 * (qb + 1) if causal else ntile
                exp_toks = [None] * nkc
                pv_last = None
                adone = 0
                na = len(athunks)

                def emitPVSA(kc_pv, parts):
                    nonlocal pv_last
                    fns = []
                    deps = []
                    if kc_pv is not None:
                        psl = kc_pv % 2
                        deps += [exp_toks[kc_pv], vtok]
                        if kc_pv == 0 and self.ac_psO[sl2]:
                            deps.append(self.ac_psO[sl2])

                        def pvfn(t, kc=kc_pv, psl=psl):
                            return t.matmul(
                                psO[sl2][:], vh[ksl][:, kc * 65:(kc + 1) * 65], pt16[psl][:],
                                start=(kc == 0), stop=(kc == nkc - 1))
                        fns.append(pvfn)
                    for (safn, sadeps, _) in parts:
                        fns.append(safn)
                        deps += sadeps
                    if not fns:
                        return
                    tok = S.op("T", lambda t, fns=tuple(fns): [f(t) for f in fns][-1],
                               deps=deps, name=f"{tag}_PVSA")
                    if kc_pv is not None:
                        self.ac_pt[kc_pv % 2] = tok
                        pv_last = tok
                    for (_, _, post) in parts:
                        post(tok)

                for kc in range(nkc):
                    psl = kc % 2
                    stdeps = [nmd, ktok, kone]
                    if self.ac_psT[psl]:
                        stdeps.append(self.ac_psT[psl])
                    stt_ = S.op("T", lambda t, kc=kc, psl=psl: t.matmul(
                        psT[psl][:], katt[ksl][:, kc * 128:(kc + 1) * 128], qatt[usl][:],
                        start=True, stop=True), deps=stdeps, name=f"{tag}_ST")
                    edep = [stt_]
                    if causal and kc >= 4 * qb:
                        mk = S.op("V", lambda v, psl=psl, kc=kc, qb=qb: v.tensor_tensor(
                            psT[psl][:], psT[psl][:], mt[:, (kc - 4 * qb) * 512:(kc - 4 * qb + 1) * 512],
                            op=ALU.add), deps=[stt_], name=f"{tag}_Tmask")
                        edep = [mk]
                    if self.ac_pt[psl]:
                        edep.append(self.ac_pt[psl])
                    ex = S.op("S", lambda s, psl=psl: s.activation(
                        out=pt16[psl][:], in_=psT[psl][:], func=AF.Exp), deps=edep,
                        name=f"{tag}_exp")
                    self.ac_psT[psl] = ex
                    exp_toks[kc] = ex
                    parts = []
                    while adone * nkc < (kc + 1) * na:
                        parts.append(athunks[adone]())
                        adone += 1
                    emitPVSA(kc - 1 if kc > 0 else None, parts)
                parts = [athunks[j]() for j in range(adone, na)]
                adone = na
                emitPVSA(nkc - 1, parts)
                l1 = S.op("S", lambda s: s.copy(l16row[:], psO[sl2][64:65, :]),
                          deps=[pv_last] + ([self.ac_oev] if self.ac_oev else []), name=f"{tag}_lrow")
                oc = S.op("S", lambda s: s.copy(osb[:], psO[sl2][0:64, :]),
                          deps=[pv_last] + ([self.ac_oev] if self.ac_oev else []), name=f"{tag}_ocopy")
                bmm = S.op("T", lambda t: t.matmul(psB[:], self.ones_row_bf[0:1, 0:64],
                                                   l16row[:], start=True, stop=True),
                           deps=[l1] + ([self.ac_oev] if self.ac_oev else []), name=f"{tag}_bmm")
                rec = S.op("V", lambda v: v.reciprocal(rsb64[:], psB[:]),
                           deps=[bmm] + ([self.ac_oev] if self.ac_oev else []), name=f"{tag}_rec")
                oev = S.op("V", lambda v: v.tensor_tensor(o16[:], osb[:], rsb64[:], op=ALU.mult),
                           deps=[rec, oc] + ([self.ac_ost] if self.ac_ost else []), name=f"{tag}_oev")
                ost = S.dma("Y", f"{tag}_ost", Of[h * 64:(h + 1) * 64, qb * 512:(qb + 1) * 512],
                            o16[:], deps=[oev])
                self.ac_oev = oev
                self.ac_ost = ost
                self.ac_psO[sl2] = oev
                self.ac_q_users[usl] = pv_last
                self.ac_kv_users[ksl] = pv_last

            self.ac_nredg = 0
            for th in buildA(0):
                safn, sadeps, post = th()
                tok = S.op("T", lambda t, safn=safn: safn(t), deps=sadeps,
                           name=f"{tag}_S0")
                post(tok)
            for i in range(len(units)):
                athunks = buildA(i + 1) if i + 1 < len(units) else []
                emitB(i, athunks)
            S.barrier()


    def emit_moe(self, zsrc, xsrc, xdst, wbf, cols, tag):
        S, nc = self.S, self.nc
        TB = self.tb
        with contextlib.ExitStack() as ph:
            zsb = ph.enter_context(nc.sbuf_tensor(f"{tag}_z", [128, 8 * TB], F32))
            xsb = ph.enter_context(nc.sbuf_tensor(f"{tag}_x", [128, 8 * TB], F32))
            xa = [ph.enter_context(nc.sbuf_tensor(f"{tag}_xa{i}", [128, TB], F32)) for i in range(2)]
            z16 = ph.enter_context(nc.sbuf_tensor(f"{tag}_z16", [128, 8 * TB], BF16))
            sq16 = ph.enter_context(nc.sbuf_tensor(f"{tag}_sq", [128, 8 * TB], BF16))
            h16 = ph.enter_context(nc.sbuf_tensor(f"{tag}_h16", [128, 8 * TB], BF16))
            he16 = ph.enter_context(nc.sbuf_tensor(f"{tag}_he", [128, 8 * TB], BF16))
            gl16 = ph.enter_context(nc.sbuf_tensor(f"{tag}_gl", [128, 32 * TB], BF16))
            macc = ph.enter_context(nc.sbuf_tensor(f"{tag}_macc", [128, 8 * TB], F32))
            sgb = [ph.enter_context(nc.sbuf_tensor(f"{tag}_sg{i}", [128, TB], BF16)) for i in range(2)]
            msb = ph.enter_context(nc.sbuf_tensor(f"{tag}_msb", [128, 4 * TB], BF16))
            m4 = ph.enter_context(nc.sbuf_tensor(f"{tag}_m4", [128, 16], BF16))
            mT = ph.enter_context(nc.sbuf_tensor(f"{tag}_mT", [4, 512], BF16))
            mrow = ph.enter_context(nc.sbuf_tensor(f"{tag}_mrow", [1, 4 * TB], BF16))
            rmax = ph.enter_context(nc.sbuf_tensor(f"{tag}_rmax", [128, 1], F32))
            stat = ph.enter_context(nc.sbuf_tensor(f"{tag}_st", [1, 6 * TB], F32))
            rsb = ph.enter_context(nc.sbuf_tensor(f"{tag}_rw", [128, 32], BF16))
            wsl = {"wv": [ph.enter_context(nc.sbuf_tensor(f"{tag}_wv{i}", [128, 1024], BF16)) for i in range(4)],
                   "pj": [ph.enter_context(nc.sbuf_tensor(f"{tag}_pj{i}", [128, 4096], BF16)) for i in range(4)]}
            WQE = ("Y", "Y", "Y", "Y")
            ps_sum = ph.enter_context(nc.psum_tensor(f"{tag}_psum", [1, TB], F32))
            ps_sq = ph.enter_context(nc.psum_tensor(f"{tag}_psq", [1, TB], F32))
            ps_A = ph.enter_context(nc.psum_tensor(f"{tag}_psA", [128, TB], F32))
            ps_C = ph.enter_context(nc.psum_tensor(f"{tag}_psC", [128, TB], F32))
            ps = [ph.enter_context(nc.psum_tensor(f"{tag}_ps{i}", [128, TB], F32)) for i in range(2)]
            ps_r = ph.enter_context(nc.psum_tensor(f"{tag}_psr", [128, 4], F32))
            ps_nm = ph.enter_context(nc.psum_tensor(f"{tag}_psnm", [4, 128], BF16))
            for kc in range(8):
                S.dma("Y", f"{tag}_rld", rsb[:, kc * 4:(kc + 1) * 4],
                      wbf["router"][kc * 128:(kc + 1) * 128, :])
            S.barrier()
            prev_x_readers = None
            prev_statreaders = []
            prev_psAC = None
            prev_xa = [None, None]
            prev_h_mm = None
            prev_he_mm = None
            prev_gl_mm = None
            prev_m4 = None
            prev_mT = None
            prev_mrow_rd = None
            prev_msb_rd = None
            prev_sg = [None, None]
            wtoks = {"wv": [None] * 4, "pj": [None] * 4}
            ltoks = {"wv": [None] * 4, "pj": [None] * 4}
            prev_macc_st = None
            for b in range(self.nb):
                tslice = slice(b * TB, (b + 1) * TB)
                xdep = list(prev_x_readers) if prev_x_readers else []
                zts = [S.dma("Y", f"{tag}_zin", zsb[:, c * TB:(c + 1) * TB],
                             zsrc[c * 128:(c + 1) * 128, tslice], deps=xdep) for c in range(8)]
                xts = [S.dma("Y", f"{tag}_xin", xsb[:, c * TB:(c + 1) * TB],
                             xsrc[c * 128:(c + 1) * 128, tslice], deps=xdep) for c in range(8)]
                ztok, xtok = zts[-1], xts[-1]
                cast_toks, sq_toks = [], []
                for c in range(8):
                    cast_toks.append(S.op("V", lambda v, c=c: v.tensor_copy(
                        z16[:, c * TB:(c + 1) * TB], zsb[:, c * TB:(c + 1) * TB]), deps=[ztok]))
                    sq_toks.append(S.op("S", lambda s, c=c: s.activation(
                        out=sq16[:, c * TB:(c + 1) * TB], in_=zsb[:, c * TB:(c + 1) * TB],
                        func=AF.Square), deps=[ztok]))
                def mmsum(t):
                    for c in range(8):
                        ins = t.matmul(ps_sum[:], self.ones_col_bf[:], z16[:, c * TB:(c + 1) * TB],
                                       start=(c == 0), stop=(c == 7))
                    return ins
                def mmsq(t):
                    for c in range(8):
                        ins = t.matmul(ps_sq[:], self.ones_col_bf[:], sq16[:, c * TB:(c + 1) * TB],
                                       start=(c == 0), stop=(c == 7))
                    return ins
                sdep = cast_toks + sq_toks
                tsum = S.op("T", mmsum, deps=sdep)
                tsq = S.op("T", mmsq, deps=sdep)
                mu = stat[:, 0:TB]; ex2 = stat[:, TB:2 * TB]; var = stat[:, 2 * TB:3 * TB]
                std = stat[:, 3 * TB:4 * TB]; rinv = stat[:, 4 * TB:5 * TB]; nmr = stat[:, 5 * TB:6 * TB]
                v1 = S.op("V", lambda v: v.tensor_scalar_mul(mu, ps_sum[:], 1.0 / D),
                          deps=[tsum] + prev_statreaders)
                v2 = S.op("V", lambda v: v.tensor_scalar_mul(ex2, ps_sq[:], 1.0 / D), deps=[tsq])
                v3 = S.op("V", lambda v: v.scalar_tensor_tensor(
                    var, mu, 1.0, mu, op0=ALU.mult, op1=ALU.mult), deps=[v1])
                v4 = S.op("V", lambda v: v.tensor_sub(var, ex2, var), deps=[v2, v3])
                s5 = S.op("S", lambda s: s.activation(out=std, in_=var, func=AF.Sqrt, bias=EPS),
                          deps=[v4])
                v6 = S.op("V", lambda v: v.reciprocal(rinv, std), deps=[s5])
                v7 = S.op("V", lambda v: v.scalar_tensor_tensor(
                    nmr, mu, -1.0, rinv, op0=ALU.mult, op1=ALU.mult), deps=[v6, v1])
                pdep = [prev_psAC] if prev_psAC else []
                tA = S.op("T", lambda t: t.matmul(ps_A[:], self.ones_row_f[:], rinv,
                                                  start=True, stop=True), deps=[v6] + pdep)
                tC = S.op("T", lambda t: t.matmul(ps_C[:], self.ones_row_f[:], nmr,
                                                  start=True, stop=True), deps=[v7] + pdep)
                norm_toks = []
                for c in range(8):
                    sl2 = c % 2
                    d1 = [tA, ztok] + ([prev_xa[sl2]] if prev_xa[sl2] else [])
                    n1 = S.op("V", lambda v, c=c: v.tensor_mul(
                        xa[c % 2][:], zsb[:, c * TB:(c + 1) * TB], ps_A[:]), deps=d1)
                    d2 = [tC, n1] + ([prev_h_mm] if prev_h_mm else [])
                    n2 = S.op("V", lambda v, c=c: v.tensor_tensor(
                        xa[c % 2][:], xa[c % 2][:], ps_C[:], op=ALU.add), deps=d2)
                    n3 = S.op("V", lambda v, c=c: v.tensor_scalar(
                        h16[:, c * TB:(c + 1) * TB], xa[c % 2][:],
                        self.col(cols["ln2w"][0], c), self.col(cols["ln2b"][0], c),
                        op0=ALU.mult, op1=ALU.add), deps=[n2])
                    prev_xa[sl2] = n3
                    norm_toks.append(n3)
                prev_statreaders = [tA, tC]
                prev_psAC = norm_toks[-1]
                m4w = []
                for tt in range(TB // 128):
                    rdep = norm_toks + ([prev_m4] if prev_m4 else [])
                    def mmr(t, tt=tt):
                        for kc in range(8):
                            ins = t.matmul(ps_r[:], h16[:, kc * TB + tt * 128:kc * TB + tt * 128 + 128],
                                           rsb[:, kc * 4:(kc + 1) * 4], start=(kc == 0), stop=(kc == 7))
                        return ins
                    rt = S.op("T", mmr, deps=rdep, name=f"{tag}_router")
                    rm = S.op("V", lambda v: v.reduce_max(rmax[:], ps_r[:], axis=AX.X), deps=[rt])
                    mw = S.op("V", lambda v, tt=tt: v.tensor_scalar(
                        m4[:, tt * 4:(tt + 1) * 4], ps_r[:], rmax[:, 0:1], None,
                        op0=ALU.is_ge), deps=[rm])
                    prev_m4 = mw
                    m4w.append(mw)
                mTev = []
                for tt in range(TB // 128):
                    trd = [m4w[tt]] + ([prev_mT] if prev_mT else [])
                    ttr = S.op("T", lambda t, tt=tt: t.transpose(
                        ps_nm[:], m4[:, tt * 4:(tt + 1) * 4], self.idn_bf[:]), deps=trd)
                    tev = S.op("V", lambda v, tt=tt: v.tensor_copy(
                        mT[:, tt * 128:(tt + 1) * 128], ps_nm[:]), deps=[ttr] +
                        ([prev_mrow_rd] if prev_mrow_rd else []))
                    prev_mT = tev
                    mTev.append(tev)
                mrd = S.dma("G", f"{tag}_mrow", mrow[0:1, :], mT[:],
                            deps=mTev + [("T", S.ecnt["T"])])
                prev_mrow_rd = None
                exp_toks = []
                psA_rd = prev_psAC
                for e in range(4):
                    bdep = [mrd, psA_rd] + ([exp_toks[-1][0]] if exp_toks else [])
                    bc = S.op("T", lambda t, e=e: t.matmul(
                        ps_A[:], self.ones_row_bf[:],
                        mrow[0:1, e * TB:(e + 1) * TB], start=True, stop=True),
                        deps=bdep, name=f"{tag}_mbc")
                    mev = S.op("V", lambda v, e=e: v.tensor_copy(
                        msb[:, e * TB:(e + 1) * TB], ps_A[:]), deps=[bc] +
                        ([prev_msb_rd] if prev_msb_rd else []))
                    psA_rd = mev
                    hetoks = []
                    for c in range(8):
                        hdep = [mev, norm_toks[c]] + ([prev_he_mm] if prev_he_mm else [])
                        hetoks.append(S.op("V", lambda v, c=c, e=e: v.tensor_mul(
                            he16[:, c * TB:(c + 1) * TB], h16[:, c * TB:(c + 1) * TB],
                            msb[:, e * TB:(e + 1) * TB]), deps=hdep))
                    gl_toks = []
                    lastmm = None
                    prev_sg_tok = getattr(self, "_moe_prev_sg", None)
                    for ci in range(32):
                        for half, o in ((0, ci), (1, 32 + ci)):
                            sl = (2 * ci + half) % 4
                            dep = [wtoks["wv"][sl]] if wtoks["wv"][sl] else []
                            ltoks["wv"][sl] = S.dma(WQE[sl], f"{tag}_wvl{sl}", wsl["wv"][sl][:],
                                                    wbf["wv"][e][o * 128:(o + 1) * 128, :], deps=dep)
                            pdeps = hetoks + [ltoks["wv"][sl]]
                            psl = half
                            if half == 0 and prev_sg_tok is not None:
                                pdeps.append(prev_sg_tok)
                            if half == 1 and len(gl_toks) >= 1:
                                pdeps.append(gl_toks[-1])
                            def mmwv(t, o=o, psl=psl, sl=sl):
                                for k in range(8):
                                    ins = t.matmul(ps[psl][:], wsl["wv"][sl][:, k * 128:(k + 1) * 128],
                                                   he16[:, k * TB:(k + 1) * TB], start=(k == 0),
                                                   stop=(k == 7))
                                return ins
                            mm = S.op("T", mmwv, deps=pdeps, name=f"{tag}_wv{e}_{o}")
                            wtoks["wv"][sl] = mm
                            if half == 0:
                                sdep2 = [mm] + ([prev_sg[ci % 2]] if prev_sg[ci % 2] else [])
                                sg = S.op("S", lambda s, ci=ci: s.activation(
                                    out=sgb[ci % 2][:], in_=ps[0][:], func=AF.Silu), deps=sdep2)
                                prev_sg[ci % 2] = None
                                prev_sg_tok = sg
                                self._moe_prev_sg = sg
                                gmm = mm
                            else:
                                gdep = [sg, mm] + ([prev_gl_mm] if prev_gl_mm and ci == 0 else [])
                                gl = S.op("V", lambda v, ci=ci: v.tensor_tensor(
                                    gl16[:, ci * TB:(ci + 1) * TB], sgb[ci % 2][:], ps[1][:],
                                    op=ALU.mult), deps=gdep)
                                prev_sg[ci % 2] = gl
                                gl_toks.append(gl)
                    ev_toks = []
                    for o in range(8):
                        sl = o % 4
                        psl = o % 2
                        dep = [wtoks["pj"][sl]] if wtoks["pj"][sl] else []
                        ltoks["pj"][sl] = S.dma(WQE[sl], f"{tag}_pjl{sl}", wsl["pj"][sl][:],
                                                wbf["pj"][e][o * 128:(o + 1) * 128, :], deps=dep)
                        pdeps = gl_toks + [ltoks["pj"][sl]]
                        if len(ev_toks) >= 2:
                            pdeps.append(ev_toks[-2])
                        def mmpj(t, o=o, sl=sl, psl=psl):
                            for k in range(32):
                                ins = t.matmul(ps[psl][:], wsl["pj"][sl][:, k * 128:(k + 1) * 128],
                                               gl16[:, k * TB:(k + 1) * TB], start=(k == 0),
                                               stop=(k == 31))
                            return ins
                        mm = S.op("T", mmpj, deps=pdeps, name=f"{tag}_pj{e}_{o}")
                        wtoks["pj"][sl] = mm
                        if e == 0:
                            mdep = [mm] + ([prev_macc_st] if prev_macc_st else [])
                            ev = S.op("V", lambda v, o=o, psl=psl: v.tensor_copy(
                                macc[:, o * TB:(o + 1) * TB], ps[psl][:]), deps=mdep)
                        else:
                            ev = S.op("V", lambda v, o=o, psl=psl: v.tensor_tensor(
                                macc[:, o * TB:(o + 1) * TB], ps[psl][:],
                                macc[:, o * TB:(o + 1) * TB], op=ALU.add), deps=[mm, exp_toks[-1][1][o]])
                        ev_toks.append(ev)
                    exp_toks.append((mm, ev_toks))
                    prev_he_mm = ("T", S.ecnt["T"])
                    prev_gl_mm = ("T", S.ecnt["T"])
                prev_msb_rd = ("V", S.ecnt["V"])
                prev_psAC = psA_rd
                stE = []
                evE = []
                for c in range(8):
                    e1 = S.op("V", lambda v, c=c: v.scalar_tensor_tensor(
                        xsb[:, c * TB:(c + 1) * TB], zsb[:, c * TB:(c + 1) * TB], 0.5,
                        xsb[:, c * TB:(c + 1) * TB], op0=ALU.mult, op1=ALU.add),
                        deps=[xtok, ztok, norm_toks[c], exp_toks[-1][1][c], cast_toks[c], sq_toks[c]])
                    e2 = S.op("V", lambda v, c=c: v.scalar_tensor_tensor(
                        xsb[:, c * TB:(c + 1) * TB], macc[:, c * TB:(c + 1) * TB], 0.5,
                        xsb[:, c * TB:(c + 1) * TB], op0=ALU.mult, op1=ALU.add), deps=[e1])
                    evE.append(e2)
                    stE.append(S.dma("Y", f"{tag}_xout", xdst[c * 128:(c + 1) * 128, tslice],
                                     xsb[:, c * TB:(c + 1) * TB], deps=[e2]))
                prev_macc_st = evE[-1]
                prev_x_readers = evE + [stE[-1]]
                prev_h_mm = ("T", S.ecnt["T"])
                prev_m4 = mTev[-1]
                prev_mrow_rd = ("T", S.ecnt["T"])
            S.barrier()

    def emit_proj_resid(self, srcs, w_pre, resid_src, xdst, tag, bias_col=None, oc_n=8):
        S, nc = self.S, self.nc
        TB = self.tb
        kc_n = sum(n for _, n in srcs)
        with contextlib.ExitStack() as ph:
            asb = ph.enter_context(nc.sbuf_tensor(f"{tag}_a", [128, kc_n * TB], BF16))
            xsb = ph.enter_context(nc.sbuf_tensor(f"{tag}_x", [128, oc_n * TB], F32))
            wsl = [ph.enter_context(nc.sbuf_tensor(f"{tag}_w{i}", [128, kc_n * 128], BF16))
                   for i in range(2)]
            ps = [ph.enter_context(nc.psum_tensor(f"{tag}_ps{i}", [128, TB], F32)) for i in range(2)]
            wtoks = [None, None]
            ltoks = [None, None]
            prev_rd = None
            prev_wr = None
            for b in range(self.nb):
                tslice = slice(b * TB, (b + 1) * TB)
                adep = [prev_rd] if prev_rd else []
                ats = []
                cbase = 0
                for (ap, nch) in srcs:
                    for c in range(nch):
                        ats.append(S.dma("Y", f"{tag}_ain", asb[:, (cbase + c) * TB:(cbase + c + 1) * TB],
                                         ap[c * 128:(c + 1) * 128, tslice], deps=adep))
                    cbase += nch
                atok = ats[-1]
                xdep = [prev_wr] if prev_wr else []
                xts = [S.dma("Y", f"{tag}_xin", xsb[:, c * TB:(c + 1) * TB],
                             resid_src[c * 128:(c + 1) * 128, tslice], deps=xdep)
                       for c in range(oc_n)]
                xtok = xts[-1]
                evs = []
                for o in range(oc_n):
                    sl = o % 2
                    if b == 0 and o < 2:
                        dep = [wtoks[sl]] if wtoks[sl] else []
                        ltoks[sl] = S.dma("Y", f"{tag}_wld{sl}", wsl[sl][:],
                                          w_pre[o * 128:(o + 1) * 128, :], deps=dep)
                    pdeps = [ltoks[sl], atok]
                    if len(evs) >= 2:
                        pdeps.append(evs[-2])
                    def mmg(t, o=o, sl=sl):
                        for k in range(kc_n):
                            ins = t.matmul(ps[sl][:], wsl[sl][:, k * 128:(k + 1) * 128],
                                           asb[:, k * TB:(k + 1) * TB], start=(k == 0),
                                           stop=(k == kc_n - 1))
                        return ins
                    te = S.op("T", mmg, deps=pdeps, name=f"{tag}_mm{b}_{o}")
                    wtoks[sl] = te
                    nxt = o + 2
                    if nxt < oc_n or b + 1 < self.nb:
                        wo = nxt % oc_n
                        ltoks[sl] = S.dma("Y", f"{tag}_wld{sl}", wsl[sl][:],
                                          w_pre[wo * 128:(wo + 1) * 128, :], deps=[te])
                    if bias_col is not None:
                        ev = S.op("V", lambda v, o=o, sl=sl: v.scalar_tensor_tensor(
                            xsb[:, o * TB:(o + 1) * TB], ps[sl][:], self.col(bias_col[0], o),
                            xsb[:, o * TB:(o + 1) * TB], op0=ALU.add, op1=ALU.add),
                            deps=[te, xtok], name=f"{tag}_ev{b}_{o}")
                    else:
                        ev = S.op("V", lambda v, o=o, sl=sl: v.scalar_tensor_tensor(
                            xsb[:, o * TB:(o + 1) * TB], ps[sl][:], 1.0,
                            xsb[:, o * TB:(o + 1) * TB], op0=ALU.mult, op1=ALU.add),
                            deps=[te, xtok], name=f"{tag}_ev{b}_{o}")
                    evs.append(ev)
                    last_st = S.dma("Y", f"{tag}_xout", xdst[o * 128:(o + 1) * 128, tslice],
                                    xsb[:, o * TB:(o + 1) * TB], deps=[ev])
                prev_rd = te
                prev_wr = last_st
            S.barrier()

    def emit_l1a_proj(self, xsrc, Qf, Kf, Af, Bf, Vtok, w, cols, tag):
        S, nc = self.S, self.nc
        TB = self.tb
        with contextlib.ExitStack() as ph:
            xsb = ph.enter_context(nc.sbuf_tensor(f"{tag}_x", [128, 8 * TB], F32))
            xa = [ph.enter_context(nc.sbuf_tensor(f"{tag}_xa{i}", [128, TB], F32)) for i in range(2)]
            x16 = ph.enter_context(nc.sbuf_tensor(f"{tag}_x16", [128, 8 * TB], BF16))
            sq16 = ph.enter_context(nc.sbuf_tensor(f"{tag}_sq", [128, 8 * TB], BF16))
            t16 = ph.enter_context(nc.sbuf_tensor(f"{tag}_t16", [128, 8 * TB], BF16))
            stat = ph.enter_context(nc.sbuf_tensor(f"{tag}_st", [1, 6 * TB], F32))
            out16 = ph.enter_context(nc.sbuf_tensor(f"{tag}_o16", [128, TB], BF16))
            vout = ph.enter_context(nc.sbuf_tensor(f"{tag}_vout", [128, 1024], BF16))
            wvsb = ph.enter_context(nc.sbuf_tensor(f"{tag}_wv", [128, 8 * 1024], BF16))
            bvsb = ph.enter_context(nc.sbuf_tensor(f"{tag}_bv", [1, 1024], F32))
            bv16 = ph.enter_context(nc.sbuf_tensor(f"{tag}_bv16", [1, 1024], BF16))
            wsl = [ph.enter_context(nc.sbuf_tensor(f"{tag}_w{i}", [128, 1024], BF16)) for i in range(2)]
            ps_sum = ph.enter_context(nc.psum_tensor(f"{tag}_psum", [1, TB], F32))
            ps_sq = ph.enter_context(nc.psum_tensor(f"{tag}_psq", [1, TB], F32))
            ps_A = ph.enter_context(nc.psum_tensor(f"{tag}_psA", [128, TB], F32))
            ps_C = ph.enter_context(nc.psum_tensor(f"{tag}_psC", [128, TB], F32))
            ps = [ph.enter_context(nc.psum_tensor(f"{tag}_ps{i}", [128, TB], F32)) for i in range(2)]
            for kc in range(8):
                S.dma("Y", f"{tag}_wvl", wvsb[:, kc * 1024:(kc + 1) * 1024],
                      w["wv_pl"][kc * 128:(kc + 1) * 128, :])
            bvl = S.dma("Y", f"{tag}_bvl", bvsb[:], w["bv_row"][:])
            bvc = S.op("V", lambda v: v.tensor_copy(bv16[:], bvsb[:]), deps=[bvl])
            S.barrier()
            projs = [("q", w["wq"], Qf, cols.get("bq32"), True),
                     ("k", w["wk"], Kf, cols.get("bk"), False),
                     ("a", w["wm1a"], Af, None, False),
                     ("b", w["wm1b"], Bf, None, False)]
            wtoks = [None, None]
            ltoks = [None, None]
            prev_x_readers = None
            prev_statreaders = []
            prev_psAC = None
            prev_xa = [None, None]
            prev_t_mm = None
            prev_o16 = None
            prev_vout = None
            for b in range(self.nb):
                tslice = slice(b * TB, (b + 1) * TB)
                xdep = list(prev_x_readers) if prev_x_readers else []
                xts = [S.dma("Y", f"{tag}_xin", xsb[:, c * TB:(c + 1) * TB],
                             xsrc[c * 128:(c + 1) * 128, tslice], deps=xdep) for c in range(8)]
                xtok = xts[-1]
                cast_toks, sq_toks = [], []
                for c in range(8):
                    cast_toks.append(S.op("V", lambda v, c=c: v.tensor_copy(
                        x16[:, c * TB:(c + 1) * TB], xsb[:, c * TB:(c + 1) * TB]), deps=[xtok]))
                    sq_toks.append(S.op("S", lambda s, c=c: s.activation(
                        out=sq16[:, c * TB:(c + 1) * TB], in_=xsb[:, c * TB:(c + 1) * TB],
                        func=AF.Square), deps=[xtok]))
                def mmsum(t):
                    for c in range(8):
                        ins = t.matmul(ps_sum[:], self.ones_col_bf[:], x16[:, c * TB:(c + 1) * TB],
                                       start=(c == 0), stop=(c == 7))
                    return ins
                def mmsq(t):
                    for c in range(8):
                        ins = t.matmul(ps_sq[:], self.ones_col_bf[:], sq16[:, c * TB:(c + 1) * TB],
                                       start=(c == 0), stop=(c == 7))
                    return ins
                sdep = cast_toks + sq_toks
                tsum = S.op("T", mmsum, deps=sdep)
                tsq = S.op("T", mmsq, deps=sdep)
                mu = stat[:, 0:TB]; ex2 = stat[:, TB:2 * TB]; var = stat[:, 2 * TB:3 * TB]
                std = stat[:, 3 * TB:4 * TB]; rinv = stat[:, 4 * TB:5 * TB]; nmr = stat[:, 5 * TB:6 * TB]
                v1 = S.op("V", lambda v: v.tensor_scalar_mul(mu, ps_sum[:], 1.0 / D),
                          deps=[tsum] + prev_statreaders)
                v2 = S.op("V", lambda v: v.tensor_scalar_mul(ex2, ps_sq[:], 1.0 / D), deps=[tsq])
                v3 = S.op("V", lambda v: v.scalar_tensor_tensor(
                    var, mu, 1.0, mu, op0=ALU.mult, op1=ALU.mult), deps=[v1])
                v4 = S.op("V", lambda v: v.tensor_sub(var, ex2, var), deps=[v2, v3])
                s5 = S.op("S", lambda s: s.activation(out=std, in_=var, func=AF.Sqrt, bias=EPS),
                          deps=[v4])
                v6 = S.op("V", lambda v: v.reciprocal(rinv, std), deps=[s5])
                v7 = S.op("V", lambda v: v.scalar_tensor_tensor(
                    nmr, mu, -1.0, rinv, op0=ALU.mult, op1=ALU.mult), deps=[v6, v1])
                pdep = [prev_psAC] if prev_psAC else []
                tA = S.op("T", lambda t: t.matmul(ps_A[:], self.ones_row_f[:], rinv,
                                                  start=True, stop=True), deps=[v6] + pdep)
                tC = S.op("T", lambda t: t.matmul(ps_C[:], self.ones_row_f[:], nmr,
                                                  start=True, stop=True), deps=[v7] + pdep)
                norm_toks = []
                for c in range(8):
                    sl2 = c % 2
                    d1 = [tA, xtok] + ([prev_xa[sl2]] if prev_xa[sl2] else [])
                    n1 = S.op("V", lambda v, c=c: v.tensor_mul(
                        xa[c % 2][:], xsb[:, c * TB:(c + 1) * TB], ps_A[:]), deps=d1)
                    d2 = [tC, n1] + ([prev_t_mm] if prev_t_mm else [])
                    n2 = S.op("V", lambda v, c=c: v.tensor_tensor(
                        t16[:, c * TB:(c + 1) * TB], xa[c % 2][:], ps_C[:], op=ALU.add), deps=d2)
                    prev_xa[sl2] = n2
                    norm_toks.append(n2)
                prev_statreaders = [tA, tC]
                prev_psAC = norm_toks[-1]
                evs = []
                for (pn, wpre, dst, bc, scaled) in projs:
                    for o in range(8):
                        sl = o % 2
                        dep = [wtoks[sl]] if wtoks[sl] else []
                        ltoks[sl] = S.dma("Y", f"{tag}_wld{sl}", wsl[sl][:],
                                          wpre[o * 128:(o + 1) * 128, :], deps=dep)
                        pdeps = [ltoks[sl]] + norm_toks
                        if len(evs) >= 2:
                            pdeps.append(evs[-2])
                        def mmg(t, o=o, sl=sl):
                            for k in range(8):
                                ins = t.matmul(ps[sl][:], wsl[sl][:, k * 128:(k + 1) * 128],
                                               t16[:, k * TB:(k + 1) * TB], start=(k == 0),
                                               stop=(k == 7))
                            return ins
                        te = S.op("T", mmg, deps=pdeps, name=f"{tag}_{pn}{b}_{o}")
                        wtoks[sl] = te
                        edep = [te] + ([prev_o16] if prev_o16 else [])
                        if bc is not None and scaled:
                            ev = S.op("S", lambda s, o=o, sl=sl, bc=bc: s.activation(
                                out=out16[:], in_=ps[sl][:], func=AF.Identity,
                                bias=self.col(bc[0], o), scale=1.0 / 32), deps=edep)
                        elif bc is not None:
                            ev = S.op("S", lambda s, o=o, sl=sl, bc=bc: s.activation(
                                out=out16[:], in_=ps[sl][:], func=AF.Identity,
                                bias=self.col(bc[0], o)), deps=edep)
                        else:
                            ev = S.op("S", lambda s, o=o, sl=sl: s.copy(
                                out16[:], ps[sl][:]), deps=edep)
                        st = S.dma("Y", f"{tag}_pst", dst[o * 128:(o + 1) * 128, tslice],
                                   out16[:], deps=[ev])
                        prev_o16 = st
                        evs.append(ev)
                for tt in range(TB // 128):
                    gtile = b * (TB // 128) + tt
                    for half in range(2):
                        psl = half
                        pdeps = norm_toks + [bvc, evs[-1], evs[-2]]
                        if prev_vout:
                            pdeps.append(prev_vout)
                        def mmv(t, tt=tt, half=half, psl=psl):
                            ins = t.matmul(ps[psl][:, 0:512], self.ones_row_bf[0:1, :],
                                           bv16[0:1, half * 512:(half + 1) * 512],
                                           start=True, stop=False)
                            for kc in range(8):
                                ins = t.matmul(ps[psl][:, 0:512],
                                               t16[:, kc * TB + tt * 128:kc * TB + tt * 128 + 128],
                                               wvsb[:, kc * 1024 + half * 512:kc * 1024 + (half + 1) * 512],
                                               start=False, stop=(kc == 7))
                            return ins
                        tv = S.op("T", mmv, deps=pdeps, name=f"{tag}_v{b}_{tt}_{half}")
                        vev = S.op("V", lambda v, half=half, psl=psl: v.tensor_copy(
                            vout[:, half * 512:(half + 1) * 512], ps[psl][:, 0:512]),
                            deps=[tv] + ([prev_vout] if prev_vout else []))
                        evs.append(vev)
                    vst = S.dma("Y", f"{tag}_vst", Vtok[gtile * 128:(gtile + 1) * 128, :],
                                vout[:], deps=[evs[-1], evs[-2]])
                    prev_vout = vst
                prev_t_mm = ("T", S.ecnt["T"])
                prev_x_readers = norm_toks + [cast_toks[-1], sq_toks[-1]]
            S.barrier()

    def emit_l1a_core(self, Qf, Kf, Vtok, Gf, w, tag):
        S, nc = self.S, self.nc
        seq = self.seq
        NQB = seq // 512
        with contextlib.ExitStack() as ph:
            kres = ph.enter_context(nc.sbuf_tensor(f"{tag}_k", [128, 8 * seq], BF16))
            qblk = ph.enter_context(nc.sbuf_tensor(f"{tag}_q", [128, 8 * 512], BF16))
            ssb = ph.enter_context(nc.sbuf_tensor(f"{tag}_s", [128, seq], F32))
            p16 = ph.enter_context(nc.sbuf_tensor(f"{tag}_p", [128, seq], BF16))
            m16 = ph.enter_context(nc.sbuf_tensor(f"{tag}_m", [128, seq], BF16))
            pt = ph.enter_context(nc.sbuf_tensor(f"{tag}_pt", [128, (seq // 128) * 512], BF16))
            t8 = ph.enter_context(nc.sbuf_tensor(f"{tag}_t8", [128, 8], F32))
            negm = ph.enter_context(nc.sbuf_tensor(f"{tag}_ngm", [128, 1], F32))
            lsum = ph.enter_context(nc.sbuf_tensor(f"{tag}_l", [128, 1], F32))
            linv = ph.enter_context(nc.sbuf_tensor(f"{tag}_li", [128, 1], F32))
            mq = ph.enter_context(nc.sbuf_tensor(f"{tag}_mq", [128, 4 * 512], F32))
            g16 = ph.enter_context(nc.sbuf_tensor(f"{tag}_g16", [128, 512], BF16))
            vsl = [ph.enter_context(nc.sbuf_tensor(f"{tag}_vs{i}", [128, 1024], BF16)) for i in range(2)]
            psS = [ph.enter_context(nc.psum_tensor(f"{tag}_psS{i}", [128, 512], F32)) for i in range(2)]
            psD = [ph.enter_context(nc.psum_tensor(f"{tag}_psD{i}", [128, 512], F32)) for i in range(4)]
            psTr = ph.enter_context(nc.psum_tensor(f"{tag}_ptr", [128, 512], BF16))
            for kc in range(8):
                S.dma("Y", f"{tag}_kld", kres[:, kc * seq:(kc + 1) * seq], Kf[kc * 128:(kc + 1) * 128, :])
            for i in range(4):
                S.dma("Y", f"{tag}_mld", mq[:, i * 512:(i + 1) * 512], w["maskq"][i * 128:(i + 1) * 128, :])
            S.barrier()
            prev_q_rd = None
            prev_s = [None, None]
            prev_ssb_rd = []
            prev_p_rd = None
            prev_pt_rd = None
            prev_tr = None
            prev_g = None
            prev_gst = None
            vltoks = [None, None]
            vwtoks = [None, None]
            for qb in range(NQB):
                qdep = [prev_q_rd] if prev_q_rd else []
                qts = [S.dma("Y", f"{tag}_qld", qblk[:, c * 512:(c + 1) * 512],
                             Qf[c * 128:(c + 1) * 128, qb * 512:(qb + 1) * 512], deps=qdep)
                       for c in range(8)]
                qtok = qts[-1]
                nkc = 4 * (qb + 1)
                tr_toks = []
                for qtl in range(4):
                    qt = qb * 4 + qtl
                    L = qt * 128 + 128
                    nkb = (L + 511) // 512
                    ev_toks = []
                    for kb in range(nkb):
                        wdt = min(512, L - kb * 512) if kb == nkb - 1 else 512
                        wdt = 512
                        psl = kb % 2
                        sdeps = [qtok] + ([prev_s[psl]] if prev_s[psl] else [])
                        def mms(t, qtl=qtl, kb=kb, psl=psl):
                            for c in range(8):
                                ins = t.matmul(psS[psl][:], qblk[:, c * 512 + qtl * 128:c * 512 + qtl * 128 + 128],
                                               kres[:, c * seq + kb * 512:c * seq + kb * 512 + 512],
                                               start=(c == 0), stop=(c == 7))
                            return ins
                        ts_ = S.op("T", mms, deps=sdeps, name=f"{tag}_S{qt}_{kb}")
                        if kb == nkb - 1:
                            ev = S.op("V", lambda v, psl=psl, kb=kb, qtl=qtl: v.tensor_tensor(
                                ssb[:, kb * 512:kb * 512 + 512], psS[psl][:],
                                mq[:, qtl * 512:(qtl + 1) * 512], op=ALU.add),
                                deps=[ts_] + (prev_ssb_rd or []), name=f"{tag}_Sev")
                        else:
                            ev = S.op("S", lambda s, psl=psl, kb=kb: s.copy(
                                ssb[:, kb * 512:kb * 512 + 512], psS[psl][:]),
                                deps=[ts_] + (prev_ssb_rd or []), name=f"{tag}_Sev")
                        prev_s[psl] = ev
                        ev_toks.append(ev)
                    t1 = S.op("V", lambda v, L=L: v.max(out=t8[:], in_=ssb[:, 0:L]),
                              deps=ev_toks, name=f"{tag}_top8")
                    t2 = S.op("V", lambda v: v.tensor_scalar_mul(negm[:], t8[:, 0:1], -1.0), deps=[t1])
                    t3dep = [t2] + ([prev_p_rd] if prev_p_rd else [])
                    t3 = S.op("S", lambda s, L=L: s.activation(
                        out=p16[:, 0:L], in_=ssb[:, 0:L], func=AF.Exp, bias=negm[:, 0:1]),
                        deps=t3dep, name=f"{tag}_exp")
                    t4 = S.op("V", lambda v, L=L: v.tensor_scalar(
                        m16[:, 0:L], ssb[:, 0:L], t8[:, 7:8], None, op0=ALU.is_ge),
                        deps=[t1] + ([prev_p_rd] if prev_p_rd else []), name=f"{tag}_thr")
                    t5 = S.op("V", lambda v, L=L: v.tensor_mul(p16[:, 0:L], p16[:, 0:L], m16[:, 0:L]),
                              deps=[t3, t4], name=f"{tag}_pm")
                    t6 = S.op("V", lambda v, L=L: v.reduce_sum(lsum[:], p16[:, 0:L], axis=AX.X),
                              deps=[t5], name=f"{tag}_lsum")
                    t7 = S.op("V", lambda v: v.reciprocal(linv[:], lsum[:]), deps=[t6])
                    t8_ = S.op("V", lambda v, L=L: v.tensor_scalar_mul(p16[:, 0:L], p16[:, 0:L],
                                                                       linv[:, 0:1]),
                               deps=[t7], name=f"{tag}_pn")
                    nblk = L // 128
                    for grp in range((nblk + 3) // 4):
                        lo, hi = grp * 4, min(nblk, grp * 4 + 4)
                        tdeps = [t8_] + ([prev_tr] if prev_tr else [])
                        def trf(t, lo=lo, hi=hi):
                            for c in range(lo, hi):
                                ins = t.transpose(psTr[:, (c - lo) * 128:(c - lo + 1) * 128],
                                                  p16[:, c * 128:(c + 1) * 128], self.idn_bf[:])
                            return ins
                        trt = S.op("T", trf, deps=tdeps)
                        evdeps = [trt] + ([prev_pt_rd] if prev_pt_rd else [])
                        def evf(v, lo=lo, hi=hi, qtl=qtl):
                            return v.tensor_copy(
                                pt[:].rearrange("p (c q) -> p c q", q=512)[:, lo:hi, qtl * 128:qtl * 128 + 128],
                                psTr[:, 0:(hi - lo) * 128].rearrange("p (c q) -> p c q", q=128))
                        evt = S.op("V", evf, deps=evdeps)
                        prev_tr = evt
                        tr_toks.append(evt)
                    prev_ssb_rd = [t4, t3]
                    prev_p_rd = ("T", S.ecnt["T"])
                for dd in range(1, 4):
                    kc = 4 * qb + dd
                    zt = S.op("V", lambda v, kc=kc, dd=dd: v.memset(
                        pt[:, kc * 512:kc * 512 + dd * 128], 0.0),
                        deps=[prev_pt_rd] if prev_pt_rd else [])
                    tr_toks.append(zt)
                gsts = []
                for half in range(2):
                    pv_toks = []
                    for kc in range(nkc):
                        sl = kc % 2
                        vdep = [vwtoks[sl]] if vwtoks[sl] else []
                        vltoks[sl] = S.dma("Y", f"{tag}_vld{sl}", vsl[sl][:],
                                           Vtok[kc * 128:(kc + 1) * 128, :], deps=vdep)
                        mdeps = [vltoks[sl]] + tr_toks
                        if kc == 0:
                            mdeps += [prev_g] if prev_g else []
                        def mpv(t, kc=kc, sl=sl, half=half, nkc=nkc):
                            for dh in range(4):
                                ins = t.matmul(psD[dh][:], vsl[sl][:, (half * 4 + dh) * 128:(half * 4 + dh + 1) * 128],
                                               pt[:, kc * 512:(kc + 1) * 512],
                                               start=(kc == 0), stop=(kc == nkc - 1))
                            return ins
                        pv = S.op("T", mpv, deps=mdeps, name=f"{tag}_pv{qb}_{half}_{kc}")
                        vwtoks[sl] = pv
                        pv_toks.append(pv)
                    for dh in range(4):
                        gdep = [pv_toks[-1]] + ([prev_gst] if prev_gst else [])
                        gev = S.op("S", lambda s, dh=dh: s.copy(g16[:], psD[dh][:]), deps=gdep,
                                   name=f"{tag}_gev")
                        gst = S.dma("Y", f"{tag}_gst",
                                    Gf[(half * 4 + dh) * 128:(half * 4 + dh + 1) * 128,
                                       qb * 512:(qb + 1) * 512], g16[:], deps=[gev])
                        prev_gst = gst
                        gsts.append(gst)
                        prev_g = gev
                prev_q_rd = ("T", S.ecnt["T"])
                prev_pt_rd = ("T", S.ecnt["T"])
            S.barrier()

    def emit_l1a_local(self, Af, Bf, Lf, w, cols, tag):
        S, nc = self.S, self.nc
        TB = self.tb
        with contextlib.ExitStack() as ph:
            asb = ph.enter_context(nc.sbuf_tensor(f"{tag}_a", [128, 8 * TB], BF16))
            bsb = ph.enter_context(nc.sbuf_tensor(f"{tag}_b", [128, 8 * (TB + 4)], BF16))
            zb = ph.enter_context(nc.sbuf_tensor(f"{tag}_zb", [128, TB], F32))
            sw = ph.enter_context(nc.sbuf_tensor(f"{tag}_sw", [128, TB], F32))
            acc = ph.enter_context(nc.sbuf_tensor(f"{tag}_acc", [128, TB], F32))
            sm16 = ph.enter_context(nc.sbuf_tensor(f"{tag}_sm", [128, 8 * TB], BF16))
            o16 = ph.enter_context(nc.sbuf_tensor(f"{tag}_o", [128, TB], BF16))
            wsl = [ph.enter_context(nc.sbuf_tensor(f"{tag}_w{i}", [128, 1024], BF16)) for i in range(2)]
            ps = [ph.enter_context(nc.psum_tensor(f"{tag}_ps{i}", [128, TB], F32)) for i in range(2)]
            S.barrier()
            wtoks = [None, None]
            ltoks = [None, None]
            prev_sm_mm = None
            prev_rd = None
            prev_ost = None
            for b in range(self.nb):
                tslice = slice(b * TB, (b + 1) * TB)
                adep = [prev_rd] if prev_rd else []
                ats = [S.dma("Y", f"{tag}_ain", asb[:, c * TB:(c + 1) * TB],
                             Af[c * 128:(c + 1) * 128, tslice], deps=adep) for c in range(8)]
                bts = []
                zpads = []
                for c in range(8):
                    boff = c * (TB + 4)
                    if b == 0:
                        zpads.append(S.op("V", lambda v, boff=boff: v.memset(
                            bsb[:, boff:boff + 4], 0.0), deps=adep))
                        bts.append(S.dma("Y", f"{tag}_bin", bsb[:, boff + 4:boff + 4 + TB],
                                         Bf[c * 128:(c + 1) * 128, tslice], deps=adep))
                    else:
                        bts.append(S.dma("Y", f"{tag}_bin", bsb[:, boff:boff + 4 + TB],
                                         Bf[c * 128:(c + 1) * 128, b * TB - 4:(b + 1) * TB],
                                         deps=adep))
                atok, btok = ats[-1], bts[-1]
                sm_toks = []
                for c in range(8):
                    boff = c * (TB + 4)
                    accdep = [atok, btok] + zpads + (sm_toks[-1:] if sm_toks else [])
                    prevacc = None
                    for wd in range(1, 5):
                        z1 = S.op("V", lambda v, c=c, boff=boff, wd=wd: v.tensor_tensor(
                            zb[:], asb[:, c * TB:(c + 1) * TB],
                            bsb[:, boff + 4 - wd:boff + 4 - wd + TB], op=ALU.add),
                            deps=accdep + ([prevacc] if prevacc else []))
                        if wd == 1:
                            sv = S.op("S", lambda s, c=c: s.activation(
                                out=acc[:], in_=zb[:], func=AF.Silu,
                                bias=self.col(cols["bm1"][0], c)), deps=[z1])
                            prevacc = sv
                        else:
                            sv = S.op("S", lambda s, c=c: s.activation(
                                out=sw[:], in_=zb[:], func=AF.Silu,
                                bias=self.col(cols["bm1"][0], c)), deps=[z1, prevacc])
                            av = S.op("V", lambda v: v.tensor_add(acc[:], acc[:], sw[:]),
                                      deps=[sv])
                            prevacc = av
                    smdep = [prevacc] + ([prev_sm_mm] if prev_sm_mm else [])
                    smt = S.op("V", lambda v, c=c: v.tensor_scalar_mul(
                        sm16[:, c * TB:(c + 1) * TB], acc[:], 0.25), deps=smdep)
                    sm_toks.append(smt)
                evs = []
                for o in range(8):
                    sl = o % 2
                    dep = [wtoks[sl]] if wtoks[sl] else []
                    ltoks[sl] = S.dma("Y", f"{tag}_wld{sl}", wsl[sl][:],
                                      w["wm2"][o * 128:(o + 1) * 128, :], deps=dep)
                    pdeps = [ltoks[sl]] + sm_toks
                    if len(evs) >= 2:
                        pdeps.append(evs[-2])
                    def mmg(t, o=o, sl=sl):
                        for k in range(8):
                            ins = t.matmul(ps[sl][:], wsl[sl][:, k * 128:(k + 1) * 128],
                                           sm16[:, k * TB:(k + 1) * TB], start=(k == 0), stop=(k == 7))
                        return ins
                    te = S.op("T", mmg, deps=pdeps, name=f"{tag}_mm{b}_{o}")
                    wtoks[sl] = te
                    ev = S.op("S", lambda s, o=o, sl=sl: s.activation(
                        out=o16[:], in_=ps[sl][:], func=AF.Identity,
                        bias=self.col(cols["bm2"][0], o)),
                        deps=[te] + ([prev_ost] if prev_ost else []))
                    st = S.dma("Y", f"{tag}_lst", Lf[o * 128:(o + 1) * 128, tslice], o16[:],
                               deps=[ev])
                    prev_ost = st
                    evs.append(ev)
                prev_sm_mm = te
                prev_rd = te
            S.barrier()



def build_program(seq=4096, tb=512, debug_outs=()):
    mk = MK(seq, tb)
    mk.debug_outs = list(debug_outs)
    S = mk.S
    x_in = mk.din("x", [seq, D], F32)
    out = mk.dout("out", [seq, D], F32)
    xA = mk.dscratch("xA", [D, seq], F32, debug=True)
    xB = mk.dscratch("xB", [D, seq], F32, debug=True)

    mk.emit_consts()
    S.barrier()
    return mk, x_in, out, xA, xB


def build_l0ffn_weights(mk, I):
    wbf = {}
    w1T = I["L0_ffn_w1_w"].T
    w2T = I["L0_ffn_w2_w"].T
    h1T = I["L0_ffn_h1_w"].T
    h2T = I["L0_ffn_h2_w"].T
    w3T = I["L0_ffn_w3_w"].T
    for name, w in (("w1", w1T), ("w2", w2T), ("h1", h1T), ("h2", h2T), ("w3", w3T)):
        wbf[name], _ = mk.precast("l0f_" + name, prechunk(np.asarray(w, np.float32)))
    scol = {}
    scol["s1"] = mk.load_cols("s1", colsplit(np.asarray(I["L0_ffn_w1_s"], np.float32)))
    scol["s2"] = mk.load_cols("s2", colsplit(np.asarray(I["L0_ffn_w2_s"], np.float32)))
    scol["h1s"] = mk.load_cols("h1s", colsplit(np.asarray(I["L0_ffn_h1_s"], np.float32)))
    scol["h2s"] = mk.load_cols("h2s", colsplit(np.asarray(I["L0_ffn_h2_s"], np.float32)))
    c0, _ = mk.load_cols("w3sh", colsplit(np.asarray(I["L0_ffn_w3_s"], np.float32)))
    mk.S.barrier()
    mk.S.op("V", lambda v: v.tensor_scalar_mul(
        mk.colsb[:, c0:c0 + 8], mk.colsb[:, c0:c0 + 8], 0.5))
    scol["w3sh"] = (c0, None)
    return wbf, scol


def rope_tables(seq):
    inv = 10000.0 ** (-np.arange(32, dtype=np.float64) / 32.0)
    fr = np.arange(seq, dtype=np.float64)[:, None] * inv[None, :]
    c = np.cos(fr); s = np.sin(fr)
    C64 = np.concatenate([c, c], 1)
    S64 = np.concatenate([-s, s], 1)
    CT = np.tile(C64, (1, 16))
    ST = np.tile(S64, (1, 16))
    f = lambda a: np.ascontiguousarray(a.astype(NBF))
    return f(CT / 8.0), f(ST / 8.0), f(CT), f(ST)


def causal_masks():
    i = np.arange(128)[:, None]
    j = np.arange(512)[None, :]
    mq = np.concatenate([np.where(j <= i + o * 128, 0.0, -1e30).astype(np.float32)
                         for o in range(4)], 0)
    mt = np.concatenate([np.where(d * 128 + i <= j, 0.0, -1e30).astype(np.float32)
                         for d in range(4)], 0)
    return np.ascontiguousarray(mq), np.ascontiguousarray(mt)


def build_l0attn_weights(mk, I):
    w = {}
    w["wq"], _ = mk.precast("l0a_wq", np.asarray(I["L0_attn_w_q"], np.float32))
    w["wk"], _ = mk.precast("l0a_wk", np.asarray(I["L0_attn_w_k"], np.float32))
    w["wv"], _ = mk.precast("l0a_wv", np.asarray(I["L0_attn_w_v"], np.float32))
    w["wo"], _ = mk.precast("l0a_wo", prechunk(np.asarray(I["L0_attn_w_o"], np.float32)))
    cq, sq, ck, sk = rope_tables(mk.seq)
    for nm, a in (("cq", cq), ("sq", sq), ("ck", ck), ("sk", sk)):
        ap = mk.din("c_" + nm, list(a.shape), BF16)
        mk.inputs["c_" + nm] = a
        w[nm] = ap
    return w


def build_l1ffn_weights(mk, I):
    w = {}
    w["attn_w"], _ = mk.precast("l1f_aw", np.asarray(I["L1_ffn_attn_w"], np.float32))
    w["proj"], _ = mk.precast("l1f_pw", prechunk(np.asarray(I["L1_ffn_proj_w"], np.float32)))
    w["router"], _ = mk.precast("l1f_rt", np.asarray(I["L1_ffn_router_w"], np.float32))
    w["wv"] = []
    w["pj"] = []
    for e in range(4):
        a, _ = mk.precast(f"l1f_wv{e}", prechunk(np.asarray(I["L1_ffn_expert_wv"][e], np.float32)))
        w["wv"].append(a)
        a, _ = mk.precast(f"l1f_pj{e}", prechunk(np.asarray(I["L1_ffn_expert_proj"][e], np.float32)))
        w["pj"].append(a)
    cols = {}
    cols["ln1w"] = mk.load_cols("l1f_ln1w", colsplit(np.asarray(I["L1_ffn_ln1_w"], np.float32)))
    cols["ln1b"] = mk.load_cols("l1f_ln1b", colsplit(np.asarray(I["L1_ffn_ln1_b"], np.float32)))
    cols["ln2w"] = mk.load_cols("l1f_ln2w", colsplit(np.asarray(I["L1_ffn_ln2_w"], np.float32)))
    cols["ln2b"] = mk.load_cols("l1f_ln2b", colsplit(np.asarray(I["L1_ffn_ln2_b"], np.float32)))
    cw = np.asarray(I["L1_ffn_conv_w"], np.float32)
    for i in range(3):
        cols[f"cw{i}"] = mk.load_cols(f"l1f_cw{i}", colsplit(np.ascontiguousarray(cw[:, 0, i])))
    mq, mt = causal_masks()
    for nm, a in (("maskq", mq), ("maskt", mt)):
        ap = mk.din("c_" + nm, list(a.shape), F32)
        mk.inputs["c_" + nm] = a
        w[nm] = ap
    return w, cols


def emit_l1ffn(mk, xsrc, xdst, w1f, c1f, wa, Qf, Kf, Vt, Of, Tf, Zf, tag):
    cfg = {"mode": "l1f", "kv": 16, "wqkv": w1f["attn_w"],
           "tabs": (wa["cq"], wa["sq"], wa["ck"], wa["sk"]),
           "ln1w": c1f["ln1w"], "ln1b": c1f["ln1b"],
           "cw0": c1f["cw0"], "cw1": c1f["cw1"], "cw2": c1f["cw2"], "tdst": Tf}
    mk.emit_qkv(xsrc, Qf, Kf, Vt, cfg, tag + "q")
    mk.emit_attn_core(Qf, Kf, Vt, Of, causal=True, KV=16, tag=tag + "c",
                      maskq=w1f["maskq"], maskt=w1f["maskt"])
    mk.emit_proj_resid([(Of, 8)], w1f["proj"], Tf, Zf, tag + "p")
    mk.emit_moe(Zf, xsrc, xdst, {"router": w1f["router"], "wv": w1f["wv"], "pj": w1f["pj"]},
                c1f, tag + "m")


def build_l1attn_weights(mk, I):
    w = {}
    w["wq"], _ = mk.precast("l1a_wq", prechunk(np.asarray(I["L1_attn_Wq"], np.float32)))
    w["wk"], _ = mk.precast("l1a_wk", prechunk(np.asarray(I["L1_attn_Wk"], np.float32)))
    wm1 = np.asarray(I["L1_attn_Wm1"], np.float32)
    w["wm1a"], _ = mk.precast("l1a_m1a", prechunk(np.ascontiguousarray(wm1[0:1024])))
    w["wm1b"], _ = mk.precast("l1a_m1b", prechunk(np.ascontiguousarray(wm1[1024:2048])))
    w["wm2"], _ = mk.precast("l1a_m2", prechunk(np.asarray(I["L1_attn_Wm2"], np.float32)))
    w["wo"], _ = mk.precast("l1a_wo", prechunk(np.asarray(I["L1_attn_Wo"], np.float32)))
    w["wv_pl"], _ = mk.precast("l1a_wv", np.asarray(I["L1_attn_Wv"], np.float32))
    bv = np.ascontiguousarray(np.asarray(I["L1_attn_bv"], np.float32).reshape(1, 1024))
    w["bv_row"] = mk.din("l1a_bv", [1, 1024], F32)
    mk.inputs["l1a_bv"] = bv
    cols = {}
    c0, _ = mk.load_cols("l1a_bq32", colsplit(np.asarray(I["L1_attn_bq"], np.float32)))
    mk.S.barrier()
    mk.S.op("V", lambda v: v.tensor_scalar_mul(
        mk.colsb[:, c0:c0 + 8], mk.colsb[:, c0:c0 + 8], 1.0 / 32))
    cols["bq32"] = (c0, None)
    cols["bk"] = mk.load_cols("l1a_bk", colsplit(np.asarray(I["L1_attn_bk"], np.float32)))
    cols["bm1"] = mk.load_cols("l1a_bm1", colsplit(np.asarray(I["L1_attn_bm1"], np.float32)))
    cols["bm2"] = mk.load_cols("l1a_bm2", colsplit(np.asarray(I["L1_attn_bm2"], np.float32)))
    cols["bo"] = mk.load_cols("l1a_bo", colsplit(np.asarray(I["L1_attn_bo"], np.float32)))
    mk.fconst(1.0 / 32)
    return w, cols


def build_all(inputs, seq=4096, tb=512, debug_outs=(), upto=99):
    mk, x_in, out, xA, xB = build_program(seq=seq, tb=tb, debug_outs=debug_outs)
    wbf, scol = build_l0ffn_weights(mk, inputs)
    wa = build_l0attn_weights(mk, inputs)
    w1f, c1f = (None, None)
    if upto >= 4:
        w1f, c1f = build_l1ffn_weights(mk, inputs)
    ntile = seq // 128
    Qf = mk.dscratch("Qf", [D, seq], BF16, debug=True)
    Kf = mk.dscratch("Kf", [D, seq], BF16, debug=True)
    Vt = mk.dscratch("Vt", [16 * ntile * 128, 65], BF16, debug=True)
    Of = mk.dscratch("Of", [D, seq], BF16, debug=True)
    Tf = mk.dscratch("Tf", [D, seq], F32, debug=True)
    Zf = mk.dscratch("Zf", [D, seq], F32, debug=True)
    S = mk.S
    S.barrier()
    mk.emit_prologue(x_in, xA)
    mk.emit_ffn0(xA, xB, wbf, scol, "f0a")
    final = xB
    if upto >= 2:
        cfg = {"mode": "l0", "kv": 4, "wq": wa["wq"], "wk": wa["wk"], "wv": wa["wv"],
               "tabs": (wa["cq"], wa["sq"], wa["ck"], wa["sk"])}
        mk.emit_qkv(xB, Qf, Kf, Vt, cfg, "qk0")
        mk.emit_attn_core(Qf, Kf, Vt, Of, causal=False, KV=4, tag="ac0")
        mk.emit_proj_resid([(Of, 8)], wa["wo"], xB, xA, "pr0")
        final = xA
    if upto >= 3:
        mk.emit_ffn0(xA, xB, wbf, scol, "f0b")
        final = xB
    if upto >= 4:
        emit_l1ffn(mk, xB, xA, w1f, c1f, wa, Qf, Kf, Vt, Of, Tf, Zf, "l1a")
        final = xA
    if upto >= 5:
        w1a, c1a = build_l1attn_weights(mk, inputs)
        Af = mk.dscratch("Af", [D, seq], BF16, debug=True)
        Bf = mk.dscratch("Bf", [D, seq], BF16, debug=True)
        Vtok = mk.dscratch("Vtok", [seq, D], BF16, debug=True)
        Gf = mk.dscratch("Gf", [D, seq], BF16, debug=True)
        Lf = mk.dscratch("Lf", [D, seq], BF16, debug=True)
        S.barrier()
        mk.emit_l1a_proj(xA, Qf, Kf, Af, Bf, Vtok, w1a, c1a, "laP")
        mk.emit_l1a_core(Qf, Kf, Vtok, Gf, {"maskq": w1f["maskq"]}, "laC")
        mk.emit_l1a_local(Af, Bf, Lf, w1a, c1a, "laL")
        mk.emit_proj_resid([(Lf, 8), (Gf, 8)], w1a["wo"], xA, xB, "laF",
                           bias_col=c1a["bo"])
        final = xB
    if upto >= 6:
        emit_l1ffn(mk, xB, xA, w1f, c1f, wa, Qf, Kf, Vt, Of, Tf, Zf, "l1b")
        final = xA
    mk.emit_epilogue(final, out)
    S.finish()
    return mk


def kernel(**inputs):
    seq = inputs["L0_attn_x"].shape[1]
    mk = build_all(inputs, seq=seq, tb=512)
    x = np.ascontiguousarray(np.asarray(inputs["L0_attn_x"], np.float32))
    in_maps = []
    for c in range(8):
        m = dict(mk.inputs)
        m["x"] = np.ascontiguousarray(x[c])
        in_maps.append(m)
    res = run_bass_kernel_spmd(mk.nc, in_maps, core_ids=list(range(8)))
    return np.stack([r["out"] for r in res.results], axis=0)


# version 14
# speedup vs baseline: 1.1892x; 1.0193x over previous
import contextlib
import numpy as np
import ml_dtypes
import concourse.bass as bass
import concourse.mybir as mybir
from concourse.bass_utils import run_bass_kernel_spmd

F32 = mybir.dt.float32
BF16 = mybir.dt.bfloat16
AF = mybir.ActivationFunctionType
ALU = mybir.AluOpType
AX = mybir.AxisListType
NBF = ml_dtypes.bfloat16

D = 1024
DF = 4096
EPS = 1e-5


class Sched:
    ENGS = ("Y", "G", "T", "S", "V")

    def __init__(self, nc, es):
        self.nc = nc
        self.esem = {}
        self.ecnt = {e: 0 for e in self.ENGS}
        self.streams = {}
        self.ops = []
        self.hw = {e: {} for e in self.ENGS}
        self.es = es
        for e in self.ENGS:
            self.esem[e] = es.enter_context(nc.semaphore(f"p{e}"))
        self.phase_deps = ()

    def stream(self, name, cls):
        if not hasattr(self, "_spool"):
            self._spool = {"H": [], "G": []}
            self._sems = []
            self._scnt = []
        if name in self.streams:
            idx = self.streams[name]
            assert self._sems[idx][1] == cls, f"stream {name} class mismatch"
            return idx
        if self._spool[cls]:
            idx = self._spool[cls].pop()
        else:
            idx = len(self._sems)
            sem = self.es.enter_context(self.nc.semaphore(f"dma{idx}"))
            self._sems.append((sem, cls))
            self._scnt.append(0)
        self.streams[name] = idx
        return idx

    def op(self, e, fn, deps=(), name=""):
        self.ecnt[e] += 1
        tok = (e, self.ecnt[e])
        self.ops.append((e, fn, tuple(deps) + tuple(self.phase_deps), tok, name))
        return tok

    def dma(self, e, stream, out, in_, deps=(), name="", **kw):
        cls = "G" if e == "G" else "H"
        idx = self.stream(stream, cls)
        self._scnt[idx] += 16
        tok = (("d", idx), self._scnt[idx])
        def fn(eng, out=out, in_=in_, kw=kw):
            return eng.dma_start(out=out, in_=in_, **kw)
        self.ops.append((e, fn, tuple(deps) + tuple(self.phase_deps), tok,
                         name or f"dma:{stream}"))
        return tok

    def tok_now(self):
        toks = [(e, self.ecnt[e]) for e in self.ENGS if self.ecnt[e] > 0]
        if hasattr(self, "_scnt"):
            toks += [(("d", i), c) for i, c in enumerate(self._scnt) if c > 0]
        return tuple(toks)

    def _sem_of(self, key):
        if key in self.esem:
            return self.esem[key]
        return self._sems[key[1]][0]

    def flush(self):
        nc = self.nc
        ops = self.ops
        self.ops = []
        if not ops:
            return
        with nc.Block() as block:
            meths = {"Y": block.sync, "G": block.gpsimd, "T": block.tensor,
                     "S": block.scalar, "V": block.vector}
            for ekey in self.ENGS:
                eops = [o for o in ops if o[0] == ekey]
                if not eops:
                    continue
                def body(eng, eops=eops, ekey=ekey):
                    hw = self.hw[ekey]
                    for (_, fn, deps, tok, name) in eops:
                        for (dk, dv) in deps:
                            if dk == ekey and tok[0] == ekey and dv >= tok[1]:
                                raise RuntimeError(f"self-dep future {name}")
                            if hw.get(dk, 0) < dv:
                                eng.wait_ge(self._sem_of(dk), dv)
                                hw[dk] = dv
                        ins = fn(eng)
                        if tok[0] in self.esem:
                            ins.then_inc(self.esem[ekey], 1)
                        else:
                            ins.then_inc(self._sem_of(tok[0]), 16)
                meths[ekey](body)

    def barrier(self):
        self.flush()
        self.phase_deps = self.tok_now()
        if hasattr(self, "_spool"):
            for name, idx in self.streams.items():
                self._spool[self._sems[idx][1]].append(idx)
            self.streams = {}

    def finish(self, extra_deps=()):
        self.flush()
        toks = self.tok_now() + tuple(extra_deps)
        nc = self.nc
        with nc.Block() as block:
            def body(eng):
                for (dk, dv) in toks:
                    eng.wait_ge(self._sem_of(dk), dv)
            block.sync(body)



def prechunk(w_lhsT):
    d_in, d_out = w_lhsT.shape
    KC, OC = d_in // 128, d_out // 128
    w = w_lhsT.reshape(KC, 128, OC, 128).transpose(2, 1, 0, 3).reshape(OC * 128, KC * 128)
    return np.ascontiguousarray(w)

def colsplit(vec):
    return np.ascontiguousarray(vec.reshape(-1, 128))



class MK:
    def __init__(self, seq=4096, tb=512):
        self.seq = seq
        self.tb = tb
        self.nb = seq // tb
        nc = bass.Bass("TRN2", target_bir_lowering=False)
        self.nc = nc
        self.es = contextlib.ExitStack()
        self.S = Sched(nc, self.es)
        self.inputs = {}
        self.dram = {}
        self.debug_outs = []

    def din(self, name, shape, dtype=F32):
        ap = self.nc.dram_tensor(name, list(shape), dtype, kind="ExternalInput").ap()
        self.dram[name] = ap
        return ap

    def dscratch(self, name, shape, dtype=F32, debug=False):
        kind = "ExternalOutput" if (debug and name in self.debug_outs) else "Internal"
        ap = self.nc.dram_tensor(name, list(shape), dtype, kind=kind).ap()
        self.dram[name] = ap
        return ap

    def dout(self, name, shape, dtype=F32):
        ap = self.nc.dram_tensor(name, list(shape), dtype, kind="ExternalOutput").ap()
        self.dram[name] = ap
        return ap

    def emit_consts(self):
        nc, S = self.nc, self.S
        es = self.es
        self.ones_col_bf = es.enter_context(nc.sbuf_tensor("ones_col_bf", [128, 1], BF16))
        self.ones_row_f = es.enter_context(nc.sbuf_tensor("ones_row_f", [1, 128], F32))
        self.ones_row_bf = es.enter_context(nc.sbuf_tensor("ones_row_bf", [1, 128], BF16))
        self.idn_bf = es.enter_context(nc.sbuf_tensor("idn_bf", [128, 128], BF16))
        self.idn_f = es.enter_context(nc.sbuf_tensor("idn_f", [128, 128], F32))
        idn_host = self.din("c_idn", [128, 128], F32)
        S.op("V", lambda v: v.memset(self.ones_col_bf[:], 1.0))
        S.op("V", lambda v: v.memset(self.ones_row_f[:], 1.0))
        S.op("V", lambda v: v.memset(self.ones_row_bf[:], 1.0))
        t1 = S.dma("Y", "consts", self.idn_f[:], idn_host[:])
        t2 = S.dma("G", "gconsts", self.idn_bf[:], idn_host[:])
        self.inputs["c_idn"] = np.eye(128, dtype=np.float32)
        self.colsb = es.enter_context(nc.sbuf_tensor("colsb", [128, 256], F32))
        self.ncols = 0
        self.fcsb = es.enter_context(nc.sbuf_tensor("fcsb", [128, 16], F32))
        self.nfc = 0
        self.fcmap = {}
        for v in (EPS,):
            self.fconst(v)

    def fconst(self, val):
        if val in self.fcmap:
            return self.fcmap[val]
        i = self.nfc
        assert i < 16
        self.nfc += 1
        ap = self.fcsb[:, i:i + 1]
        self.S.op("V", lambda v, ap=ap, val=val: v.memset(ap, float(val)))
        self.nc.const_aps.register(F32, float(val), ap)
        self.fcmap[val] = ap
        return ap

    def load_cols(self, name, host_oc128):
        OC = host_oc128.shape[0]
        assert host_oc128.shape == (OC, 128)
        iname = "c_" + name
        ap = self.din(iname, [OC, 128], F32)
        self.inputs[iname] = np.ascontiguousarray(host_oc128.astype(np.float32))
        c0 = self.ncols
        assert c0 + OC <= 256
        self.ncols += OC
        for o in range(OC):
            self.S.dma("Y", "consts", self.colsb[:, c0 + o:c0 + o + 1], ap[o:o + 1, :])
        return c0, None

    def col(self, c0, o):
        return self.colsb[:, c0 + o:c0 + o + 1]

    def precast(self, name, host_arr, chunk_elems=4 << 20):
        arr = np.ascontiguousarray(host_arr.astype(np.float32))
        shp = list(arr.shape)
        ap_in = self.din(name, shp, F32)
        self.inputs[name] = arr
        ap_bf = self.dscratch("b_" + name, shp, BF16, debug=True)
        n = int(np.prod(shp))
        flat_in = ap_in.rearrange("a b -> (a b)") if len(shp) == 2 else ap_in
        flat_out = ap_bf.rearrange("a b -> (a b)") if len(shp) == 2 else ap_bf
        toks = []
        for i in range(0, n, chunk_elems):
            j = min(n, i + chunk_elems)
            toks.append(self.S.dma("G", "cast_" + name, flat_out[i:j], flat_in[i:j]))
        return ap_bf, toks

    def emit_prologue(self, x_in, xA):
        S = self.S
        nc = self.nc
        seq = self.seq
        with contextlib.ExitStack() as ph:
            xt = [ph.enter_context(nc.sbuf_tensor(f"pro_xt{i}", [128, D], F32)) for i in range(2)]
            ot = [ph.enter_context(nc.sbuf_tensor(f"pro_ot{i}", [128, D], F32)) for i in range(2)]
            pst = [ph.enter_context(nc.psum_tensor(f"pro_ps{i}", [128, 512], F32)) for i in range(2)]
            last_evac = [None, None]
            last_store = [None, None]
            for i in range(seq // 128):
                sl = i % 2
                deps = [last_evac[sl]] if last_evac[sl] else []
                tl = S.dma("Y", f"pro_x{sl}", xt[sl][:], x_in[i * 128:(i + 1) * 128, :], deps=deps)
                th = []
                ev0 = None
                for h in range(2):
                    def tr(t, sl=sl, h=h):
                        ins = None
                        for c in range(4):
                            cc = h * 4 + c
                            ins = t.transpose(pst[sl][:, c * 128:(c + 1) * 128],
                                              xt[sl][:, cc * 128:(cc + 1) * 128], self.idn_f[:])
                        return ins
                    pdeps = [tl] + ([last_evac[sl]] if last_evac[sl] else [])
                    if h == 1:
                        pdeps.append(ev0)
                    ttok = S.op("T", tr, deps=pdeps, name=f"pro_tr{i}_{h}")
                    def evac(v, sl=sl, h=h):
                        return v.tensor_copy(ot[sl][:, h * 512:(h + 1) * 512], pst[sl][:])
                    sdeps = [ttok] + ([last_store[sl]] if last_store[sl] else [])
                    evtok = S.op("V", evac, deps=sdeps, name=f"pro_ev{i}_{h}")
                    if h == 0:
                        ev0 = evtok
                    th.append(evtok)
                last_evac[sl] = th[1]
                sttoks = []
                for c in range(8):
                    sttoks.append(S.dma("Y", f"pro_o{sl}",
                                        xA[c * 128:(c + 1) * 128, i * 128:(i + 1) * 128],
                                        ot[sl][:, c * 128:(c + 1) * 128], deps=[th[c // 4]]))
                last_store[sl] = sttoks[-1]
            S.barrier()

    def emit_epilogue(self, xN, out):
        S = self.S
        nc = self.nc
        seq = self.seq
        with contextlib.ExitStack() as ph:
            it = [ph.enter_context(nc.sbuf_tensor(f"epi_it{i}", [128, D], F32)) for i in range(2)]
            ot = [ph.enter_context(nc.sbuf_tensor(f"epi_ot{i}", [128, D], F32)) for i in range(2)]
            pst = [ph.enter_context(nc.psum_tensor(f"epi_ps{i}", [128, 512], F32)) for i in range(2)]
            last_evac = [None, None]
            last_store = [None, None]
            for i in range(seq // 128):
                sl = i % 2
                deps = [last_evac[sl]] if last_evac[sl] else []
                ltoks = []
                for c in range(8):
                    ltoks.append(S.dma("Y", f"epi_x{sl}", it[sl][:, c * 128:(c + 1) * 128],
                                       xN[c * 128:(c + 1) * 128, i * 128:(i + 1) * 128], deps=deps))
                tl = ltoks[-1]
                ev0 = None
                th = []
                for h in range(2):
                    def tr(t, sl=sl, h=h):
                        ins = None
                        for c in range(4):
                            cc = h * 4 + c
                            ins = t.transpose(pst[sl][:, c * 128:(c + 1) * 128],
                                              it[sl][:, cc * 128:(cc + 1) * 128], self.idn_f[:])
                        return ins
                    pdeps = [tl] + ([last_evac[sl]] if last_evac[sl] else [])
                    if h == 1:
                        pdeps.append(ev0)
                    ttok = S.op("T", tr, deps=pdeps, name=f"epi_tr{i}_{h}")
                    def evac(v, sl=sl, h=h):
                        return v.tensor_copy(ot[sl][:, (h * 4) * 128:(h * 4 + 4) * 128], pst[sl][:])
                    sdeps = [ttok] + ([last_store[sl]] if last_store[sl] else [])
                    evtok = S.op("V", evac, deps=sdeps, name=f"epi_ev{i}_{h}")
                    if h == 0:
                        ev0 = evtok
                    th.append(evtok)
                last_evac[sl] = th[1]
                last_store[sl] = S.dma("Y", f"epi_o{sl}", out[i * 128:(i + 1) * 128, :],
                                       ot[sl][:], deps=th)
            S.barrier()

    def emit_ffn0(self, xsrc, xdst, wbf, scol, tag):
        S, nc = self.S, self.nc
        TB = self.tb
        with contextlib.ExitStack() as ph:
            xsb = ph.enter_context(nc.sbuf_tensor(f"{tag}_x", [128, 8 * TB], F32))
            xa = [ph.enter_context(nc.sbuf_tensor(f"{tag}_xa{i}", [128, TB], F32)) for i in range(2)]
            x16 = ph.enter_context(nc.sbuf_tensor(f"{tag}_x16", [128, 8 * TB], BF16))
            sq16 = ph.enter_context(nc.sbuf_tensor(f"{tag}_sq", [128, 8 * TB], BF16))
            t16 = ph.enter_context(nc.sbuf_tensor(f"{tag}_t16", [128, 8 * TB], BF16))
            u16 = ph.enter_context(nc.sbuf_tensor(f"{tag}_u16", [128, 32 * TB], BF16))
            v16 = ph.enter_context(nc.sbuf_tensor(f"{tag}_v16", [128, 32 * TB], BF16))
            sA = [ph.enter_context(nc.sbuf_tensor(f"{tag}_sA{i}", [128, TB], BF16)) for i in range(2)]
            stat = ph.enter_context(nc.sbuf_tensor(f"{tag}_st", [1, 6 * TB], F32))
            wsl = {}
            for wname, wid in (("w1", 8), ("w2", 8), ("h1", 32), ("h2", 32), ("w3", 32)):
                wsl[wname] = [ph.enter_context(
                    nc.sbuf_tensor(f"{tag}_{wname}s{i}", [128, wid * 128], BF16)) for i in range(2)]
            ps_sum = ph.enter_context(nc.psum_tensor(f"{tag}_psum", [1, TB], F32))
            ps_sq = ph.enter_context(nc.psum_tensor(f"{tag}_psq", [1, TB], F32))
            ps_A = ph.enter_context(nc.psum_tensor(f"{tag}_psA", [128, TB], F32))
            ps_C = ph.enter_context(nc.psum_tensor(f"{tag}_psC", [128, TB], F32))
            ps = [ph.enter_context(nc.psum_tensor(f"{tag}_ps{i}", [128, TB], F32)) for i in range(2)]

            wtoks = {k: [None, None] for k in wsl}
            ltoks = {k: [None, None] for k in wsl}
            prev_x_readers = None
            prev_u16_mm = None
            prev_statreaders = []
            prev_norm = None
            prev_psAC = None
            prev_t16_mm = None
            prev_xa = [None, None]

            def loadw(wname, o, OC):
                sl = o % 2
                dep = [wtoks[wname][sl]] if wtoks[wname][sl] else []
                ltoks[wname][sl] = S.dma("Y", f"{tag}_{wname}{sl}", wsl[wname][sl][:],
                                         wbf[wname][o * 128:(o + 1) * 128, :], deps=dep)

            for b in range(self.nb):
                tsl = slice(b * TB, (b + 1) * TB)
                xdep = list(prev_x_readers) if prev_x_readers else []
                xts = [S.dma("Y", f"{tag}_xin", xsb[:, c * TB:(c + 1) * TB],
                             xsrc[c * 128:(c + 1) * 128, tsl], deps=xdep) for c in range(8)]
                xtok = xts[-1]
                cast_toks = []
                sq_toks = []
                for c in range(8):
                    dep = [xtok] + ([prev_norm[c]] if prev_norm else [])
                    cast_toks.append(S.op("V", lambda v, c=c: v.tensor_copy(
                        x16[:, c * TB:(c + 1) * TB], xsb[:, c * TB:(c + 1) * TB]), deps=dep))
                    sq_toks.append(S.op("S", lambda s, c=c: s.activation(
                        out=sq16[:, c * TB:(c + 1) * TB], in_=xsb[:, c * TB:(c + 1) * TB],
                        func=AF.Square), deps=dep))
                def mmsum(t):
                    ins = None
                    for c in range(8):
                        ins = t.matmul(ps_sum[:], self.ones_col_bf[:], x16[:, c * TB:(c + 1) * TB],
                                       start=(c == 0), stop=(c == 7))
                    return ins
                def mmsq(t):
                    ins = None
                    for c in range(8):
                        ins = t.matmul(ps_sq[:], self.ones_col_bf[:], sq16[:, c * TB:(c + 1) * TB],
                                       start=(c == 0), stop=(c == 7))
                    return ins
                sdep = cast_toks + sq_toks
                tsum = S.op("T", mmsum, deps=sdep, name=f"{tag}_lnsum{b}")
                tsq = S.op("T", mmsq, deps=sdep, name=f"{tag}_lnsq{b}")
                mu = stat[:, 0:TB]; ex2 = stat[:, TB:2 * TB]; var = stat[:, 2 * TB:3 * TB]
                std = stat[:, 3 * TB:4 * TB]; rinv = stat[:, 4 * TB:5 * TB]; nmr = stat[:, 5 * TB:6 * TB]
                v1 = S.op("V", lambda v: v.tensor_scalar_mul(mu, ps_sum[:], 1.0 / D),
                          deps=[tsum] + prev_statreaders)
                v2 = S.op("V", lambda v: v.tensor_scalar_mul(ex2, ps_sq[:], 1.0 / D), deps=[tsq])
                v3 = S.op("V", lambda v: v.scalar_tensor_tensor(
                    var, mu, 1.0, mu, op0=ALU.mult, op1=ALU.mult), deps=[v1])
                v4 = S.op("V", lambda v: v.tensor_sub(var, ex2, var), deps=[v2, v3])
                s5 = S.op("S", lambda s: s.activation(out=std, in_=var, func=AF.Sqrt, bias=EPS), deps=[v4])
                v6 = S.op("V", lambda v: v.reciprocal(rinv, std), deps=[s5])
                v7 = S.op("V", lambda v: v.scalar_tensor_tensor(
                    nmr, mu, -1.0, rinv, op0=ALU.mult, op1=ALU.mult), deps=[v6, v1])
                pdep = [prev_psAC] if prev_psAC else []
                tA = S.op("T", lambda t: t.matmul(ps_A[:], self.ones_row_f[:], rinv,
                                                  start=True, stop=True), deps=[v6] + pdep)
                tC = S.op("T", lambda t: t.matmul(ps_C[:], self.ones_row_f[:], nmr,
                                                  start=True, stop=True), deps=[v7] + pdep)
                norm_toks = []
                for c in range(8):
                    sl2 = c % 2
                    d1 = [tA, xtok] + ([prev_xa[sl2]] if prev_xa[sl2] else [])
                    n1 = S.op("V", lambda v, c=c: v.tensor_mul(
                        xa[c % 2][:], xsb[:, c * TB:(c + 1) * TB], ps_A[:]), deps=d1)
                    d2 = [tC, n1]
                    if prev_t16_mm is not None:
                        d2.append(prev_t16_mm)
                    n2 = S.op("V", lambda v, c=c: v.tensor_tensor(
                        t16[:, c * TB:(c + 1) * TB], xa[c % 2][:], ps_C[:],
                        op=ALU.add), deps=d2)
                    prev_xa[sl2] = n2
                    norm_toks.append(n2)
                prev_statreaders = [tA, tC]
                prev_norm = norm_toks
                prev_psAC = norm_toks[-1]
                xrtok = xtok

                evB = []
                mmB = []
                for o in range(32):
                    if b == 0 and o < 2:
                        loadw("w1", o, 32); loadw("w2", o, 32)
                    psl = o % 2
                    pdeps = [ltoks["w1"][o % 2]] + norm_toks
                    if len(evB) >= 2:
                        pdeps.append(evB[-2][1])
                    def mm1(t, o=o):
                        ins = None
                        sl = wsl["w1"][o % 2]
                        for k in range(8):
                            ins = t.matmul(ps[o % 2][:], sl[:, k * 128:(k + 1) * 128],
                                           t16[:, k * TB:(k + 1) * TB], start=(k == 0), stop=(k == 7))
                        return ins
                    ta = S.op("T", mm1, deps=pdeps, name=f"{tag}_B1_{b}_{o}")
                    sa = S.op("S", lambda s, o=o: s.activation(
                        out=sA[o % 2][:], in_=ps[o % 2][:], func=AF.Silu,
                        scale=self.col(scol["s1"][0], o)),
                        deps=[ta] + ([evB[-2][1]] if len(evB) >= 2 else []), name=f"{tag}_Bs_{b}_{o}")
                    def mm2(t, o=o):
                        ins = None
                        sl = wsl["w2"][o % 2]
                        for k in range(8):
                            ins = t.matmul(ps[o % 2][:], sl[:, k * 128:(k + 1) * 128],
                                           t16[:, k * TB:(k + 1) * TB], start=(k == 0), stop=(k == 7))
                        return ins
                    tb_ = S.op("T", mm2, deps=[sa, ltoks["w2"][o % 2]], name=f"{tag}_B2_{b}_{o}")
                    wtoks["w1"][o % 2] = ta
                    wtoks["w2"][o % 2] = tb_
                    if o + 2 < 32:
                        loadw("w1", o + 2, 32); loadw("w2", o + 2, 32)
                    elif b + 1 < self.nb and o + 2 < 34:
                        loadw("w1", o + 2 - 32, 32); loadw("w2", o + 2 - 32, 32)
                    udep = [tb_]
                    if b > 0 and o < 2:
                        udep.append(prev_u16_mm)
                    uev = S.op("V", lambda v, o=o: v.scalar_tensor_tensor(
                        u16[:, o * TB:(o + 1) * TB], ps[o % 2][:], self.col(scol["s2"][0], o),
                        sA[o % 2][:], op0=ALU.mult, op1=ALU.mult), deps=udep,
                        name=f"{tag}_Bu_{b}_{o}")
                    evB.append((ta, uev, sa))
                    mmB.append(tb_)
                evC = []
                lastB = [evB[-1][1], evB[-2][1]] + mmB[-2:]
                for o in range(32):
                    if o < 2:
                        sl = o % 2
                        dep = [wtoks["h1"][sl]] if wtoks["h1"][sl] else []
                        ltoks["h1"][sl] = S.dma("Y", f"{tag}_h1{sl}", wsl["h1"][sl][:],
                                                wbf["h1"][o * 128:(o + 1) * 128, :],
                                                deps=dep)
                    psl = o % 2
                    pdeps = [ltoks["h1"][psl]] + [x[1] for x in evB]
                    if len(evC) >= 2:
                        pdeps.append(evC[-2])
                    def mmc(t, o=o):
                        ins = None
                        sl = wsl["h1"][o % 2]
                        for k in range(32):
                            ins = t.matmul(ps[o % 2][:], sl[:, k * 128:(k + 1) * 128],
                                           u16[:, k * TB:(k + 1) * TB], start=(k == 0), stop=(k == 31))
                        return ins
                    tc = S.op("T", mmc, deps=pdeps, name=f"{tag}_C_{b}_{o}")
                    wtoks["h1"][psl] = tc
                    if o + 2 < 32:
                        sl = o % 2
                        ltoks["h1"][sl] = S.dma("Y", f"{tag}_h1{sl}", wsl["h1"][sl][:],
                                                wbf["h1"][(o + 2) * 128:(o + 3) * 128, :],
                                                deps=[tc])
                    ev = S.op("S", lambda s, o=o: s.activation(
                        out=v16[:, o * TB:(o + 1) * TB], in_=ps[o % 2][:], func=AF.Gelu,
                        scale=self.col(scol["h1s"][0], o)), deps=[tc], name=f"{tag}_Cg_{b}_{o}")
                    evC.append(ev)
                evD = []
                for o in range(32):
                    if o < 2:
                        sl = o % 2
                        dep = [wtoks["h2"][sl]] if wtoks["h2"][sl] else []
                        ltoks["h2"][sl] = S.dma("Y", f"{tag}_h2{sl}", wsl["h2"][sl][:],
                                                wbf["h2"][o * 128:(o + 1) * 128, :], deps=dep)
                    psl = o % 2
                    pdeps = [ltoks["h2"][psl]] + evC
                    pdeps += [x[1] for x in evB]
                    pdeps.append(evC[-1])
                    if len(evD) >= 2:
                        pdeps.append(evD[-2])
                    def mmd(t, o=o):
                        ins = None
                        sl = wsl["h2"][o % 2]
                        for k in range(32):
                            ins = t.matmul(ps[o % 2][:], sl[:, k * 128:(k + 1) * 128],
                                           v16[:, k * TB:(k + 1) * TB], start=(k == 0), stop=(k == 31))
                        return ins
                    td = S.op("T", mmd, deps=pdeps, name=f"{tag}_D_{b}_{o}")
                    wtoks["h2"][psl] = td
                    if o + 2 < 32:
                        sl = o % 2
                        ltoks["h2"][sl] = S.dma("Y", f"{tag}_h2{sl}", wsl["h2"][sl][:],
                                                wbf["h2"][(o + 2) * 128:(o + 3) * 128, :], deps=[td])
                    ev = S.op("S", lambda s, o=o: s.activation(
                        out=u16[:, o * TB:(o + 1) * TB], in_=ps[o % 2][:], func=AF.Gelu,
                        scale=self.col(scol["h2s"][0], o)), deps=[td], name=f"{tag}_Dg_{b}_{o}")
                    evD.append(ev)
                evE = []
                stE = []
                for o in range(8):
                    if o < 2:
                        sl = o % 2
                        dep = [wtoks["w3"][sl]] if wtoks["w3"][sl] else []
                        ltoks["w3"][sl] = S.dma("Y", f"{tag}_w3{sl}", wsl["w3"][sl][:],
                                                wbf["w3"][o * 128:(o + 1) * 128, :], deps=dep)
                    psl = o % 2
                    pdeps = [ltoks["w3"][psl]] + evD
                    if len(evE) >= 2:
                        pdeps.append(evE[-2])
                    def mme(t, o=o):
                        ins = None
                        sl = wsl["w3"][o % 2]
                        for k in range(32):
                            ins = t.matmul(ps[o % 2][:], sl[:, k * 128:(k + 1) * 128],
                                           u16[:, k * TB:(k + 1) * TB], start=(k == 0), stop=(k == 31))
                        return ins
                    te = S.op("T", mme, deps=pdeps, name=f"{tag}_E_{b}_{o}")
                    wtoks["w3"][psl] = te
                    if o + 2 < 8:
                        sl = o % 2
                        ltoks["w3"][sl] = S.dma("Y", f"{tag}_w3{sl}", wsl["w3"][sl][:],
                                                wbf["w3"][(o + 2) * 128:(o + 3) * 128, :], deps=[te])
                    ev = S.op("V", lambda v, o=o: v.scalar_tensor_tensor(
                        xsb[:, o * TB:(o + 1) * TB], ps[o % 2][:],
                        self.col(scol["w3sh"][0], o), xsb[:, o * TB:(o + 1) * TB],
                        op0=ALU.mult, op1=ALU.add),
                        deps=[te, xrtok, norm_toks[o], sq_toks[o], cast_toks[o]],
                        name=f"{tag}_Ee_{b}_{o}")
                    evE.append(ev)
                    stE.append(S.dma("Y", f"{tag}_xout", xdst[o * 128:(o + 1) * 128, tsl],
                                     xsb[:, o * TB:(o + 1) * TB], deps=[ev]))
                prev_x_readers = evE + [stE[-1]]
                prev_t16_mm = mmB[-1]
                prev_u16_mm = te
            S.barrier()


    def emit_qkv(self, xsrc, Qf, Kf, Vt, cfg, tag):
        S, nc = self.S, self.nc
        TB = self.tb
        mode = cfg["mode"]
        KV = cfg["kv"]
        ntile = self.seq // 128
        with contextlib.ExitStack() as ph:
            xsb = ph.enter_context(nc.sbuf_tensor(f"{tag}_x", [128, 8 * TB], F32))
            xa = [ph.enter_context(nc.sbuf_tensor(f"{tag}_xa{i}", [128, TB], F32)) for i in range(2)]
            x16 = ph.enter_context(nc.sbuf_tensor(f"{tag}_x16", [128, 8 * TB], BF16))
            sq16 = ph.enter_context(nc.sbuf_tensor(f"{tag}_sq", [128, 8 * TB], BF16))
            act16 = ph.enter_context(nc.sbuf_tensor(f"{tag}_a16", [128, 8 * TB], BF16))
            stat = ph.enter_context(nc.sbuf_tensor(f"{tag}_st", [1, 8 * TB], F32))
            wq_w = {"l0": 1024, "l1f": 3072}[mode]
            wk_w = {"l0": 256, "l1f": 0}[mode]
            wtot = wq_w + 2 * wk_w if mode == "l0" else 3072
            wsb = ph.enter_context(nc.sbuf_tensor(f"{tag}_w", [128, 8 * wtot], BF16))
            tabs = cfg["tabs"]
            tsl = [ph.enter_context(nc.sbuf_tensor(f"{tag}_tab{i}", [128, 4 * 1024], BF16))
                   for i in range(2)]
            rA = ph.enter_context(nc.sbuf_tensor(f"{tag}_rA", [128, 512], F32))
            rB = ph.enter_context(nc.sbuf_tensor(f"{tag}_rB", [128, 512], F32))
            qrot = ph.enter_context(nc.sbuf_tensor(f"{tag}_qrot", [128, 1024], BF16))
            krot = ph.enter_context(nc.sbuf_tensor(f"{tag}_krot", [128, 1024], BF16))
            qf_asm = ph.enter_context(nc.sbuf_tensor(f"{tag}_qasm", [128, 8 * 512], BF16))
            kf_asm = ph.enter_context(nc.sbuf_tensor(f"{tag}_kasm", [128, (2 if mode == "l0" else 8) * 512], BF16))
            vasm = ph.enter_context(nc.sbuf_tensor(f"{tag}_vasm", [128, KV * 65], BF16))
            if mode == "l1f":
                tsb = ph.enter_context(nc.sbuf_tensor(f"{tag}_t", [128, 8 * TB], F32))
                hb = ph.enter_context(nc.sbuf_tensor(f"{tag}_hb", [128, 8 * (TB + 2)], F32))
                ctmp = ph.enter_context(nc.sbuf_tensor(f"{tag}_ctmp", [128, TB], F32))
            ps_sum = ph.enter_context(nc.psum_tensor(f"{tag}_psum", [1, TB], F32))
            ps_sq = ph.enter_context(nc.psum_tensor(f"{tag}_psq", [1, TB], F32))
            ps_A = ph.enter_context(nc.psum_tensor(f"{tag}_psA", [128, TB], F32))
            ps_C = ph.enter_context(nc.psum_tensor(f"{tag}_psC", [128, TB], F32))
            ps = [ph.enter_context(nc.psum_tensor(f"{tag}_ps{i}", [128, 512], F32)) for i in range(2)]
            ps_tr = ph.enter_context(nc.psum_tensor(f"{tag}_ptr", [128, 512], BF16))

            if mode == "l0":
                for kc in range(8):
                    S.dma("Y", f"{tag}_wld", wsb[:, kc * wtot:kc * wtot + 1024],
                          cfg["wq"][kc * 128:(kc + 1) * 128, :])
                    S.dma("Y", f"{tag}_wld", wsb[:, kc * wtot + 1024:kc * wtot + 1280],
                          cfg["wk"][kc * 128:(kc + 1) * 128, :])
                    S.dma("Y", f"{tag}_wld", wsb[:, kc * wtot + 1280:kc * wtot + 1536],
                          cfg["wv"][kc * 128:(kc + 1) * 128, :])
            else:
                for kc in range(8):
                    S.dma("Y", f"{tag}_wld", wsb[:, kc * wtot:(kc + 1) * wtot],
                          cfg["wqkv"][kc * 128:(kc + 1) * 128, :])
            wld_tok = S.tok_now()
            vset = S.op("V", lambda v: v.memset(
                vasm[:].rearrange("p (k j) -> p k j", k=KV, j=65)[:, :, 64:65], 1.0))
            if mode == "l1f":
                hz = S.op("V", lambda v: v.memset(
                    hb[:].rearrange("p (c j) -> p c j", c=8, j=TB + 2)[:, :, 0:2], 0.0))
            S.barrier()

            prev_x_readers = None
            prev_statreaders = []
            prev_psAC = None
            prev_act_mm = None
            prev_xa = [None, None]
            prev_sliver = None
            prev_tsb_store = None
            self.prev_tab = [None, None]
            self.ps_free = [None, None]
            self.rope_scr = None
            self.rot_rd = None
            self.vst_last = None
            self.tr_ev = None
            self.asm_st = []
            for b in range(self.nb):
                tslice = slice(b * TB, (b + 1) * TB)
                xdep = list(prev_x_readers) if prev_x_readers else []
                xts = [S.dma("Y", f"{tag}_xin", xsb[:, c * TB:(c + 1) * TB],
                             xsrc[c * 128:(c + 1) * 128, tslice], deps=xdep) for c in range(8)]
                xtok = xts[-1]
                cast_toks, sq_toks = [], []
                for c in range(8):
                    dep = [xtok]
                    cast_toks.append(S.op("V", lambda v, c=c: v.tensor_copy(
                        x16[:, c * TB:(c + 1) * TB], xsb[:, c * TB:(c + 1) * TB]), deps=dep))
                    sq_toks.append(S.op("S", lambda s, c=c: s.activation(
                        out=sq16[:, c * TB:(c + 1) * TB], in_=xsb[:, c * TB:(c + 1) * TB],
                        func=AF.Square), deps=dep))
                def mmsum(t):
                    for c in range(8):
                        ins = t.matmul(ps_sum[:], self.ones_col_bf[:], x16[:, c * TB:(c + 1) * TB],
                                       start=(c == 0), stop=(c == 7))
                    return ins
                def mmsq(t):
                    for c in range(8):
                        ins = t.matmul(ps_sq[:], self.ones_col_bf[:], sq16[:, c * TB:(c + 1) * TB],
                                       start=(c == 0), stop=(c == 7))
                    return ins
                sdep = cast_toks + sq_toks
                tsum = S.op("T", mmsum, deps=sdep)
                tsq = S.op("T", mmsq, deps=sdep)
                mu = stat[:, 0:TB]; ex2 = stat[:, TB:2 * TB]; var = stat[:, 2 * TB:3 * TB]
                std = stat[:, 3 * TB:4 * TB]; rinv = stat[:, 4 * TB:5 * TB]; nmr = stat[:, 5 * TB:6 * TB]
                var2 = stat[:, 6 * TB:7 * TB]; nmr2 = stat[:, 7 * TB:8 * TB]
                v1 = S.op("V", lambda v: v.tensor_scalar_mul(mu, ps_sum[:], 1.0 / D),
                          deps=[tsum] + prev_statreaders)
                v2 = S.op("V", lambda v: v.tensor_scalar_mul(ex2, ps_sq[:], 1.0 / D), deps=[tsq])
                v3 = S.op("V", lambda v: v.scalar_tensor_tensor(
                    var, mu, 1.0, mu, op0=ALU.mult, op1=ALU.mult), deps=[v1])
                v4 = S.op("V", lambda v: v.tensor_sub(var, ex2, var), deps=[v2, v3])
                s5 = S.op("S", lambda s: s.activation(out=std, in_=var, func=AF.Sqrt, bias=EPS),
                          deps=[v4])
                v6 = S.op("V", lambda v: v.reciprocal(rinv, std), deps=[s5])
                v7 = S.op("V", lambda v: v.scalar_tensor_tensor(
                    nmr, mu, -1.0, rinv, op0=ALU.mult, op1=ALU.mult), deps=[v6, v1])
                pdep = [prev_psAC] if prev_psAC else []
                tA = S.op("T", lambda t: t.matmul(ps_A[:], self.ones_row_f[:], rinv,
                                                  start=True, stop=True), deps=[v6] + pdep)
                tC = S.op("T", lambda t: t.matmul(ps_C[:], self.ones_row_f[:], nmr,
                                                  start=True, stop=True), deps=[v7] + pdep)
                norm_toks = []
                if mode == "l0":
                    for c in range(8):
                        sl2 = c % 2
                        d1 = [tA, xtok] + ([prev_xa[sl2]] if prev_xa[sl2] else [])
                        n1 = S.op("V", lambda v, c=c: v.tensor_mul(
                            xa[c % 2][:], xsb[:, c * TB:(c + 1) * TB], ps_A[:]), deps=d1)
                        d2 = [tC, n1] + ([prev_act_mm] if prev_act_mm else [])
                        n2 = S.op("V", lambda v, c=c: v.tensor_tensor(
                            act16[:, c * TB:(c + 1) * TB], xa[c % 2][:], ps_C[:],
                            op=ALU.add), deps=d2)
                        prev_xa[sl2] = n2
                        norm_toks.append(n2)
                    statr = [tA, tC]
                else:
                    t_toks = []
                    for c in range(8):
                        sl2 = c % 2
                        d1 = [tA, xtok] + ([prev_xa[sl2]] if prev_xa[sl2] else [])
                        n1 = S.op("V", lambda v, c=c: v.tensor_mul(
                            xa[c % 2][:], xsb[:, c * TB:(c + 1) * TB], ps_A[:]), deps=d1)
                        d2 = [tC, n1] + ([prev_tsb_store] if b > 0 else [])
                        n2 = S.op("V", lambda v, c=c: v.tensor_tensor(
                            tsb[:, c * TB:(c + 1) * TB], xa[c % 2][:], ps_C[:],
                            op=ALU.add), deps=d2)
                        prev_xa[sl2] = n2
                        t_toks.append(n2)
                    tst = [S.dma("Y", f"{tag}_tout", cfg["tdst"][c * 128:(c + 1) * 128, tslice],
                                 tsb[:, c * TB:(c + 1) * TB], deps=[t_toks[c]]) for c in range(8)]
                    prev_tsb_store = tst[-1]
                    e1 = S.op("V", lambda v: v.tensor_scalar(
                        var2, var, EPS, EPS, op0=ALU.add, op1=ALU.mult), deps=[v4, tA])
                    e2 = S.op("V", lambda v: v.tensor_add(var2, var, var2), deps=[e1])
                    s6 = S.op("S", lambda s: s.activation(out=std, in_=var2, func=AF.Sqrt),
                              deps=[e2, v6])
                    e3 = S.op("V", lambda v: v.reciprocal(rinv, std), deps=[s6, tA])
                    e4 = S.op("V", lambda v: v.scalar_tensor_tensor(
                        nmr2, mu, -1.0, rinv, op0=ALU.mult, op1=ALU.mult), deps=[e3])
                    ndep = t_toks + [e3]
                    tA2 = S.op("T", lambda t: t.matmul(ps_A[:], self.ones_row_f[:], rinv,
                                                       start=True, stop=True), deps=ndep)
                    tC2 = S.op("T", lambda t: t.matmul(ps_C[:], self.ones_row_f[:], nmr2,
                                                       start=True, stop=True), deps=[e4] + ndep)
                    hw_c0 = cfg["ln1w"][0]
                    hb_c0 = cfg["ln1b"][0]
                    conv_toks = []
                    for c in range(8):
                        sl2 = c % 2
                        hoff = c * (TB + 2)
                        d1 = [tA2, xtok, t_toks[c]] + ([prev_xa[sl2]] if prev_xa[sl2] else [])
                        n1 = S.op("V", lambda v, c=c: v.tensor_mul(
                            xa[c % 2][:], xsb[:, c * TB:(c + 1) * TB], ps_A[:]), deps=d1)
                        d2 = [tC2, n1] + ([prev_sliver[c]] if prev_sliver else [])
                        n2 = S.op("V", lambda v, c=c, hoff=hoff: v.tensor_tensor(
                            hb[:, hoff + 2:hoff + 2 + TB], xa[c % 2][:], ps_C[:],
                            op=ALU.add), deps=d2)
                        prev_xa[sl2] = n2
                        n3 = S.op("V", lambda v, c=c, hoff=hoff: v.tensor_scalar(
                            hb[:, hoff + 2:hoff + 2 + TB], hb[:, hoff + 2:hoff + 2 + TB],
                            self.col(hw_c0, c), self.col(hb_c0, c),
                            op0=ALU.mult, op1=ALU.add), deps=[n2])
                        norm_toks.append(n3)
                    statr = [tA, tC, tA2, tC2]
                    sliver_toks = []
                    conv_out = []
                    for c in range(8):
                        hoff = c * (TB + 2)
                        o1 = S.op("V", lambda v, c=c, hoff=hoff: v.tensor_scalar_mul(
                            ctmp[:], hb[:, hoff:hoff + TB], self.col(cfg["cw0"][0], c)),
                            deps=[norm_toks[c]] + ([conv_out[-1]] if conv_out else []))
                        o2 = S.op("V", lambda v, c=c, hoff=hoff: v.scalar_tensor_tensor(
                            ctmp[:], hb[:, hoff + 1:hoff + 1 + TB], self.col(cfg["cw1"][0], c),
                            ctmp[:], op0=ALU.mult, op1=ALU.add), deps=[o1])
                        o3dep = [o2] + ([prev_act_mm] if prev_act_mm else [])
                        o3 = S.op("V", lambda v, c=c, hoff=hoff: v.scalar_tensor_tensor(
                            act16[:, c * TB:(c + 1) * TB], hb[:, hoff + 2:hoff + 2 + TB],
                            self.col(cfg["cw2"][0], c), ctmp[:],
                            op0=ALU.mult, op1=ALU.add), deps=o3dep)
                        sv = S.op("V", lambda v, c=c, hoff=hoff: v.tensor_copy(
                            hb[:, hoff:hoff + 2], hb[:, hoff + TB:hoff + TB + 2]), deps=[o3])
                        sliver_toks.append(sv)
                        conv_out.append(o3)
                    norm_toks = conv_out
                    prev_sliver = sliver_toks
                prev_statreaders = statr
                prev_psAC = norm_toks[-1] if mode == "l0" else norm_toks[-1]
                for tt in range(TB // 128):
                    gtile = b * (TB // 128) + tt
                    sl = tt % 2
                    tdep = [self.prev_tab[sl]] if self.prev_tab[sl] else []
                    tabtok = None
                    for ti, tab in enumerate(tabs):
                        tabtok = S.dma("Y", f"{tag}_tab{sl}", tsl[sl][:, ti * 1024:(ti + 1) * 1024],
                                       tab[gtile * 128:(gtile + 1) * 128, :], deps=tdep)
                    tabtile = tsl[sl]
                    def actT(kc, tt=tt):
                        return act16[:, kc * TB + tt * 128: kc * TB + tt * 128 + 128]
                    if mode == "l0":
                        groups = [("q", 0, 512, 0), ("q", 512, 512, 1),
                                  ("k", 1024, 256, 0), ("v", 1280, 256, 0)]
                    else:
                        groups = [("q", 0, 512, 0), ("q", 512, 512, 1),
                                  ("k", 1024, 512, 0), ("k", 1536, 512, 1),
                                  ("v", 2048, 512, 0), ("v", 2560, 512, 1)]
                    rope_q = [None, None]
                    rope_k = [None, None]
                    vev_tok = None
                    for gi, (kind, ooff, width, half) in enumerate(groups):
                        psl = gi % 2
                        mdeps = list(norm_toks) + list(wld_tok)
                        if self.ps_free[psl] is not None:
                            mdeps.append(self.ps_free[psl])
                        def mmg(t, ooff=ooff, width=width, psl=psl, tt=tt):
                            for kc in range(8):
                                ins = t.matmul(ps[psl][:, 0:width],
                                               actT(kc, tt),
                                               wsb[:, kc * wtot + ooff:kc * wtot + ooff + width],
                                               start=(kc == 0), stop=(kc == 7))
                            return ins
                        gt = S.op("T", mmg, deps=mdeps, name=f"{tag}_qkv_{b}_{tt}_{kind}{half}")
                        if kind in ("q", "k"):
                            is_q = kind == "q"
                            ctab = (0 if is_q else 2) * 1024
                            stab = (1 if is_q else 3) * 1024
                            dst = qrot if is_q else krot
                            toff = half * 512
                            e1dep = [gt, tabtok] + ([self.rope_scr] if self.rope_scr else [])
                            e1 = S.op("V", lambda v, psl=psl, ctab=ctab, toff=toff, width=width, tb_=tabtile:
                                      v.tensor_mul(rA[:, 0:width], ps[psl][:, 0:width],
                                                   tb_[:, ctab + toff:ctab + toff + width]), deps=e1dep)
                            e2 = S.op("V", lambda v, psl=psl, stab=stab, toff=toff, width=width, tb_=tabtile:
                                      v.tensor_tensor(
                                rB[:, 0:width].rearrange("p (bk h j) -> p bk h j", h=2, j=32),
                                ps[psl][:, 0:width].rearrange("p (bk h j) -> p bk h j", h=2, j=32)[:, :, ::-1, :],
                                tb_[:, stab + toff:stab + toff + width].rearrange("p (bk h j) -> p bk h j", h=2, j=32),
                                op=ALU.mult), deps=[gt, tabtok, e1])
                            e3dep = [e2] + ([self.rot_rd] if self.rot_rd else [])
                            e3 = S.op("V", lambda v, dst=dst, toff=toff, width=width:
                                      v.tensor_add(dst[:, toff:toff + width],
                                                   rA[:, 0:width], rB[:, 0:width]), deps=e3dep)
                            self.rope_scr = e3
                            self.ps_free[psl] = e2
                            if is_q:
                                rope_q[half] = e3
                            else:
                                rope_k[half] = e3
                        else:
                            koff = half * 8 if mode == "l1f" else 0
                            nk = width // 64
                            evdep = [gt, vset] + ([self.vst_last] if self.vst_last else [])
                            ev = S.op("V", lambda v, psl=psl, koff=koff, nk=nk, width=width:
                                      v.tensor_copy(
                                vasm[:].rearrange("p (k j) -> p k j", k=KV, j=65)[:, koff:koff + nk, 0:64],
                                ps[psl][:, 0:width].rearrange("p (k j) -> p k j", k=nk, j=64)),
                                deps=evdep)
                            self.ps_free[psl] = ev
                            vev_tok = ev
                    vst = [S.dma("Y", f"{tag}_vst",
                                 Vt[(k * ntile + gtile) * 128:(k * ntile + gtile + 1) * 128, :],
                                 vasm[:, k * 65:(k + 1) * 65], deps=[vev_tok]) for k in range(KV)]
                    self.vst_last = vst[-1]
                    def emit_tr(srcrot, nchunk, asm, rdeps, tt=tt):
                        evts = []
                        for hgrp in range((nchunk + 3) // 4):
                            lo, hi = hgrp * 4, min(nchunk, hgrp * 4 + 4)
                            tdeps = list(rdeps) + ([self.tr_ev] if self.tr_ev else [])
                            def trf(t, srcrot=srcrot, lo=lo, hi=hi):
                                for c in range(lo, hi):
                                    ins = t.transpose(ps_tr[:, (c - lo) * 128:(c - lo + 1) * 128],
                                                      srcrot[:, c * 128:(c + 1) * 128], self.idn_bf[:])
                                return ins
                            trt = S.op("T", trf, deps=tdeps)
                            evdeps = [trt] + list(self.asm_st)
                            def evf(v, asm=asm, lo=lo, hi=hi, tt=tt):
                                return v.tensor_copy(
                                    asm[:].rearrange("p (c q) -> p c q", q=512)[:, lo:hi, tt * 128:tt * 128 + 128],
                                    ps_tr[:, 0:(hi - lo) * 128].rearrange("p (c q) -> p c q", q=128))
                            evt = S.op("V", evf, deps=evdeps)
                            self.tr_ev = evt
                            evts.append(evt)
                        return evts
                    qev = emit_tr(qrot, 8, qf_asm, [t for t in rope_q if t])
                    kev = emit_tr(krot, 2 if mode == "l0" else 8, kf_asm, [t for t in rope_k if t])
                    self.rot_rd = qev[-1]
                    self.prev_tab[sl] = e2
                    self.prev_tab[sl] = qev[-1]
                    if tt == TB // 128 - 1:
                        qsts = [S.dma("Y", f"{tag}_qst", Qf[c * 128:(c + 1) * 128, tslice],
                                      qf_asm[:, c * 512:(c + 1) * 512], deps=[qev[-1]])
                                for c in range(8)]
                        nkc_k = 2 if mode == "l0" else 8
                        ksts = [S.dma("Y", f"{tag}_kst", Kf[c * 128:(c + 1) * 128, tslice],
                                      kf_asm[:, c * 512:(c + 1) * 512], deps=[kev[-1]])
                                for c in range(nkc_k)]
                        self.asm_st = [qsts[-1], ksts[-1]]
                prev_act_mm = ("T", S.ecnt["T"])
                prev_x_readers = norm_toks + [cast_toks[-1], sq_toks[-1]]
            S.barrier()

    def emit_attn_core(self, Qf, Kf, Vt, Of, causal, KV, tag, maskq=None, maskt=None):
        S, nc = self.S, self.nc
        seq = self.seq
        ntile = seq // 128
        NQB = seq // 512
        HPK = 16 // KV
        units = [(kv, kv * HPK + hh, qb) for kv in range(KV)
                 for hh in range(HPK) for qb in range(NQB)]
        gsize = HPK * NQB
        with contextlib.ExitStack() as ph:
            katt = [ph.enter_context(nc.sbuf_tensor(f"{tag}_katt{i}", [128, seq], BF16)) for i in range(2)]
            qatt = [ph.enter_context(nc.sbuf_tensor(f"{tag}_qatt{i}", [128, 512], BF16)) for i in range(2)]
            q128 = [ph.enter_context(nc.sbuf_tensor(f"{tag}_q128{i}", [128, 512], BF16)) for i in range(2)]
            vh = [ph.enter_context(nc.sbuf_tensor(f"{tag}_vh{i}", [128, ntile * 65], BF16)) for i in range(2)]
            nm4 = [ph.enter_context(nc.sbuf_tensor(f"{tag}_nm4{i}", [128, 4], BF16)) for i in range(2)]
            nmsb = [ph.enter_context(nc.sbuf_tensor(f"{tag}_nmsb{i}", [4, 128], BF16)) for i in range(2)]
            mxall = ph.enter_context(nc.sbuf_tensor(f"{tag}_mxall", [128, 32], F32))
            pt16 = [ph.enter_context(nc.sbuf_tensor(f"{tag}_pt{i}", [128, 512], BF16)) for i in range(2)]
            osb = ph.enter_context(nc.sbuf_tensor(f"{tag}_osb", [64, 512], F32))
            o16 = ph.enter_context(nc.sbuf_tensor(f"{tag}_o16", [64, 512], BF16))
            l16row = ph.enter_context(nc.sbuf_tensor(f"{tag}_l16row", [1, 512], BF16))
            rsb64 = ph.enter_context(nc.sbuf_tensor(f"{tag}_rsb64", [64, 512], F32))
            if causal:
                mq = ph.enter_context(nc.sbuf_tensor(f"{tag}_mq", [128, 4 * 512], F32))
                mt = ph.enter_context(nc.sbuf_tensor(f"{tag}_mt", [128, 4 * 512], F32))
            psS = [ph.enter_context(nc.psum_tensor(f"{tag}_psS{i}", [128, 512], F32)) for i in range(2)]
            psT = [ph.enter_context(nc.psum_tensor(f"{tag}_psT{i}", [128, 512], F32)) for i in range(2)]
            psO = [ph.enter_context(nc.psum_tensor(f"{tag}_psO{i}", [65, 512], F32)) for i in range(2)]
            psB = ph.enter_context(nc.psum_tensor(f"{tag}_psB", [64, 512], F32))
            psNM = ph.enter_context(nc.psum_tensor(f"{tag}_psNM", [4, 128], BF16))
            if causal:
                for i in range(4):
                    S.dma("Y", f"{tag}_mld", mq[:, i * 512:(i + 1) * 512], maskq[i * 128:(i + 1) * 128, :])
                    S.dma("Y", f"{tag}_mld", mt[:, i * 512:(i + 1) * 512], maskt[i * 128:(i + 1) * 128, :])
            kz = [S.op("V", lambda v, i=i: v.memset(katt[i][:], 0.0)) for i in range(2)]
            qz = [S.op("V", lambda v, i=i: v.memset(qatt[i][:], 0.0)) for i in range(2)]
            q2z = [S.op("V", lambda v, i=i: v.memset(q128[i][:], 0.0)) for i in range(2)]
            S.barrier()

            st_state = {}
            self.ac_kv = [None, None]
            self.ac_kv_users = [None, None]
            self.ac_q_users = [None, None]
            self.ac_psS = [None, None]
            self.ac_psT = [None, None]
            self.ac_pt = [None, None]
            self.ac_nm = [None, None]
            self.ac_oev = None
            self.ac_ost = None
            self.ac_rfin = None
            self.ac_psO = [None, None]

            def buildA(i):
                kv, h, qb = units[i]
                ksl = kv % 2
                usl = i % 2
                if i % gsize == 0:
                    kdeps = [self.ac_kv_users[ksl]] if self.ac_kv_users[ksl] else []
                    kdeps += [kz[ksl]]
                    ktok = S.dma("Y", f"{tag}_kld{ksl}", katt[ksl][0:64, :],
                                 Kf[kv * 64:(kv + 1) * 64, :], deps=kdeps)
                    kone = S.op("V", lambda v: v.memset(katt[ksl][64:65, :], 1.0), deps=kdeps)
                    vtok = [S.dma("Y", f"{tag}_vld{ksl}", vh[ksl][:, kc * 65:(kc + 1) * 65],
                                  Vt[(kv * ntile + kc) * 128:(kv * ntile + kc + 1) * 128, :],
                                  deps=kdeps) for kc in range(ntile)][-1]
                    self.ac_kv[ksl] = (ktok, kone, vtok)
                ktok, kone, vtok = self.ac_kv[ksl]
                nkb = (qb + 1) if causal else NQB
                qdep = [self.ac_q_users[usl], qz[usl], q2z[usl]] if self.ac_q_users[usl] \
                    else [qz[usl], q2z[usl]]
                qtok = S.dma("Y", f"{tag}_qld{usl}", qatt[usl][0:64, :],
                             Qf[h * 64:(h + 1) * 64, qb * 512:(qb + 1) * 512], deps=qdep)
                qtok2 = S.dma("Y", f"{tag}_q2ld{usl}", q128[usl][0:64, :],
                              Qf[h * 64:(h + 1) * 64, qb * 512:(qb + 1) * 512], deps=qdep)
                nm_writes = []
                red_toks = []
                thunks = []

                def piece_parts(qt, kb):
                    psl = self.ac_nredg % 2
                    self.ac_nredg += 1
                    sdeps = [qtok, qtok2, ktok, kone]
                    if self.ac_psS[psl]:
                        sdeps.append(self.ac_psS[psl])

                    def safn(t, qt=qt, kb=kb, psl=psl):
                        return t.matmul(
                            psS[psl][:], q128[usl][:, qt * 128:(qt + 1) * 128],
                            katt[ksl][:, kb * 512:(kb + 1) * 512], start=True, stop=True)

                    def post(smt, qt=qt, kb=kb, psl=psl):
                        base = 16 * (qt % 2)
                        rdep = [smt]
                        if causal and kb == qb:
                            mk = S.op("V", lambda v, psl=psl, qt=qt: v.tensor_tensor(
                                psS[psl][:], psS[psl][:], mq[:, qt * 512:(qt + 1) * 512],
                                op=ALU.add), deps=[smt], name=f"{tag}_Smask")
                            rdep = [mk]
                        red = S.op("V", lambda v, psl=psl, kb=kb, base=base: v.reduce_max(
                            mxall[:, base + kb:base + kb + 1], psS[psl][:], axis=AX.X),
                            deps=rdep, name=f"{tag}_red")
                        self.ac_psS[psl] = red
                        red_toks.append(red)
                        if kb == nkb - 1:
                            nmdep = list(red_toks) + ([self.ac_nm[usl]] if self.ac_nm[usl] else [])
                            red_toks.clear()
                            rfin = S.op("V", lambda v, base=base: v.reduce_max(
                                mxall[:, base + 15:base + 16], mxall[:, base:base + nkb],
                                axis=AX.X), deps=nmdep)
                            nmw = S.op("V", lambda v, qt=qt, base=base: v.tensor_scalar_mul(
                                nm4[usl][:, qt:qt + 1], mxall[:, base + 15:base + 16], -1.0),
                                deps=[rfin])
                            self.ac_rfin = nmw
                            nm_writes.append(nmw)
                        if qt == 3 and kb == nkb - 1:
                            ntr = S.op("T", lambda t: t.transpose(
                                psNM[:], nm4[usl][:], self.idn_bf[:]), deps=nm_writes +
                                ([self.ac_nm[usl]] if self.ac_nm[usl] else []), name=f"{tag}_nmtr")
                            nev = S.op("V", lambda v: v.tensor_copy(nmsb[usl][:], psNM[:]),
                                       deps=[ntr], name=f"{tag}_nmev")
                            self.ac_nm[usl] = nev
                            nmd = S.dma("G", f"{tag}_nmdma{usl}", qatt[usl][64:65, 0:512],
                                        nmsb[usl][:], deps=[nev, qtok])
                            st_state[i] = (qtok, nmd, ktok, kone, vtok)
                    return safn, sdeps, post

                for qt in range(4):
                    for kb in range(nkb):
                        thunks.append(lambda qt=qt, kb=kb: piece_parts(qt, kb))
                return thunks

            def emitB(i, athunks):
                kv, h, qb = units[i]
                ksl = kv % 2
                usl = i % 2
                sl2 = i % 2
                qtok, nmd, ktok, kone, vtok = st_state.pop(i)
                nkc = 4 * (qb + 1) if causal else ntile
                exp_toks = [None] * nkc
                pv_last = None
                adone = 0
                na = len(athunks)

                def emitPVSA(kc_pv, parts):
                    nonlocal pv_last
                    fns = []
                    deps = []
                    if kc_pv is not None:
                        psl = kc_pv % 2
                        deps += [exp_toks[kc_pv], vtok]
                        if kc_pv == 0 and self.ac_psO[sl2]:
                            deps.append(self.ac_psO[sl2])

                        def pvfn(t, kc=kc_pv, psl=psl):
                            return t.matmul(
                                psO[sl2][:], vh[ksl][:, kc * 65:(kc + 1) * 65], pt16[psl][:],
                                start=(kc == 0), stop=(kc == nkc - 1))
                        fns.append(pvfn)
                    for (safn, sadeps, _) in parts:
                        fns.append(safn)
                        deps += sadeps
                    if not fns:
                        return
                    tok = S.op("T", lambda t, fns=tuple(fns): [f(t) for f in fns][-1],
                               deps=deps, name=f"{tag}_PVSA")
                    if kc_pv is not None:
                        self.ac_pt[kc_pv % 2] = tok
                        pv_last = tok
                    for (_, _, post) in parts:
                        post(tok)

                for kc in range(nkc):
                    psl = kc % 2
                    stdeps = [nmd, ktok, kone]
                    if self.ac_psT[psl]:
                        stdeps.append(self.ac_psT[psl])
                    stt_ = S.op("T", lambda t, kc=kc, psl=psl: t.matmul(
                        psT[psl][:], katt[ksl][:, kc * 128:(kc + 1) * 128], qatt[usl][:],
                        start=True, stop=True), deps=stdeps, name=f"{tag}_ST")
                    edep = [stt_]
                    if causal and kc >= 4 * qb:
                        mk = S.op("V", lambda v, psl=psl, kc=kc, qb=qb: v.tensor_tensor(
                            psT[psl][:], psT[psl][:], mt[:, (kc - 4 * qb) * 512:(kc - 4 * qb + 1) * 512],
                            op=ALU.add), deps=[stt_], name=f"{tag}_Tmask")
                        edep = [mk]
                    if self.ac_pt[psl]:
                        edep.append(self.ac_pt[psl])
                    ex = S.op("S", lambda s, psl=psl: s.activation(
                        out=pt16[psl][:], in_=psT[psl][:], func=AF.Exp), deps=edep,
                        name=f"{tag}_exp")
                    self.ac_psT[psl] = ex
                    exp_toks[kc] = ex
                    parts = []
                    while adone * nkc < (kc + 1) * na:
                        parts.append(athunks[adone]())
                        adone += 1
                    emitPVSA(kc - 1 if kc > 0 else None, parts)
                parts = [athunks[j]() for j in range(adone, na)]
                adone = na
                emitPVSA(nkc - 1, parts)
                l1 = S.op("S", lambda s: s.copy(l16row[:], psO[sl2][64:65, :]),
                          deps=[pv_last] + ([self.ac_oev] if self.ac_oev else []), name=f"{tag}_lrow")
                oc = S.op("S", lambda s: s.copy(osb[:], psO[sl2][0:64, :]),
                          deps=[pv_last] + ([self.ac_oev] if self.ac_oev else []), name=f"{tag}_ocopy")
                bmm = S.op("T", lambda t: t.matmul(psB[:], self.ones_row_bf[0:1, 0:64],
                                                   l16row[:], start=True, stop=True),
                           deps=[l1] + ([self.ac_oev] if self.ac_oev else []), name=f"{tag}_bmm")
                rec = S.op("V", lambda v: v.reciprocal(rsb64[:], psB[:]),
                           deps=[bmm] + ([self.ac_oev] if self.ac_oev else []), name=f"{tag}_rec")
                oev = S.op("V", lambda v: v.tensor_tensor(o16[:], osb[:], rsb64[:], op=ALU.mult),
                           deps=[rec, oc] + ([self.ac_ost] if self.ac_ost else []), name=f"{tag}_oev")
                ost = S.dma("Y", f"{tag}_ost", Of[h * 64:(h + 1) * 64, qb * 512:(qb + 1) * 512],
                            o16[:], deps=[oev])
                self.ac_oev = oev
                self.ac_ost = ost
                self.ac_psO[sl2] = oev
                self.ac_q_users[usl] = pv_last
                self.ac_kv_users[ksl] = pv_last

            self.ac_nredg = 0
            for th in buildA(0):
                safn, sadeps, post = th()
                tok = S.op("T", lambda t, safn=safn: safn(t), deps=sadeps,
                           name=f"{tag}_S0")
                post(tok)
            for i in range(len(units)):
                athunks = buildA(i + 1) if i + 1 < len(units) else []
                emitB(i, athunks)
            S.barrier()


    def emit_moe(self, zsrc, xsrc, xdst, wbf, cols, tag):
        S, nc = self.S, self.nc
        TB = self.tb
        with contextlib.ExitStack() as ph:
            zsb = ph.enter_context(nc.sbuf_tensor(f"{tag}_z", [128, 8 * TB], F32))
            xsb = ph.enter_context(nc.sbuf_tensor(f"{tag}_x", [128, 8 * TB], F32))
            xa = [ph.enter_context(nc.sbuf_tensor(f"{tag}_xa{i}", [128, TB], F32)) for i in range(2)]
            z16 = ph.enter_context(nc.sbuf_tensor(f"{tag}_z16", [128, 8 * TB], BF16))
            sq16 = ph.enter_context(nc.sbuf_tensor(f"{tag}_sq", [128, 8 * TB], BF16))
            h16 = ph.enter_context(nc.sbuf_tensor(f"{tag}_h16", [128, 8 * TB], BF16))
            he16 = ph.enter_context(nc.sbuf_tensor(f"{tag}_he", [128, 8 * TB], BF16))
            gl16 = ph.enter_context(nc.sbuf_tensor(f"{tag}_gl", [128, 32 * TB], BF16))
            macc = ph.enter_context(nc.sbuf_tensor(f"{tag}_macc", [128, 8 * TB], F32))
            sgb = [ph.enter_context(nc.sbuf_tensor(f"{tag}_sg{i}", [128, TB], BF16)) for i in range(2)]
            msb = ph.enter_context(nc.sbuf_tensor(f"{tag}_msb", [128, 4 * TB], BF16))
            m4 = ph.enter_context(nc.sbuf_tensor(f"{tag}_m4", [128, 16], BF16))
            mT = ph.enter_context(nc.sbuf_tensor(f"{tag}_mT", [4, 512], BF16))
            mrow = ph.enter_context(nc.sbuf_tensor(f"{tag}_mrow", [1, 4 * TB], BF16))
            rmax = ph.enter_context(nc.sbuf_tensor(f"{tag}_rmax", [128, 1], F32))
            stat = ph.enter_context(nc.sbuf_tensor(f"{tag}_st", [1, 6 * TB], F32))
            rsb = ph.enter_context(nc.sbuf_tensor(f"{tag}_rw", [128, 32], BF16))
            wsl = {"wv": [ph.enter_context(nc.sbuf_tensor(f"{tag}_wv{i}", [128, 1024], BF16)) for i in range(4)],
                   "pj": [ph.enter_context(nc.sbuf_tensor(f"{tag}_pj{i}", [128, 4096], BF16)) for i in range(4)]}
            WQE = ("Y", "Y", "Y", "Y")
            ps_sum = ph.enter_context(nc.psum_tensor(f"{tag}_psum", [1, TB], F32))
            ps_sq = ph.enter_context(nc.psum_tensor(f"{tag}_psq", [1, TB], F32))
            ps_A = ph.enter_context(nc.psum_tensor(f"{tag}_psA", [128, TB], F32))
            ps_C = ph.enter_context(nc.psum_tensor(f"{tag}_psC", [128, TB], F32))
            ps = [ph.enter_context(nc.psum_tensor(f"{tag}_ps{i}", [128, TB], F32)) for i in range(2)]
            ps_r = ph.enter_context(nc.psum_tensor(f"{tag}_psr", [128, 4], F32))
            ps_nm = ph.enter_context(nc.psum_tensor(f"{tag}_psnm", [4, 128], BF16))
            for kc in range(8):
                S.dma("Y", f"{tag}_rld", rsb[:, kc * 4:(kc + 1) * 4],
                      wbf["router"][kc * 128:(kc + 1) * 128, :])
            S.barrier()
            prev_x_readers = None
            prev_statreaders = []
            prev_psAC = None
            prev_xa = [None, None]
            prev_h_mm = None
            prev_he_mm = None
            prev_gl_mm = None
            prev_m4 = None
            prev_mT = None
            prev_mrow_rd = None
            prev_msb_rd = None
            prev_sg = [None, None]
            wtoks = {"wv": [None] * 4, "pj": [None] * 4}
            ltoks = {"wv": [None] * 4, "pj": [None] * 4}
            prev_macc_st = None
            for b in range(self.nb):
                tslice = slice(b * TB, (b + 1) * TB)
                xdep = list(prev_x_readers) if prev_x_readers else []
                zts = [S.dma("Y", f"{tag}_zin", zsb[:, c * TB:(c + 1) * TB],
                             zsrc[c * 128:(c + 1) * 128, tslice], deps=xdep) for c in range(8)]
                xts = [S.dma("Y", f"{tag}_xin", xsb[:, c * TB:(c + 1) * TB],
                             xsrc[c * 128:(c + 1) * 128, tslice], deps=xdep) for c in range(8)]
                ztok, xtok = zts[-1], xts[-1]
                cast_toks, sq_toks = [], []
                for c in range(8):
                    cast_toks.append(S.op("V", lambda v, c=c: v.tensor_copy(
                        z16[:, c * TB:(c + 1) * TB], zsb[:, c * TB:(c + 1) * TB]), deps=[ztok]))
                    sq_toks.append(S.op("S", lambda s, c=c: s.activation(
                        out=sq16[:, c * TB:(c + 1) * TB], in_=zsb[:, c * TB:(c + 1) * TB],
                        func=AF.Square), deps=[ztok]))
                def mmsum(t):
                    for c in range(8):
                        ins = t.matmul(ps_sum[:], self.ones_col_bf[:], z16[:, c * TB:(c + 1) * TB],
                                       start=(c == 0), stop=(c == 7))
                    return ins
                def mmsq(t):
                    for c in range(8):
                        ins = t.matmul(ps_sq[:], self.ones_col_bf[:], sq16[:, c * TB:(c + 1) * TB],
                                       start=(c == 0), stop=(c == 7))
                    return ins
                sdep = cast_toks + sq_toks
                tsum = S.op("T", mmsum, deps=sdep)
                tsq = S.op("T", mmsq, deps=sdep)
                mu = stat[:, 0:TB]; ex2 = stat[:, TB:2 * TB]; var = stat[:, 2 * TB:3 * TB]
                std = stat[:, 3 * TB:4 * TB]; rinv = stat[:, 4 * TB:5 * TB]; nmr = stat[:, 5 * TB:6 * TB]
                v1 = S.op("V", lambda v: v.tensor_scalar_mul(mu, ps_sum[:], 1.0 / D),
                          deps=[tsum] + prev_statreaders)
                v2 = S.op("V", lambda v: v.tensor_scalar_mul(ex2, ps_sq[:], 1.0 / D), deps=[tsq])
                v3 = S.op("V", lambda v: v.scalar_tensor_tensor(
                    var, mu, 1.0, mu, op0=ALU.mult, op1=ALU.mult), deps=[v1])
                v4 = S.op("V", lambda v: v.tensor_sub(var, ex2, var), deps=[v2, v3])
                s5 = S.op("S", lambda s: s.activation(out=std, in_=var, func=AF.Sqrt, bias=EPS),
                          deps=[v4])
                v6 = S.op("V", lambda v: v.reciprocal(rinv, std), deps=[s5])
                v7 = S.op("V", lambda v: v.scalar_tensor_tensor(
                    nmr, mu, -1.0, rinv, op0=ALU.mult, op1=ALU.mult), deps=[v6, v1])
                pdep = [prev_psAC] if prev_psAC else []
                tA = S.op("T", lambda t: t.matmul(ps_A[:], self.ones_row_f[:], rinv,
                                                  start=True, stop=True), deps=[v6] + pdep)
                tC = S.op("T", lambda t: t.matmul(ps_C[:], self.ones_row_f[:], nmr,
                                                  start=True, stop=True), deps=[v7] + pdep)
                norm_toks = []
                for c in range(8):
                    sl2 = c % 2
                    d1 = [tA, ztok] + ([prev_xa[sl2]] if prev_xa[sl2] else [])
                    n1 = S.op("V", lambda v, c=c: v.tensor_mul(
                        xa[c % 2][:], zsb[:, c * TB:(c + 1) * TB], ps_A[:]), deps=d1)
                    d2 = [tC, n1] + ([prev_h_mm] if prev_h_mm else [])
                    n2 = S.op("V", lambda v, c=c: v.tensor_tensor(
                        xa[c % 2][:], xa[c % 2][:], ps_C[:], op=ALU.add), deps=d2)
                    n3 = S.op("V", lambda v, c=c: v.tensor_scalar(
                        h16[:, c * TB:(c + 1) * TB], xa[c % 2][:],
                        self.col(cols["ln2w"][0], c), self.col(cols["ln2b"][0], c),
                        op0=ALU.mult, op1=ALU.add), deps=[n2])
                    prev_xa[sl2] = n3
                    norm_toks.append(n3)
                prev_statreaders = [tA, tC]
                prev_psAC = norm_toks[-1]
                m4w = []
                for tt in range(TB // 128):
                    rdep = norm_toks + ([prev_m4] if prev_m4 else [])
                    def mmr(t, tt=tt):
                        for kc in range(8):
                            ins = t.matmul(ps_r[:], h16[:, kc * TB + tt * 128:kc * TB + tt * 128 + 128],
                                           rsb[:, kc * 4:(kc + 1) * 4], start=(kc == 0), stop=(kc == 7))
                        return ins
                    rt = S.op("T", mmr, deps=rdep, name=f"{tag}_router")
                    rm = S.op("V", lambda v: v.reduce_max(rmax[:], ps_r[:], axis=AX.X), deps=[rt])
                    mw = S.op("V", lambda v, tt=tt: v.tensor_scalar(
                        m4[:, tt * 4:(tt + 1) * 4], ps_r[:], rmax[:, 0:1], None,
                        op0=ALU.is_ge), deps=[rm])
                    prev_m4 = mw
                    m4w.append(mw)
                mTev = []
                for tt in range(TB // 128):
                    trd = [m4w[tt]] + ([prev_mT] if prev_mT else [])
                    ttr = S.op("T", lambda t, tt=tt: t.transpose(
                        ps_nm[:], m4[:, tt * 4:(tt + 1) * 4], self.idn_bf[:]), deps=trd)
                    tev = S.op("V", lambda v, tt=tt: v.tensor_copy(
                        mT[:, tt * 128:(tt + 1) * 128], ps_nm[:]), deps=[ttr] +
                        ([prev_mrow_rd] if prev_mrow_rd else []))
                    prev_mT = tev
                    mTev.append(tev)
                mrd = S.dma("G", f"{tag}_mrow", mrow[0:1, :], mT[:],
                            deps=mTev + [("T", S.ecnt["T"])])
                prev_mrow_rd = None
                exp_toks = []
                psA_rd = prev_psAC
                for e in range(4):
                    bdep = [mrd, psA_rd] + ([exp_toks[-1][0]] if exp_toks else [])
                    bc = S.op("T", lambda t, e=e: t.matmul(
                        ps_A[:], self.ones_row_bf[:],
                        mrow[0:1, e * TB:(e + 1) * TB], start=True, stop=True),
                        deps=bdep, name=f"{tag}_mbc")
                    mev = S.op("V", lambda v, e=e: v.tensor_copy(
                        msb[:, e * TB:(e + 1) * TB], ps_A[:]), deps=[bc] +
                        ([prev_msb_rd] if prev_msb_rd else []))
                    psA_rd = mev
                    hetoks = []
                    for c in range(8):
                        hdep = [mev, norm_toks[c]] + ([prev_he_mm] if prev_he_mm else [])
                        hetoks.append(S.op("V", lambda v, c=c, e=e: v.tensor_mul(
                            he16[:, c * TB:(c + 1) * TB], h16[:, c * TB:(c + 1) * TB],
                            msb[:, e * TB:(e + 1) * TB]), deps=hdep))
                    gl_toks = []
                    lastmm = None
                    prev_sg_tok = getattr(self, "_moe_prev_sg", None)
                    for ci in range(32):
                        for half, o in ((0, ci), (1, 32 + ci)):
                            sl = (2 * ci + half) % 4
                            dep = [wtoks["wv"][sl]] if wtoks["wv"][sl] else []
                            ltoks["wv"][sl] = S.dma(WQE[sl], f"{tag}_wvl{sl}", wsl["wv"][sl][:],
                                                    wbf["wv"][e][o * 128:(o + 1) * 128, :], deps=dep)
                            pdeps = hetoks + [ltoks["wv"][sl]]
                            psl = half
                            if half == 0 and prev_sg_tok is not None:
                                pdeps.append(prev_sg_tok)
                            if half == 1 and len(gl_toks) >= 1:
                                pdeps.append(gl_toks[-1])
                            def mmwv(t, o=o, psl=psl, sl=sl):
                                for k in range(8):
                                    ins = t.matmul(ps[psl][:], wsl["wv"][sl][:, k * 128:(k + 1) * 128],
                                                   he16[:, k * TB:(k + 1) * TB], start=(k == 0),
                                                   stop=(k == 7))
                                return ins
                            mm = S.op("T", mmwv, deps=pdeps, name=f"{tag}_wv{e}_{o}")
                            wtoks["wv"][sl] = mm
                            if half == 0:
                                sdep2 = [mm] + ([prev_sg[ci % 2]] if prev_sg[ci % 2] else [])
                                sg = S.op("S", lambda s, ci=ci: s.activation(
                                    out=sgb[ci % 2][:], in_=ps[0][:], func=AF.Silu), deps=sdep2)
                                prev_sg[ci % 2] = None
                                prev_sg_tok = sg
                                self._moe_prev_sg = sg
                                gmm = mm
                            else:
                                gdep = [sg, mm] + ([prev_gl_mm] if prev_gl_mm and ci == 0 else [])
                                gl = S.op("V", lambda v, ci=ci: v.tensor_tensor(
                                    gl16[:, ci * TB:(ci + 1) * TB], sgb[ci % 2][:], ps[1][:],
                                    op=ALU.mult), deps=gdep)
                                prev_sg[ci % 2] = gl
                                gl_toks.append(gl)
                    ev_toks = []
                    for o in range(8):
                        sl = o % 4
                        psl = o % 2
                        dep = [wtoks["pj"][sl]] if wtoks["pj"][sl] else []
                        ltoks["pj"][sl] = S.dma(WQE[sl], f"{tag}_pjl{sl}", wsl["pj"][sl][:],
                                                wbf["pj"][e][o * 128:(o + 1) * 128, :], deps=dep)
                        pdeps = gl_toks + [ltoks["pj"][sl]]
                        if len(ev_toks) >= 2:
                            pdeps.append(ev_toks[-2])
                        def mmpj(t, o=o, sl=sl, psl=psl):
                            for k in range(32):
                                ins = t.matmul(ps[psl][:], wsl["pj"][sl][:, k * 128:(k + 1) * 128],
                                               gl16[:, k * TB:(k + 1) * TB], start=(k == 0),
                                               stop=(k == 31))
                            return ins
                        mm = S.op("T", mmpj, deps=pdeps, name=f"{tag}_pj{e}_{o}")
                        wtoks["pj"][sl] = mm
                        if e == 0:
                            mdep = [mm] + ([prev_macc_st] if prev_macc_st else [])
                            ev = S.op("V", lambda v, o=o, psl=psl: v.tensor_copy(
                                macc[:, o * TB:(o + 1) * TB], ps[psl][:]), deps=mdep)
                        else:
                            ev = S.op("V", lambda v, o=o, psl=psl: v.tensor_tensor(
                                macc[:, o * TB:(o + 1) * TB], ps[psl][:],
                                macc[:, o * TB:(o + 1) * TB], op=ALU.add), deps=[mm, exp_toks[-1][1][o]])
                        ev_toks.append(ev)
                    exp_toks.append((mm, ev_toks))
                    prev_he_mm = ("T", S.ecnt["T"])
                    prev_gl_mm = ("T", S.ecnt["T"])
                prev_msb_rd = ("V", S.ecnt["V"])
                prev_psAC = psA_rd
                stE = []
                evE = []
                for c in range(8):
                    e1 = S.op("V", lambda v, c=c: v.scalar_tensor_tensor(
                        xsb[:, c * TB:(c + 1) * TB], zsb[:, c * TB:(c + 1) * TB], 0.5,
                        xsb[:, c * TB:(c + 1) * TB], op0=ALU.mult, op1=ALU.add),
                        deps=[xtok, ztok, norm_toks[c], exp_toks[-1][1][c], cast_toks[c], sq_toks[c]])
                    e2 = S.op("V", lambda v, c=c: v.scalar_tensor_tensor(
                        xsb[:, c * TB:(c + 1) * TB], macc[:, c * TB:(c + 1) * TB], 0.5,
                        xsb[:, c * TB:(c + 1) * TB], op0=ALU.mult, op1=ALU.add), deps=[e1])
                    evE.append(e2)
                    stE.append(S.dma("Y", f"{tag}_xout", xdst[c * 128:(c + 1) * 128, tslice],
                                     xsb[:, c * TB:(c + 1) * TB], deps=[e2]))
                prev_macc_st = evE[-1]
                prev_x_readers = evE + [stE[-1]]
                prev_h_mm = ("T", S.ecnt["T"])
                prev_m4 = mTev[-1]
                prev_mrow_rd = ("T", S.ecnt["T"])
            S.barrier()

    def emit_proj_resid(self, srcs, w_pre, resid_src, xdst, tag, bias_col=None, oc_n=8):
        S, nc = self.S, self.nc
        TB = self.tb
        kc_n = sum(n for _, n in srcs)
        with contextlib.ExitStack() as ph:
            asb = ph.enter_context(nc.sbuf_tensor(f"{tag}_a", [128, kc_n * TB], BF16))
            xsb = ph.enter_context(nc.sbuf_tensor(f"{tag}_x", [128, oc_n * TB], F32))
            wsl = [ph.enter_context(nc.sbuf_tensor(f"{tag}_w{i}", [128, kc_n * 128], BF16))
                   for i in range(2)]
            ps = [ph.enter_context(nc.psum_tensor(f"{tag}_ps{i}", [128, TB], F32)) for i in range(2)]
            wtoks = [None, None]
            ltoks = [None, None]
            prev_rd = None
            prev_wr = None
            for b in range(self.nb):
                tslice = slice(b * TB, (b + 1) * TB)
                adep = [prev_rd] if prev_rd else []
                ats = []
                cbase = 0
                for (ap, nch) in srcs:
                    for c in range(nch):
                        ats.append(S.dma("Y", f"{tag}_ain", asb[:, (cbase + c) * TB:(cbase + c + 1) * TB],
                                         ap[c * 128:(c + 1) * 128, tslice], deps=adep))
                    cbase += nch
                atok = ats[-1]
                xdep = [prev_wr] if prev_wr else []
                xts = [S.dma("Y", f"{tag}_xin", xsb[:, c * TB:(c + 1) * TB],
                             resid_src[c * 128:(c + 1) * 128, tslice], deps=xdep)
                       for c in range(oc_n)]
                xtok = xts[-1]
                evs = []
                for o in range(oc_n):
                    sl = o % 2
                    if b == 0 and o < 2:
                        dep = [wtoks[sl]] if wtoks[sl] else []
                        ltoks[sl] = S.dma("Y", f"{tag}_wld{sl}", wsl[sl][:],
                                          w_pre[o * 128:(o + 1) * 128, :], deps=dep)
                    pdeps = [ltoks[sl], atok]
                    if len(evs) >= 2:
                        pdeps.append(evs[-2])
                    def mmg(t, o=o, sl=sl):
                        for k in range(kc_n):
                            ins = t.matmul(ps[sl][:], wsl[sl][:, k * 128:(k + 1) * 128],
                                           asb[:, k * TB:(k + 1) * TB], start=(k == 0),
                                           stop=(k == kc_n - 1))
                        return ins
                    te = S.op("T", mmg, deps=pdeps, name=f"{tag}_mm{b}_{o}")
                    wtoks[sl] = te
                    nxt = o + 2
                    if nxt < oc_n or b + 1 < self.nb:
                        wo = nxt % oc_n
                        ltoks[sl] = S.dma("Y", f"{tag}_wld{sl}", wsl[sl][:],
                                          w_pre[wo * 128:(wo + 1) * 128, :], deps=[te])
                    if bias_col is not None:
                        ev = S.op("V", lambda v, o=o, sl=sl: v.scalar_tensor_tensor(
                            xsb[:, o * TB:(o + 1) * TB], ps[sl][:], self.col(bias_col[0], o),
                            xsb[:, o * TB:(o + 1) * TB], op0=ALU.add, op1=ALU.add),
                            deps=[te, xtok], name=f"{tag}_ev{b}_{o}")
                    else:
                        ev = S.op("V", lambda v, o=o, sl=sl: v.scalar_tensor_tensor(
                            xsb[:, o * TB:(o + 1) * TB], ps[sl][:], 1.0,
                            xsb[:, o * TB:(o + 1) * TB], op0=ALU.mult, op1=ALU.add),
                            deps=[te, xtok], name=f"{tag}_ev{b}_{o}")
                    evs.append(ev)
                    last_st = S.dma("Y", f"{tag}_xout", xdst[o * 128:(o + 1) * 128, tslice],
                                    xsb[:, o * TB:(o + 1) * TB], deps=[ev])
                prev_rd = te
                prev_wr = last_st
            S.barrier()

    def emit_l1a_proj(self, xsrc, Qf, Kf, Af, Bf, Vtok, w, cols, tag):
        S, nc = self.S, self.nc
        TB = self.tb
        with contextlib.ExitStack() as ph:
            xsb = ph.enter_context(nc.sbuf_tensor(f"{tag}_x", [128, 8 * TB], F32))
            xa = [ph.enter_context(nc.sbuf_tensor(f"{tag}_xa{i}", [128, TB], F32)) for i in range(2)]
            x16 = ph.enter_context(nc.sbuf_tensor(f"{tag}_x16", [128, 8 * TB], BF16))
            sq16 = ph.enter_context(nc.sbuf_tensor(f"{tag}_sq", [128, 8 * TB], BF16))
            t16 = ph.enter_context(nc.sbuf_tensor(f"{tag}_t16", [128, 8 * TB], BF16))
            stat = ph.enter_context(nc.sbuf_tensor(f"{tag}_st", [1, 6 * TB], F32))
            out16 = ph.enter_context(nc.sbuf_tensor(f"{tag}_o16", [128, 4 * TB], BF16))
            vout = ph.enter_context(nc.sbuf_tensor(f"{tag}_vout", [128, 1024], BF16))
            wvsb = ph.enter_context(nc.sbuf_tensor(f"{tag}_wv", [128, 8 * 1024], BF16))
            bvsb = ph.enter_context(nc.sbuf_tensor(f"{tag}_bv", [1, 1024], F32))
            bv16 = ph.enter_context(nc.sbuf_tensor(f"{tag}_bv16", [1, 1024], BF16))
            wrsb = ph.enter_context(nc.sbuf_tensor(f"{tag}_wr", [128, 4 * 8 * 1024], BF16))
            ps_sum = ph.enter_context(nc.psum_tensor(f"{tag}_psum", [1, TB], F32))
            ps_sq = ph.enter_context(nc.psum_tensor(f"{tag}_psq", [1, TB], F32))
            ps_A = ph.enter_context(nc.psum_tensor(f"{tag}_psA", [128, TB], F32))
            ps_C = ph.enter_context(nc.psum_tensor(f"{tag}_psC", [128, TB], F32))
            ps = [ph.enter_context(nc.psum_tensor(f"{tag}_ps{i}", [128, TB], F32)) for i in range(2)]
            for kc in range(8):
                S.dma("Y", f"{tag}_wvl", wvsb[:, kc * 1024:(kc + 1) * 1024],
                      w["wv_pl"][kc * 128:(kc + 1) * 128, :])
            projs = [("q", w["wq"], Qf, cols.get("bq32"), True),
                     ("k", w["wk"], Kf, cols.get("bk"), False),
                     ("a", w["wm1a"], Af, None, False),
                     ("b", w["wm1b"], Bf, None, False)]
            for pi, (pn, wpre, dst, bc, scaled) in enumerate(projs):
                for o in range(8):
                    S.dma("Y", f"{tag}_wrl", wrsb[:, (pi * 8 + o) * 1024:(pi * 8 + o + 1) * 1024],
                          wpre[o * 128:(o + 1) * 128, :])
            bvl = S.dma("Y", f"{tag}_bvl", bvsb[:], w["bv_row"][:])
            bvc = S.op("V", lambda v: v.tensor_copy(bv16[:], bvsb[:]), deps=[bvl])
            S.barrier()
            prev_x_readers = None
            prev_statreaders = []
            prev_psAC = None
            prev_xa = [None, None]
            prev_t_mm = None
            o16_sts = [None] * 4
            prev_vout = None
            for b in range(self.nb):
                tslice = slice(b * TB, (b + 1) * TB)
                xdep = list(prev_x_readers) if prev_x_readers else []
                xts = [S.dma("Y", f"{tag}_xin", xsb[:, c * TB:(c + 1) * TB],
                             xsrc[c * 128:(c + 1) * 128, tslice], deps=xdep) for c in range(8)]
                xtok = xts[-1]
                cast_toks, sq_toks = [], []
                for c in range(8):
                    cast_toks.append(S.op("V", lambda v, c=c: v.tensor_copy(
                        x16[:, c * TB:(c + 1) * TB], xsb[:, c * TB:(c + 1) * TB]), deps=[xtok]))
                    sq_toks.append(S.op("S", lambda s, c=c: s.activation(
                        out=sq16[:, c * TB:(c + 1) * TB], in_=xsb[:, c * TB:(c + 1) * TB],
                        func=AF.Square), deps=[xtok]))
                def mmsum(t):
                    for c in range(8):
                        ins = t.matmul(ps_sum[:], self.ones_col_bf[:], x16[:, c * TB:(c + 1) * TB],
                                       start=(c == 0), stop=(c == 7))
                    return ins
                def mmsq(t):
                    for c in range(8):
                        ins = t.matmul(ps_sq[:], self.ones_col_bf[:], sq16[:, c * TB:(c + 1) * TB],
                                       start=(c == 0), stop=(c == 7))
                    return ins
                sdep = cast_toks + sq_toks
                tsum = S.op("T", mmsum, deps=sdep)
                tsq = S.op("T", mmsq, deps=sdep)
                mu = stat[:, 0:TB]; ex2 = stat[:, TB:2 * TB]; var = stat[:, 2 * TB:3 * TB]
                std = stat[:, 3 * TB:4 * TB]; rinv = stat[:, 4 * TB:5 * TB]; nmr = stat[:, 5 * TB:6 * TB]
                v1 = S.op("V", lambda v: v.tensor_scalar_mul(mu, ps_sum[:], 1.0 / D),
                          deps=[tsum] + prev_statreaders)
                v2 = S.op("V", lambda v: v.tensor_scalar_mul(ex2, ps_sq[:], 1.0 / D), deps=[tsq])
                v3 = S.op("V", lambda v: v.scalar_tensor_tensor(
                    var, mu, 1.0, mu, op0=ALU.mult, op1=ALU.mult), deps=[v1])
                v4 = S.op("V", lambda v: v.tensor_sub(var, ex2, var), deps=[v2, v3])
                s5 = S.op("S", lambda s: s.activation(out=std, in_=var, func=AF.Sqrt, bias=EPS),
                          deps=[v4])
                v6 = S.op("V", lambda v: v.reciprocal(rinv, std), deps=[s5])
                v7 = S.op("V", lambda v: v.scalar_tensor_tensor(
                    nmr, mu, -1.0, rinv, op0=ALU.mult, op1=ALU.mult), deps=[v6, v1])
                pdep = [prev_psAC] if prev_psAC else []
                tA = S.op("T", lambda t: t.matmul(ps_A[:], self.ones_row_f[:], rinv,
                                                  start=True, stop=True), deps=[v6] + pdep)
                tC = S.op("T", lambda t: t.matmul(ps_C[:], self.ones_row_f[:], nmr,
                                                  start=True, stop=True), deps=[v7] + pdep)
                norm_toks = []
                for c in range(8):
                    sl2 = c % 2
                    d1 = [tA, xtok] + ([prev_xa[sl2]] if prev_xa[sl2] else [])
                    n1 = S.op("V", lambda v, c=c: v.tensor_mul(
                        xa[c % 2][:], xsb[:, c * TB:(c + 1) * TB], ps_A[:]), deps=d1)
                    d2 = [tC, n1] + ([prev_t_mm] if prev_t_mm else [])
                    n2 = S.op("V", lambda v, c=c: v.tensor_tensor(
                        t16[:, c * TB:(c + 1) * TB], xa[c % 2][:], ps_C[:], op=ALU.add), deps=d2)
                    prev_xa[sl2] = n2
                    norm_toks.append(n2)
                prev_statreaders = [tA, tC]
                prev_psAC = norm_toks[-1]
                evs = []
                for pi, (pn, wpre, dst, bc, scaled) in enumerate(projs):
                    for o in range(8):
                        sl = o % 2
                        pdeps = list(norm_toks)
                        if len(evs) >= 2:
                            pdeps.append(evs[-2])
                        def mmg(t, o=o, sl=sl, pi=pi):
                            base = (pi * 8 + o) * 1024
                            for k in range(8):
                                ins = t.matmul(ps[sl][:], wrsb[:, base + k * 128:base + (k + 1) * 128],
                                               t16[:, k * TB:(k + 1) * TB], start=(k == 0),
                                               stop=(k == 7))
                            return ins
                        te = S.op("T", mmg, deps=pdeps, name=f"{tag}_{pn}{b}_{o}")
                        osl = len(evs) % 4
                        osli = slice(osl * TB, (osl + 1) * TB)
                        edep = [te] + ([o16_sts[osl]] if o16_sts[osl] else [])
                        if bc is not None and scaled:
                            ev = S.op("S", lambda s, o=o, sl=sl, bc=bc, osli=osli: s.activation(
                                out=out16[:, osli], in_=ps[sl][:], func=AF.Identity,
                                bias=self.col(bc[0], o), scale=1.0 / 32), deps=edep)
                        elif bc is not None:
                            ev = S.op("S", lambda s, o=o, sl=sl, bc=bc, osli=osli: s.activation(
                                out=out16[:, osli], in_=ps[sl][:], func=AF.Identity,
                                bias=self.col(bc[0], o)), deps=edep)
                        else:
                            ev = S.op("S", lambda s, o=o, sl=sl, osli=osli: s.copy(
                                out16[:, osli], ps[sl][:]), deps=edep)
                        st = S.dma("Y", f"{tag}_pst", dst[o * 128:(o + 1) * 128, tslice],
                                   out16[:, osli], deps=[ev])
                        o16_sts[osl] = st
                        evs.append(ev)
                for tt in range(TB // 128):
                    gtile = b * (TB // 128) + tt
                    for half in range(2):
                        psl = half
                        pdeps = norm_toks + [bvc, evs[-1], evs[-2]]
                        if prev_vout:
                            pdeps.append(prev_vout)
                        def mmv(t, tt=tt, half=half, psl=psl):
                            ins = t.matmul(ps[psl][:, 0:512], self.ones_row_bf[0:1, :],
                                           bv16[0:1, half * 512:(half + 1) * 512],
                                           start=True, stop=False)
                            for kc in range(8):
                                ins = t.matmul(ps[psl][:, 0:512],
                                               t16[:, kc * TB + tt * 128:kc * TB + tt * 128 + 128],
                                               wvsb[:, kc * 1024 + half * 512:kc * 1024 + (half + 1) * 512],
                                               start=False, stop=(kc == 7))
                            return ins
                        tv = S.op("T", mmv, deps=pdeps, name=f"{tag}_v{b}_{tt}_{half}")
                        vev = S.op("V", lambda v, half=half, psl=psl: v.tensor_copy(
                            vout[:, half * 512:(half + 1) * 512], ps[psl][:, 0:512]),
                            deps=[tv] + ([prev_vout] if prev_vout else []))
                        evs.append(vev)
                    vst = S.dma("Y", f"{tag}_vst", Vtok[gtile * 128:(gtile + 1) * 128, :],
                                vout[:], deps=[evs[-1], evs[-2]])
                    prev_vout = vst
                prev_t_mm = ("T", S.ecnt["T"])
                prev_x_readers = norm_toks + [cast_toks[-1], sq_toks[-1]]
            S.barrier()

    def emit_l1a_core(self, Qf, Kf, Vtok, Gf, w, tag):
        S, nc = self.S, self.nc
        seq = self.seq
        NQB = seq // 512
        with contextlib.ExitStack() as ph:
            kres = ph.enter_context(nc.sbuf_tensor(f"{tag}_k", [128, 8 * seq], BF16))
            qblk = ph.enter_context(nc.sbuf_tensor(f"{tag}_q", [128, 8 * 512], BF16))
            ssb = ph.enter_context(nc.sbuf_tensor(f"{tag}_s", [128, seq], F32))
            p16 = ph.enter_context(nc.sbuf_tensor(f"{tag}_p", [128, seq], BF16))
            m16 = ph.enter_context(nc.sbuf_tensor(f"{tag}_m", [128, seq], BF16))
            pt = ph.enter_context(nc.sbuf_tensor(f"{tag}_pt", [128, (seq // 128) * 512], BF16))
            t8 = ph.enter_context(nc.sbuf_tensor(f"{tag}_t8", [128, 8], F32))
            negm = ph.enter_context(nc.sbuf_tensor(f"{tag}_ngm", [128, 1], F32))
            lsum = ph.enter_context(nc.sbuf_tensor(f"{tag}_l", [128, 1], F32))
            linv = ph.enter_context(nc.sbuf_tensor(f"{tag}_li", [128, 1], F32))
            mq = ph.enter_context(nc.sbuf_tensor(f"{tag}_mq", [128, 4 * 512], F32))
            g16 = ph.enter_context(nc.sbuf_tensor(f"{tag}_g16", [128, 512], BF16))
            vsl = [ph.enter_context(nc.sbuf_tensor(f"{tag}_vs{i}", [128, 1024], BF16)) for i in range(2)]
            psS = [ph.enter_context(nc.psum_tensor(f"{tag}_psS{i}", [128, 512], F32)) for i in range(2)]
            psD = [ph.enter_context(nc.psum_tensor(f"{tag}_psD{i}", [128, 512], F32)) for i in range(4)]
            psTr = ph.enter_context(nc.psum_tensor(f"{tag}_ptr", [128, 512], BF16))
            for kc in range(8):
                S.dma("Y", f"{tag}_kld", kres[:, kc * seq:(kc + 1) * seq], Kf[kc * 128:(kc + 1) * 128, :])
            for i in range(4):
                S.dma("Y", f"{tag}_mld", mq[:, i * 512:(i + 1) * 512], w["maskq"][i * 128:(i + 1) * 128, :])
            S.barrier()
            prev_q_rd = None
            prev_s = [None, None]
            prev_ssb_rd = []
            prev_p_rd = None
            prev_pt_rd = None
            prev_tr = None
            prev_g = None
            prev_gst = None
            vltoks = [None, None]
            vwtoks = [None, None]
            for qb in range(NQB):
                qdep = [prev_q_rd] if prev_q_rd else []
                qts = [S.dma("Y", f"{tag}_qld", qblk[:, c * 512:(c + 1) * 512],
                             Qf[c * 128:(c + 1) * 128, qb * 512:(qb + 1) * 512], deps=qdep)
                       for c in range(8)]
                qtok = qts[-1]
                nkc = 4 * (qb + 1)
                tr_toks = []
                for qtl in range(4):
                    qt = qb * 4 + qtl
                    L = qt * 128 + 128
                    nkb = (L + 511) // 512
                    ev_toks = []
                    for kb in range(nkb):
                        wdt = min(512, L - kb * 512) if kb == nkb - 1 else 512
                        wdt = 512
                        psl = kb % 2
                        sdeps = [qtok] + ([prev_s[psl]] if prev_s[psl] else [])
                        def mms(t, qtl=qtl, kb=kb, psl=psl):
                            for c in range(8):
                                ins = t.matmul(psS[psl][:], qblk[:, c * 512 + qtl * 128:c * 512 + qtl * 128 + 128],
                                               kres[:, c * seq + kb * 512:c * seq + kb * 512 + 512],
                                               start=(c == 0), stop=(c == 7))
                            return ins
                        ts_ = S.op("T", mms, deps=sdeps, name=f"{tag}_S{qt}_{kb}")
                        if kb == nkb - 1:
                            ev = S.op("V", lambda v, psl=psl, kb=kb, qtl=qtl: v.tensor_tensor(
                                ssb[:, kb * 512:kb * 512 + 512], psS[psl][:],
                                mq[:, qtl * 512:(qtl + 1) * 512], op=ALU.add),
                                deps=[ts_] + (prev_ssb_rd or []), name=f"{tag}_Sev")
                        else:
                            ev = S.op("S", lambda s, psl=psl, kb=kb: s.copy(
                                ssb[:, kb * 512:kb * 512 + 512], psS[psl][:]),
                                deps=[ts_] + (prev_ssb_rd or []), name=f"{tag}_Sev")
                        prev_s[psl] = ev
                        ev_toks.append(ev)
                    t1 = S.op("V", lambda v, L=L: v.max(out=t8[:], in_=ssb[:, 0:L]),
                              deps=ev_toks, name=f"{tag}_top8")
                    t2 = S.op("V", lambda v: v.tensor_scalar_mul(negm[:], t8[:, 0:1], -1.0), deps=[t1])
                    t3dep = [t2] + ([prev_p_rd] if prev_p_rd else [])
                    t3 = S.op("S", lambda s, L=L: s.activation(
                        out=p16[:, 0:L], in_=ssb[:, 0:L], func=AF.Exp, bias=negm[:, 0:1]),
                        deps=t3dep, name=f"{tag}_exp")
                    t4 = S.op("V", lambda v, L=L: v.tensor_scalar(
                        m16[:, 0:L], ssb[:, 0:L], t8[:, 7:8], None, op0=ALU.is_ge),
                        deps=[t1] + ([prev_p_rd] if prev_p_rd else []), name=f"{tag}_thr")
                    t5 = S.op("V", lambda v, L=L: v.tensor_mul(p16[:, 0:L], p16[:, 0:L], m16[:, 0:L]),
                              deps=[t3, t4], name=f"{tag}_pm")
                    t6 = S.op("V", lambda v, L=L: v.reduce_sum(lsum[:], p16[:, 0:L], axis=AX.X),
                              deps=[t5], name=f"{tag}_lsum")
                    t7 = S.op("V", lambda v: v.reciprocal(linv[:], lsum[:]), deps=[t6])
                    t8_ = S.op("V", lambda v, L=L: v.tensor_scalar_mul(p16[:, 0:L], p16[:, 0:L],
                                                                       linv[:, 0:1]),
                               deps=[t7], name=f"{tag}_pn")
                    nblk = L // 128
                    for grp in range((nblk + 3) // 4):
                        lo, hi = grp * 4, min(nblk, grp * 4 + 4)
                        tdeps = [t8_] + ([prev_tr] if prev_tr else [])
                        def trf(t, lo=lo, hi=hi):
                            for c in range(lo, hi):
                                ins = t.transpose(psTr[:, (c - lo) * 128:(c - lo + 1) * 128],
                                                  p16[:, c * 128:(c + 1) * 128], self.idn_bf[:])
                            return ins
                        trt = S.op("T", trf, deps=tdeps)
                        evdeps = [trt] + ([prev_pt_rd] if prev_pt_rd else [])
                        def evf(v, lo=lo, hi=hi, qtl=qtl):
                            return v.tensor_copy(
                                pt[:].rearrange("p (c q) -> p c q", q=512)[:, lo:hi, qtl * 128:qtl * 128 + 128],
                                psTr[:, 0:(hi - lo) * 128].rearrange("p (c q) -> p c q", q=128))
                        evt = S.op("V", evf, deps=evdeps)
                        prev_tr = evt
                        tr_toks.append(evt)
                    prev_ssb_rd = [t4, t3]
                    prev_p_rd = ("T", S.ecnt["T"])
                for dd in range(1, 4):
                    kc = 4 * qb + dd
                    zt = S.op("V", lambda v, kc=kc, dd=dd: v.memset(
                        pt[:, kc * 512:kc * 512 + dd * 128], 0.0),
                        deps=[prev_pt_rd] if prev_pt_rd else [])
                    tr_toks.append(zt)
                gsts = []
                for half in range(2):
                    pv_toks = []
                    for kc in range(nkc):
                        sl = kc % 2
                        vdep = [vwtoks[sl]] if vwtoks[sl] else []
                        vltoks[sl] = S.dma("Y", f"{tag}_vld{sl}", vsl[sl][:],
                                           Vtok[kc * 128:(kc + 1) * 128, :], deps=vdep)
                        mdeps = [vltoks[sl]] + tr_toks
                        if kc == 0:
                            mdeps += [prev_g] if prev_g else []
                        def mpv(t, kc=kc, sl=sl, half=half, nkc=nkc):
                            for dh in range(4):
                                ins = t.matmul(psD[dh][:], vsl[sl][:, (half * 4 + dh) * 128:(half * 4 + dh + 1) * 128],
                                               pt[:, kc * 512:(kc + 1) * 512],
                                               start=(kc == 0), stop=(kc == nkc - 1))
                            return ins
                        pv = S.op("T", mpv, deps=mdeps, name=f"{tag}_pv{qb}_{half}_{kc}")
                        vwtoks[sl] = pv
                        pv_toks.append(pv)
                    for dh in range(4):
                        gdep = [pv_toks[-1]] + ([prev_gst] if prev_gst else [])
                        gev = S.op("S", lambda s, dh=dh: s.copy(g16[:], psD[dh][:]), deps=gdep,
                                   name=f"{tag}_gev")
                        gst = S.dma("Y", f"{tag}_gst",
                                    Gf[(half * 4 + dh) * 128:(half * 4 + dh + 1) * 128,
                                       qb * 512:(qb + 1) * 512], g16[:], deps=[gev])
                        prev_gst = gst
                        gsts.append(gst)
                        prev_g = gev
                prev_q_rd = ("T", S.ecnt["T"])
                prev_pt_rd = ("T", S.ecnt["T"])
            S.barrier()

    def emit_l1a_local(self, Af, Bf, Lf, w, cols, tag):
        S, nc = self.S, self.nc
        TB = self.tb
        with contextlib.ExitStack() as ph:
            asb = ph.enter_context(nc.sbuf_tensor(f"{tag}_a", [128, 8 * TB], BF16))
            bsb = ph.enter_context(nc.sbuf_tensor(f"{tag}_b", [128, 8 * (TB + 4)], BF16))
            zb = [ph.enter_context(nc.sbuf_tensor(f"{tag}_zb{i}", [128, TB], F32)) for i in range(2)]
            sw = [ph.enter_context(nc.sbuf_tensor(f"{tag}_sw{i}", [128, TB], F32)) for i in range(2)]
            acc = [ph.enter_context(nc.sbuf_tensor(f"{tag}_acc{i}", [128, TB], F32)) for i in range(2)]
            sm16 = ph.enter_context(nc.sbuf_tensor(f"{tag}_sm", [128, 8 * TB], BF16))
            o16 = ph.enter_context(nc.sbuf_tensor(f"{tag}_o", [128, TB], BF16))
            wsl = [ph.enter_context(nc.sbuf_tensor(f"{tag}_w{i}", [128, 1024], BF16)) for i in range(2)]
            ps = [ph.enter_context(nc.psum_tensor(f"{tag}_ps{i}", [128, TB], F32)) for i in range(2)]
            S.barrier()
            wtoks = [None, None]
            ltoks = [None, None]
            prev_sm_mm = None
            prev_rd = None
            prev_ost = None
            for b in range(self.nb):
                tslice = slice(b * TB, (b + 1) * TB)
                adep = [prev_rd] if prev_rd else []
                ats = [S.dma("Y", f"{tag}_ain", asb[:, c * TB:(c + 1) * TB],
                             Af[c * 128:(c + 1) * 128, tslice], deps=adep) for c in range(8)]
                bts = []
                zpads = []
                for c in range(8):
                    boff = c * (TB + 4)
                    if b == 0:
                        zpads.append(S.op("V", lambda v, boff=boff: v.memset(
                            bsb[:, boff:boff + 4], 0.0), deps=adep))
                        bts.append(S.dma("Y", f"{tag}_bin", bsb[:, boff + 4:boff + 4 + TB],
                                         Bf[c * 128:(c + 1) * 128, tslice], deps=adep))
                    else:
                        bts.append(S.dma("Y", f"{tag}_bin", bsb[:, boff:boff + 4 + TB],
                                         Bf[c * 128:(c + 1) * 128, b * TB - 4:(b + 1) * TB],
                                         deps=adep))
                atok, btok = ats[-1], bts[-1]
                sm_toks = []
                lane_last = [None, None]
                for c in range(8):
                    ln = c % 2
                    boff = c * (TB + 4)
                    accdep = [atok, btok] + zpads +                         ([lane_last[ln]] if lane_last[ln] else [])
                    prevacc = None
                    for wd in range(1, 5):
                        z1 = S.op("V", lambda v, c=c, boff=boff, wd=wd, ln=ln: v.tensor_tensor(
                            zb[ln][:], asb[:, c * TB:(c + 1) * TB],
                            bsb[:, boff + 4 - wd:boff + 4 - wd + TB], op=ALU.add),
                            deps=accdep + ([prevacc] if prevacc else []))
                        if wd == 1:
                            sv = S.op("S", lambda s, c=c, ln=ln: s.activation(
                                out=acc[ln][:], in_=zb[ln][:], func=AF.Silu,
                                bias=self.col(cols["bm1"][0], c)), deps=[z1])
                            prevacc = sv
                        else:
                            sv = S.op("S", lambda s, c=c, ln=ln: s.activation(
                                out=sw[ln][:], in_=zb[ln][:], func=AF.Silu,
                                bias=self.col(cols["bm1"][0], c)), deps=[z1, prevacc])
                            av = S.op("V", lambda v, ln=ln: v.tensor_add(
                                acc[ln][:], acc[ln][:], sw[ln][:]), deps=[sv])
                            prevacc = av
                    smdep = [prevacc] + ([prev_sm_mm] if prev_sm_mm else [])
                    smt = S.op("V", lambda v, c=c, ln=ln: v.tensor_scalar_mul(
                        sm16[:, c * TB:(c + 1) * TB], acc[ln][:], 0.25), deps=smdep)
                    lane_last[ln] = smt
                    sm_toks.append(smt)
                evs = []
                for o in range(8):
                    sl = o % 2
                    dep = [wtoks[sl]] if wtoks[sl] else []
                    ltoks[sl] = S.dma("Y", f"{tag}_wld{sl}", wsl[sl][:],
                                      w["wm2"][o * 128:(o + 1) * 128, :], deps=dep)
                    pdeps = [ltoks[sl]] + sm_toks
                    if len(evs) >= 2:
                        pdeps.append(evs[-2])
                    def mmg(t, o=o, sl=sl):
                        for k in range(8):
                            ins = t.matmul(ps[sl][:], wsl[sl][:, k * 128:(k + 1) * 128],
                                           sm16[:, k * TB:(k + 1) * TB], start=(k == 0), stop=(k == 7))
                        return ins
                    te = S.op("T", mmg, deps=pdeps, name=f"{tag}_mm{b}_{o}")
                    wtoks[sl] = te
                    ev = S.op("S", lambda s, o=o, sl=sl: s.activation(
                        out=o16[:], in_=ps[sl][:], func=AF.Identity,
                        bias=self.col(cols["bm2"][0], o)),
                        deps=[te] + ([prev_ost] if prev_ost else []))
                    st = S.dma("Y", f"{tag}_lst", Lf[o * 128:(o + 1) * 128, tslice], o16[:],
                               deps=[ev])
                    prev_ost = st
                    evs.append(ev)
                prev_sm_mm = te
                prev_rd = te
            S.barrier()



def build_program(seq=4096, tb=512, debug_outs=()):
    mk = MK(seq, tb)
    mk.debug_outs = list(debug_outs)
    S = mk.S
    x_in = mk.din("x", [seq, D], F32)
    out = mk.dout("out", [seq, D], F32)
    xA = mk.dscratch("xA", [D, seq], F32, debug=True)
    xB = mk.dscratch("xB", [D, seq], F32, debug=True)

    mk.emit_consts()
    S.barrier()
    return mk, x_in, out, xA, xB


def build_l0ffn_weights(mk, I):
    wbf = {}
    w1T = I["L0_ffn_w1_w"].T
    w2T = I["L0_ffn_w2_w"].T
    h1T = I["L0_ffn_h1_w"].T
    h2T = I["L0_ffn_h2_w"].T
    w3T = I["L0_ffn_w3_w"].T
    for name, w in (("w1", w1T), ("w2", w2T), ("h1", h1T), ("h2", h2T), ("w3", w3T)):
        wbf[name], _ = mk.precast("l0f_" + name, prechunk(np.asarray(w, np.float32)))
    scol = {}
    scol["s1"] = mk.load_cols("s1", colsplit(np.asarray(I["L0_ffn_w1_s"], np.float32)))
    scol["s2"] = mk.load_cols("s2", colsplit(np.asarray(I["L0_ffn_w2_s"], np.float32)))
    scol["h1s"] = mk.load_cols("h1s", colsplit(np.asarray(I["L0_ffn_h1_s"], np.float32)))
    scol["h2s"] = mk.load_cols("h2s", colsplit(np.asarray(I["L0_ffn_h2_s"], np.float32)))
    c0, _ = mk.load_cols("w3sh", colsplit(np.asarray(I["L0_ffn_w3_s"], np.float32)))
    mk.S.barrier()
    mk.S.op("V", lambda v: v.tensor_scalar_mul(
        mk.colsb[:, c0:c0 + 8], mk.colsb[:, c0:c0 + 8], 0.5))
    scol["w3sh"] = (c0, None)
    return wbf, scol


def rope_tables(seq):
    inv = 10000.0 ** (-np.arange(32, dtype=np.float64) / 32.0)
    fr = np.arange(seq, dtype=np.float64)[:, None] * inv[None, :]
    c = np.cos(fr); s = np.sin(fr)
    C64 = np.concatenate([c, c], 1)
    S64 = np.concatenate([-s, s], 1)
    CT = np.tile(C64, (1, 16))
    ST = np.tile(S64, (1, 16))
    f = lambda a: np.ascontiguousarray(a.astype(NBF))
    return f(CT / 8.0), f(ST / 8.0), f(CT), f(ST)


def causal_masks():
    i = np.arange(128)[:, None]
    j = np.arange(512)[None, :]
    mq = np.concatenate([np.where(j <= i + o * 128, 0.0, -1e30).astype(np.float32)
                         for o in range(4)], 0)
    mt = np.concatenate([np.where(d * 128 + i <= j, 0.0, -1e30).astype(np.float32)
                         for d in range(4)], 0)
    return np.ascontiguousarray(mq), np.ascontiguousarray(mt)


def build_l0attn_weights(mk, I):
    w = {}
    w["wq"], _ = mk.precast("l0a_wq", np.asarray(I["L0_attn_w_q"], np.float32))
    w["wk"], _ = mk.precast("l0a_wk", np.asarray(I["L0_attn_w_k"], np.float32))
    w["wv"], _ = mk.precast("l0a_wv", np.asarray(I["L0_attn_w_v"], np.float32))
    w["wo"], _ = mk.precast("l0a_wo", prechunk(np.asarray(I["L0_attn_w_o"], np.float32)))
    cq, sq, ck, sk = rope_tables(mk.seq)
    for nm, a in (("cq", cq), ("sq", sq), ("ck", ck), ("sk", sk)):
        ap = mk.din("c_" + nm, list(a.shape), BF16)
        mk.inputs["c_" + nm] = a
        w[nm] = ap
    return w


def build_l1ffn_weights(mk, I):
    w = {}
    w["attn_w"], _ = mk.precast("l1f_aw", np.asarray(I["L1_ffn_attn_w"], np.float32))
    w["proj"], _ = mk.precast("l1f_pw", prechunk(np.asarray(I["L1_ffn_proj_w"], np.float32)))
    w["router"], _ = mk.precast("l1f_rt", np.asarray(I["L1_ffn_router_w"], np.float32))
    w["wv"] = []
    w["pj"] = []
    for e in range(4):
        a, _ = mk.precast(f"l1f_wv{e}", prechunk(np.asarray(I["L1_ffn_expert_wv"][e], np.float32)))
        w["wv"].append(a)
        a, _ = mk.precast(f"l1f_pj{e}", prechunk(np.asarray(I["L1_ffn_expert_proj"][e], np.float32)))
        w["pj"].append(a)
    cols = {}
    cols["ln1w"] = mk.load_cols("l1f_ln1w", colsplit(np.asarray(I["L1_ffn_ln1_w"], np.float32)))
    cols["ln1b"] = mk.load_cols("l1f_ln1b", colsplit(np.asarray(I["L1_ffn_ln1_b"], np.float32)))
    cols["ln2w"] = mk.load_cols("l1f_ln2w", colsplit(np.asarray(I["L1_ffn_ln2_w"], np.float32)))
    cols["ln2b"] = mk.load_cols("l1f_ln2b", colsplit(np.asarray(I["L1_ffn_ln2_b"], np.float32)))
    cw = np.asarray(I["L1_ffn_conv_w"], np.float32)
    for i in range(3):
        cols[f"cw{i}"] = mk.load_cols(f"l1f_cw{i}", colsplit(np.ascontiguousarray(cw[:, 0, i])))
    mq, mt = causal_masks()
    for nm, a in (("maskq", mq), ("maskt", mt)):
        ap = mk.din("c_" + nm, list(a.shape), F32)
        mk.inputs["c_" + nm] = a
        w[nm] = ap
    return w, cols


def emit_l1ffn(mk, xsrc, xdst, w1f, c1f, wa, Qf, Kf, Vt, Of, Tf, Zf, tag):
    cfg = {"mode": "l1f", "kv": 16, "wqkv": w1f["attn_w"],
           "tabs": (wa["cq"], wa["sq"], wa["ck"], wa["sk"]),
           "ln1w": c1f["ln1w"], "ln1b": c1f["ln1b"],
           "cw0": c1f["cw0"], "cw1": c1f["cw1"], "cw2": c1f["cw2"], "tdst": Tf}
    mk.emit_qkv(xsrc, Qf, Kf, Vt, cfg, tag + "q")
    mk.emit_attn_core(Qf, Kf, Vt, Of, causal=True, KV=16, tag=tag + "c",
                      maskq=w1f["maskq"], maskt=w1f["maskt"])
    mk.emit_proj_resid([(Of, 8)], w1f["proj"], Tf, Zf, tag + "p")
    mk.emit_moe(Zf, xsrc, xdst, {"router": w1f["router"], "wv": w1f["wv"], "pj": w1f["pj"]},
                c1f, tag + "m")


def build_l1attn_weights(mk, I):
    w = {}
    w["wq"], _ = mk.precast("l1a_wq", prechunk(np.asarray(I["L1_attn_Wq"], np.float32)))
    w["wk"], _ = mk.precast("l1a_wk", prechunk(np.asarray(I["L1_attn_Wk"], np.float32)))
    wm1 = np.asarray(I["L1_attn_Wm1"], np.float32)
    w["wm1a"], _ = mk.precast("l1a_m1a", prechunk(np.ascontiguousarray(wm1[0:1024])))
    w["wm1b"], _ = mk.precast("l1a_m1b", prechunk(np.ascontiguousarray(wm1[1024:2048])))
    w["wm2"], _ = mk.precast("l1a_m2", prechunk(np.asarray(I["L1_attn_Wm2"], np.float32)))
    w["wo"], _ = mk.precast("l1a_wo", prechunk(np.asarray(I["L1_attn_Wo"], np.float32)))
    w["wv_pl"], _ = mk.precast("l1a_wv", np.asarray(I["L1_attn_Wv"], np.float32))
    bv = np.ascontiguousarray(np.asarray(I["L1_attn_bv"], np.float32).reshape(1, 1024))
    w["bv_row"] = mk.din("l1a_bv", [1, 1024], F32)
    mk.inputs["l1a_bv"] = bv
    cols = {}
    c0, _ = mk.load_cols("l1a_bq32", colsplit(np.asarray(I["L1_attn_bq"], np.float32)))
    mk.S.barrier()
    mk.S.op("V", lambda v: v.tensor_scalar_mul(
        mk.colsb[:, c0:c0 + 8], mk.colsb[:, c0:c0 + 8], 1.0 / 32))
    cols["bq32"] = (c0, None)
    cols["bk"] = mk.load_cols("l1a_bk", colsplit(np.asarray(I["L1_attn_bk"], np.float32)))
    cols["bm1"] = mk.load_cols("l1a_bm1", colsplit(np.asarray(I["L1_attn_bm1"], np.float32)))
    cols["bm2"] = mk.load_cols("l1a_bm2", colsplit(np.asarray(I["L1_attn_bm2"], np.float32)))
    cols["bo"] = mk.load_cols("l1a_bo", colsplit(np.asarray(I["L1_attn_bo"], np.float32)))
    mk.fconst(1.0 / 32)
    return w, cols


def build_all(inputs, seq=4096, tb=512, debug_outs=(), upto=99):
    mk, x_in, out, xA, xB = build_program(seq=seq, tb=tb, debug_outs=debug_outs)
    wbf, scol = build_l0ffn_weights(mk, inputs)
    wa = build_l0attn_weights(mk, inputs)
    w1f, c1f = (None, None)
    if upto >= 4:
        w1f, c1f = build_l1ffn_weights(mk, inputs)
    ntile = seq // 128
    Qf = mk.dscratch("Qf", [D, seq], BF16, debug=True)
    Kf = mk.dscratch("Kf", [D, seq], BF16, debug=True)
    Vt = mk.dscratch("Vt", [16 * ntile * 128, 65], BF16, debug=True)
    Of = mk.dscratch("Of", [D, seq], BF16, debug=True)
    Tf = mk.dscratch("Tf", [D, seq], F32, debug=True)
    Zf = mk.dscratch("Zf", [D, seq], F32, debug=True)
    S = mk.S
    S.barrier()
    mk.emit_prologue(x_in, xA)
    mk.emit_ffn0(xA, xB, wbf, scol, "f0a")
    final = xB
    if upto >= 2:
        cfg = {"mode": "l0", "kv": 4, "wq": wa["wq"], "wk": wa["wk"], "wv": wa["wv"],
               "tabs": (wa["cq"], wa["sq"], wa["ck"], wa["sk"])}
        mk.emit_qkv(xB, Qf, Kf, Vt, cfg, "qk0")
        mk.emit_attn_core(Qf, Kf, Vt, Of, causal=False, KV=4, tag="ac0")
        mk.emit_proj_resid([(Of, 8)], wa["wo"], xB, xA, "pr0")
        final = xA
    if upto >= 3:
        mk.emit_ffn0(xA, xB, wbf, scol, "f0b")
        final = xB
    if upto >= 4:
        emit_l1ffn(mk, xB, xA, w1f, c1f, wa, Qf, Kf, Vt, Of, Tf, Zf, "l1a")
        final = xA
    if upto >= 5:
        w1a, c1a = build_l1attn_weights(mk, inputs)
        Af = mk.dscratch("Af", [D, seq], BF16, debug=True)
        Bf = mk.dscratch("Bf", [D, seq], BF16, debug=True)
        Vtok = mk.dscratch("Vtok", [seq, D], BF16, debug=True)
        Gf = mk.dscratch("Gf", [D, seq], BF16, debug=True)
        Lf = mk.dscratch("Lf", [D, seq], BF16, debug=True)
        S.barrier()
        mk.emit_l1a_proj(xA, Qf, Kf, Af, Bf, Vtok, w1a, c1a, "laP")
        mk.emit_l1a_core(Qf, Kf, Vtok, Gf, {"maskq": w1f["maskq"]}, "laC")
        mk.emit_l1a_local(Af, Bf, Lf, w1a, c1a, "laL")
        mk.emit_proj_resid([(Lf, 8), (Gf, 8)], w1a["wo"], xA, xB, "laF",
                           bias_col=c1a["bo"])
        final = xB
    if upto >= 6:
        emit_l1ffn(mk, xB, xA, w1f, c1f, wa, Qf, Kf, Vt, Of, Tf, Zf, "l1b")
        final = xA
    mk.emit_epilogue(final, out)
    S.finish()
    return mk


def kernel(**inputs):
    seq = inputs["L0_attn_x"].shape[1]
    mk = build_all(inputs, seq=seq, tb=512)
    x = np.ascontiguousarray(np.asarray(inputs["L0_attn_x"], np.float32))
    in_maps = []
    for c in range(8):
        m = dict(mk.inputs)
        m["x"] = np.ascontiguousarray(x[c])
        in_maps.append(m)
    res = run_bass_kernel_spmd(mk.nc, in_maps, core_ids=list(range(8)))
    return np.stack([r["out"] for r in res.results], axis=0)
